# version 09
# baseline (speedup 1.0000x reference)
import numpy as np
import ml_dtypes
from contextlib import ExitStack

import concourse.bass as bass
import concourse.mybir as mybir
from concourse.bass_utils import run_bass_kernel_spmd

F32 = mybir.dt.float32
BF16 = mybir.dt.bfloat16
U8 = mybir.dt.uint8
U32 = mybir.dt.uint32
AF = mybir.ActivationFunctionType
ALU = mybir.AluOpType
AX = mybir.AxisListType
IOA = bass.IndirectOffsetOnAxis
BF = ml_dtypes.bfloat16

NCORES = 8
B, N, D = 4, 8192, 1024
NSEG, SL, NT = 8, 512, 4
NCH, NH, HD = 8, 8, 128
E, EL, CAP = 64, 8, 512
ISQN = float(1.0 / np.sqrt(np.float32(N)))
X4ROWS = EL * CAP + 1
MAX_PHASE = 7

T, V, S, G, Y = "tensor", "vector", "scalar", "gpsimd", "sync"


class Sem:
    def __init__(self, name, dma=False):
        self.name, self.n, self.h = name, 0, None
        self.dma = dma
        self.max_waited = 0
        self.last_sync = 0

    def inc(self, k=1):
        self.n += k
        return self.n


class KB:
    def __init__(self):
        self.ops = []
        self.auto = {}

    def emit(self, en, fn):
        self.ops.append((en, fn))

    def op(self, en, meth, *args, inc=None, **kw):
        aset = self.auto.get(en)
        if aset is not None:
            v = aset.n
            if v:
                self.emit(en, lambda eng, s=aset, v=v: eng.wait_ge(s.h, v))
            if meth != "dma_start":
                inc = list(inc) if inc else []
                if not any(sem is aset for sem, _ in inc):
                    inc.append((aset, 1))
        if inc:
            for sem, k in inc:
                if sem.dma and sem.max_waited > sem.last_sync:
                    v = sem.max_waited
                    sem.last_sync = v
                    self.emit(en, lambda eng, s=sem, v=v: eng.wait_ge(s.h, v))
        if inc:
            for sem, k in inc:
                sem.inc(k)

        def fn(eng, meth=meth, args=args, kw=kw, inc=inc):
            inst = getattr(eng, meth)(*args, **kw)
            if inc:
                for sem, k in inc:
                    inst = inst.then_inc(sem.h, k)
        self.emit(en, fn)

    def wait(self, en, sem, val=None):
        v = sem.n if val is None else val
        sem.max_waited = max(sem.max_waited, v)
        self.emit(en, lambda eng, s=sem, v=v: eng.wait_ge(s.h, v))


def build_program(max_phase=MAX_PHASE, dev=False, fake_cc=False):
    nc = bass.Bass("TRN2", target_bir_lowering=False)
    kb = KB()

    def din(name, shape, dt=F32):
        return nc.dram_tensor(name, shape, dt, kind="ExternalInput").ap()

    def dscr(name, shape, dt=F32):
        return nc.dram_tensor(name, shape, dt, kind="ExternalOutput" if dev else "Internal").ap()

    def dint(name, shape, dt=F32):
        return nc.dram_tensor(name, shape, dt, kind="Internal").ap()

    x_shard = din("x_shard", [NSEG, SL, D])
    x_halo = din("x_halo", [NSEG, 2, D])
    wqk_p = din("wqk_p", [D, 2 * D])
    bqk_cols = din("bqk_cols", [128, 16])
    lepe_w = din("lepe_w", [3, D, D])
    lepe_b = din("lepe_b", [1, D])
    costab = din("costab", [NSEG, 4, 128, SL], BF16)
    sintab = din("sintab", [NSEG, 4, 128, SL], BF16)
    gate_ext = din("gate_ext", [D, 72])
    sgu_w = din("sgu_w", [D, 4096])
    sdown_w = din("sdown_w", [2048, D])
    egu_w = din("egu_w", [EL, D, D])
    edown_w = din("edown_w", [EL, 512, D])
    c64 = din("c64", [128, E])
    c64p1 = din("c64p1", [128, E])
    c512 = din("c512", [128, CAP])
    ustrict = din("ustrict", [128, 128], BF16)
    ident_bf = din("ident_bf", [128, 128], BF16)
    ident_f = din("ident_f", [128, 128])
    ones_bf = din("ones_bf", [1, 128], BF16)
    onec_bf = din("onec_bf", [128, 1], BF16)
    ones_f = din("ones_f", [1, 128])
    tok_hi = din("tok_hi", [128, 32])
    tok_lo = din("tok_lo", [128, 32])
    maskcols = din("maskcols", [E, 8])
    comb_bf = din("comb_bf", [128, 128], BF16)
    zrow_col = din("zrow_col", [128, 1])
    eps_col = din("eps_col", [128, 1])

    out_d = nc.dram_tensor("out_d", [NSEG * SL, D], F32, kind="ExternalOutput").ap()

    qel_d = dscr("qel_d", [NSEG, 128, NCH * SL], BF16)
    qr_d = dscr("qr_d", [NSEG, 128, NCH * SL], BF16)
    lepe_d = dscr("lepe_d", [NSEG, SL, D], BF16)
    x2_d = dscr("x2_d", [NSEG, SL, D], F32)
    x2s_d = dscr("x2s_d", [NSEG, SL, D], F32)
    xlt2_d = dscr("xlt2_d", [NSEG, 128, NCH * SL], BF16)
    cl_d = dscr("cl_d", [NSEG, 128, NT * E], F32)
    wa_d = dscr("wa_d", [NSEG, 128, NT * E], F32)
    lg_d = dscr("lg_d", [NSEG * NT, 128, 80], F32) if dev else None
    dbg_d = dscr("dbg_d", [128, 2048], F32) if dev else None
    dbg2_d = dscr("dbg2_d", [128, 8192], BF16) if dev else None
    kv_in = dint("kv_in", [128, 4128], F32)
    kv_out = dint("kv_out", [128, 4128], F32)
    cnt_in = dint("cnt_in", [8, E], F32)
    cnt_out = dint("cnt_out", [E, E], F32)
    smw_in = dint("smw_in", [128, CAP], F32)
    smw_out = dint("smw_out", [16, CAP], F32)
    x2l_d = dint("x2l_d", [NSEG * SL, D], BF16)
    x2a_d = dint("x2a_d", [NCORES * NSEG * SL, D], BF16)
    x4l_d = dint("x4l_d", [X4ROWS, D], BF16)
    x4a_d = dint("x4a_d", [NCORES * X4ROWS, D], BF16)

    with ExitStack() as ctx:
        def sbuf(name, shape, dt=F32):
            return ctx.enter_context(nc.sbuf_tensor(name, shape, dt))

        def sem(name, dma=False):
            s = Sem(name, dma=dma)
            s.h = ctx.enter_context(nc.semaphore(name))
            return s

        c64_sb = sbuf("c64_sb", [128, E])
        c64p1_sb = sbuf("c64p1_sb", [128, E])
        c512_sb = sbuf("c512_sb", [128, CAP])
        ustrict_sb = sbuf("ustrict_sb", [128, 128], BF16)
        ident_sb = sbuf("ident_sb", [128, 128], BF16)
        identf_sb = sbuf("identf_sb", [128, 128])
        ones_bf_sb = sbuf("ones_bf_sb", [1, 128], BF16)
        onec_bf_sb = sbuf("onec_bf_sb", [128, 1], BF16)
        ones_f_sb = sbuf("ones_f_sb", [1, 128])
        tok_hi_sb = sbuf("tok_hi_sb", [128, 32])
        tok_lo_sb = sbuf("tok_lo_sb", [128, 32])
        maskcols_sb = sbuf("maskcols_sb", [E, 8])
        comb_sb = sbuf("comb_sb", [128, 128], BF16)
        zrow_sb = sbuf("zrow_sb", [128, 1])
        eps_sb = sbuf("eps_sb", [128, 1])
        bqk_sb = sbuf("bqk_sb", [128, 16])
        lepeb_sb = sbuf("lepeb_sb", [1, D], BF16)
        gate_sb = sbuf("gate_sb", [128, NCH * 72], BF16)

        kvbf_sb = sbuf("kvbf_sb", [128, B * NH * HD], BF16)
        kmean_sb = sbuf("kmean_sb", [128, B * NCH])
        sall_sb = sbuf("sall_sb", [128, B * NCH * NH], BF16)
        ksum_sb = sbuf("ksum_sb", [128, B * NCH])
        edesc_sb = sbuf("edesc_sb", [128, 256])
        pos_sb = sbuf("pos_sb", [128, 256])
        addrf_sb = sbuf("addrf_sb", [128, 256])
        addru_sb = sbuf("addru_sb", [128, 256], U32)
        sg_sb = sbuf("sg_sb", [128, 32])
        rowbase_sb = sbuf("rowbase_sb", [1, 32 * E])
        cnt1_sb = sbuf("cnt1_sb", [1, NSEG * E])
        tot_sb = sbuf("tot_sb", [1, 2 * E])
        smloc_sb = sbuf("smloc_sb", [16, CAP])
        smi_sb = sbuf("smi_sb", [16, CAP])
        smT_sb = sbuf("smT_sb", [128, 64])
        smTu_sb = sbuf("smTu_sb", [128, 64], U32)
        wt_sb = sbuf("wt_sb", [128, 32])
        stat_sb = sbuf("stat_sb", [128, 6])
        hstat_sb = sbuf("hstat_sb", [2, 6])
        bn6_sb = sbuf("bn6_sb", [128, 12])
        hbn_sb = sbuf("hbn_sb", [2, 12])
        hbn2_sb = sbuf("hbn2_sb", [128, 12])
        stat2_sb = sbuf("stat2_sb", [128, 6])
        red8_sb = sbuf("red8_sb", [128, 8])

        AAB = 68 * 1024
        AWB = 96 * 1024
        aa = sbuf("aa", [128, AAB], U8)
        aw = sbuf("aw", [128, AWB], U8)

        def av(off, nbytes, dt):
            assert off + nbytes <= AAB, (off, nbytes)
            return aa[:, off:off + nbytes].bitcast(dt)

        def wv(off, nbytes, dt):
            assert off + nbytes <= AWB, (off, nbytes)
            return aw[:, off:off + nbytes].bitcast(dt)

        xln_sb = av(0, 8192, BF16)
        xlt_sb = av(8192, 8224, BF16)
        qel_sb = av(16416, 8192, BF16)
        kel_sb = av(24608, 8192, BF16)
        qr2_sb = av(32800, 2048, BF16)
        kr2_sb = av(34848, 2048, BF16)
        krt1_sb = av(36896, 1024, BF16)
        cos_sb = av(38944, 4096, BF16)
        sin_sb = av(43040, 4096, BF16)
        eexp_sb = av(47136, 1024, BF16)
        erel_sb = av(48160, 1024, BF16)
        rexp_sb = av(49184, 1024, BF16)
        rrel_sb = av(50208, 1024, BF16)
        xt_sb = av(51232, 4096, F32)
        kvev_sb = av(59424, 4096, F32)
        lepet_sb2 = [av(63520 + 2048 * i, 2048, BF16) for i in range(2)]
        xhf_sb = sbuf("xhf_sb", [2, D])
        xlnh_sb = sbuf("xlnh_sb", [2, D], BF16)

        kvf_sb = av(0, 16512, F32)

        xln2_sb = av(0, 8192, BF16)
        xlt2_sb = av(8192, 8192, BF16)
        p3qel_sb = av(16416, 8192, BF16)
        p3qr_sb = av(24608, 8192, BF16)
        attnz_sb = [av(32800 + 4096 * i, 4096, F32) for i in range(2)]
        p3x_sb = [av(40992 + 4096 * i, 4096, F32) for i in range(2)]
        p3lep_sb = [av(49184 + 2048 * i, 2048, BF16) for i in range(2)]
        x2t3_sb = [av(53280 + 4096 * i, 4096, F32) for i in range(2)]
        zden_sb = av(61472, 2048, F32)
        zt_sb = av(63520, 128, F32)
        z_sb = av(63648, 128, F32)
        lgt_sb = [av(63776, 320, F32) for i in range(1)]
        lgt_sb = [av(63776 + 320 * i, 320, F32) for i in range(2)]
        m8_sb = [av(64416 + 32 * i, 32, F32) for i in range(2)]
        repl_sb = av(64480, 256, F32)
        selt_sb = [av(64736 + 128 * i, 128, BF16) for i in range(2)]
        eall_sb = av(64992, 256, F32)
        wm3_sb = [av(65248 + 256 * i, 256, F32) for i in range(2)]
        rsum_sb = av(65760, 16, F32)
        ed8_sb = av(65776, 32, F32)
        clt_sb = [av(65808 + 256 * i, 256, F32) for i in range(2)]
        tot2_sb = av(66320, 8, F32)

        NPOSH, NHL = 6, 3
        cnts_sb = av(0, 256, F32)
        clseg_sb = av(256, 1024, F32)
        waseg_sb = av(1280, 1024, F32)
        cg_sb = av(2304, 256, F32)
        mask_sb = av(2560, 1024, BF16)
        wmp_sb = av(3584, 2048, F32)
        junk_sb = av(5632, 256, F32)
        posh_sb = [av(5888 + 1024 * i, 1024, BF16) for i in range(NPOSH)]
        mhl_sb = [av(12032 + 2048 * i, 2048, BF16) for i in range(NHL)]
        whl_sb = [av(18176 + 2048 * i, 2048, BF16) for i in range(NHL)]
        smhlb_sb2 = av(0, 2048, BF16)
        smv_sb = av(2048, 2048, F32)
        wv_sb = av(4096, 2048, F32)
        eh_sb = [av(6144, 1024, F32), av(7168, 1024, F32)]
        a1_sb = av(8192, 1024, F32)
        alive_sb = av(9216, 1024, F32)

        h_sb = av(33792, 16384, BF16)
        sguv = wv(0, 65536, BF16)
        p5x_sb = av(50176, 4096, F32)
        p5s_sb = av(54272, 4096, F32)
        silu_sb = av(58368, 1024, BF16)
        p5xlt_sb = av(59392, 8192, BF16)

        egu_sb = [av(18432, 16384, BF16), av(34816, 16384, BF16)]
        edn_sb = [av(51200, 8192, BF16), av(59392, 8192, BF16)]
        rows_sb = av(0, 2048, BF16)
        rowsT_sb = av(2048, 8192, BF16)
        he_sb = av(10240, 4096, BF16)
        oute_sb = av(14336, 2048, BF16)
        zero_sb = av(16384, 2048, BF16)
        silu6_sb = av(17408, 1024, BF16)

        NFT = 8
        fin_sb = [av(4096 * i, 4096, F32) for i in range(NFT)]

        ps0 = ctx.enter_context(nc.psum_tensor("ps0", [128, SL], F32))
        ps1 = ctx.enter_context(nc.psum_tensor("ps1", [128, SL], F32))
        ps2 = ctx.enter_context(nc.psum_tensor("ps2", [128, SL], F32))
        ps_trF = ctx.enter_context(nc.psum_tensor("ps_trF", [128, 2 * SL], BF16))
        ps_tr = ps_trF[:, 0:SL]
        ps_tr2 = ps_trF[:, SL:2 * SL]
        ps_kv = ctx.enter_context(nc.psum_tensor("ps_kv", [128, 2 * SL], F32))
        ps3 = ctx.enter_context(nc.psum_tensor("ps3", [128, SL], F32))
        ps4 = ctx.enter_context(nc.psum_tensor("ps4", [128, SL], F32))


        qL = sem("qL", dma=True); qS = sem("qS", dma=True); qG = sem("qG", dma=True)

        qE = [sem("qE0", dma=True), sem("qE1", dma=True)]
        qX2 = [sem("qX2a", dma=True), sem("qX2b", dma=True)]
        qCL = [sem("qCLa", dma=True), sem("qCLb", dma=True)]
        qWM = [sem("qWMa", dma=True), sem("qWMb", dma=True)]
        qXS = [sem("qXS%d" % i, dma=True) for i in range(4)]
        qW5 = sem("qW5", dma=True)
        qCkv = sem("qCkv"); qCcnt = sem("qCcnt"); qCx2 = sem("qCx2")
        qCsm = sem("qCsm"); qCx4 = sem("qCx4")
        mm = sem("mm"); ve = sem("ve"); se = sem("se"); tE = sem("tE"); gv = sem("gv")
        kb.auto = {V: ve, S: se}

        for dst, src in [(c64_sb, c64), (c64p1_sb, c64p1), (c512_sb, c512),
                         (ustrict_sb, ustrict), (ident_sb, ident_bf), (identf_sb, ident_f),
                         (ones_bf_sb, ones_bf), (onec_bf_sb, onec_bf), (ones_f_sb, ones_f),
                         (tok_hi_sb, tok_hi), (tok_lo_sb, tok_lo),
                         (maskcols_sb, maskcols), (comb_sb, comb_bf),
                         (zrow_sb, zrow_col), (eps_sb, eps_col),
                         (bqk_sb, bqk_cols)]:
            kb.op(Y, "dma_start", out=dst[:], in_=src, inc=[(qL, 16)])
        P0_sync = qL.n
        kb.op(G, "dma_start", out=wv(0, 32768, BF16).rearrange("p (c f) -> p c f", f=2 * D),
              in_=wqk_p.rearrange("(c p) f -> p c f", p=128), inc=[(qG, 16)])
        kb.op(G, "dma_start", out=wv(32768, 49152, BF16).rearrange("p (k c f) -> p k c f", k=3, c=NCH),
              in_=lepe_w.rearrange("k (c p) f -> p k c f", p=128), inc=[(qG, 16)])
        kb.op(G, "dma_start", out=lepeb_sb[:], in_=lepe_b, inc=[(qG, 16)])
        kb.op(G, "dma_start", out=gate_sb[:].rearrange("p (c f) -> p c f", f=72),
              in_=gate_ext.rearrange("(c p) f -> p c f", p=128), inc=[(qG, 16)])
        P0_gp = qG.n
        wqk_sb = wv(0, 32768, BF16)
        lepew_sb = wv(32768, 49152, BF16)
        kb.op(V, "memset", ksum_sb[:], 0.0, inc=[(ve, 1)])
        kb.wait(T, qL, P0_sync)
        kb.wait(V, qL, P0_sync)
        kb.wait(S, qL, P0_sync)
        kb.wait(T, qG, P0_gp)

        def emit_ln(src_ap, dst_ap, bn, st, part=128):
            kb.op(V, "bn_stats", out=bn[0:part, 0:6], in_=src_ap[0:part, 0:SL], inc=[(ve, 1)])
            kb.op(V, "bn_stats", out=bn[0:part, 6:12], in_=src_ap[0:part, SL:D], inc=[(ve, 1)])
            kb.wait(V, ve, ve.n)
            kb.op(V, "bn_aggr", out=st[0:part, 0:2], in_=bn[0:part, :], inc=[(ve, 1)])
            kb.wait(S, ve, ve.n)
            kb.op(S, "activation", out=st[0:part, 2:3], in_=st[0:part, 1:2], func=AF.Sqrt,
                  bias=eps_sb[0:part, 0:1], scale=1.0, inc=[(se, 1)])
            kb.wait(V, se, se.n)
            kb.op(V, "reciprocal", out=st[0:part, 3:4], in_=st[0:part, 2:3], inc=[(ve, 1)])
            kb.wait(V, ve, ve.n)
            kb.op(V, "scalar_tensor_tensor", out=st[0:part, 5:6], in0=st[0:part, 0:1],
                  scalar=-1.0, in1=st[0:part, 3:4], op0=ALU.mult, op1=ALU.mult, inc=[(ve, 1)])
            kb.wait(S, ve, ve.n)
            kb.op(S, "activation", out=dst_ap, in_=src_ap[0:part, :], func=AF.Identity,
                  bias=st[0:part, 5:6], scale=st[0:part, 3:4], inc=[(se, 1)])
            return se.n

        if max_phase >= 1:
            p1_psg = [0, 0, 0, 0]
            halo_ve_g = halo_se_g = rope_g = xln_g = xlt_g = 0
            for s in range(NSEG):
                b = s // 2
                kb.wait(Y, ve, halo_ve_g)
                kb.wait(Y, se, halo_se_g)
                kb.op(Y, "dma_start", out=xhf_sb[:], in_=x_halo[s], inc=[(qL, 16)])
                kb.wait(V, qL, qL.n)
                kb.wait(S, mm, xln_g)
                emit_ln(xhf_sb, xlnh_sb[:], hbn_sb, hstat_sb, part=2)
                halo_ve_g = ve.n
                halo_se_g = se.n
                for t in range(NT):
                    kb.wait(Y, se, se.n)
                    kb.op(Y, "dma_start", out=xt_sb[:], in_=x_shard[s, t * 128:(t + 1) * 128, :],
                          inc=[(qL, 16)])
                    kb.wait(V, qL, qL.n)
                    if t == 0:
                        kb.wait(S, mm, xln_g)
                    emit_ln(xt_sb, xln_sb[:, t * D:(t + 1) * D], bn6_sb, stat_sb)
                xln_done = se.n

                kb.wait(T, se, xln_done)
                for c in range(NCH):
                    for t in range(NT):
                        kb.op(T, "transpose", ps_tr[:, t * 128:(t + 1) * 128],
                              xln_sb[:, t * D + c * 128: t * D + (c + 1) * 128], ident_sb[:],
                              inc=([(mm, 1)] if t == NT - 1 else None))
                    kb.wait(V, mm, mm.n)
                    if c == 0:
                        kb.wait(V, mm, xlt_g)
                        kb.wait(V, qS, qS.n)
                    kb.op(V, "tensor_copy", out=xlt_sb[:, c * 514 + 1: c * 514 + 513],
                          in_=ps_tr[:], inc=[(ve, 1)])
                    kb.wait(T, ve, ve.n)
                for c in range(NCH):
                    kb.op(T, "transpose", ps_tr[:, c * 2:(c + 1) * 2],
                          xlnh_sb[:, c * 128:(c + 1) * 128], ident_sb[0:2, 0:2],
                          inc=([(mm, 1)] if c == NCH - 1 else None))
                kb.wait(V, mm, mm.n)
                for c in range(NCH):
                    kb.op(V, "tensor_copy", out=xlt_sb[:, c * 514: c * 514 + 1],
                          in_=ps_tr[:, c * 2: c * 2 + 1], inc=[(ve, 1)])
                    kb.op(V, "tensor_copy", out=xlt_sb[:, c * 514 + 513: c * 514 + 514],
                          in_=ps_tr[:, c * 2 + 1: c * 2 + 2], inc=[(ve, 1)])
                kb.wait(T, ve, ve.n)

                kb.wait(Y, ve, rope_g)
                kb.op(Y, "dma_start", out=cos_sb.rearrange("p (c q) -> p c q", q=SL),
                      in_=costab[s].transpose([1, 0, 2]), inc=[(qL, 16)])
                kb.op(Y, "dma_start", out=sin_sb.rearrange("p (c q) -> p c q", q=SL),
                      in_=sintab[s].transpose([1, 0, 2]), inc=[(qL, 16)])
                rope_ld = qL.n

                qk_psg = [p1_psg[0], p1_psg[1]]
                elu_g = [0, 0]
                for f in range(16):
                    dst = qel_sb if f < 8 else kel_sb
                    fc = f % 8
                    fp = f % 2
                    psq = ps0 if fp == 0 else ps1
                    eex = eexp_sb if fp == 0 else rexp_sb
                    ere = erel_sb if fp == 0 else rrel_sb
                    kb.wait(T, se, qk_psg[fp])
                    for c in range(NCH):
                        kb.op(T, "matmul", psq[:],
                              wqk_sb[:, c * 2048 + f * 128: c * 2048 + (f + 1) * 128],
                              xlt_sb[:, c * 514 + 1: c * 514 + 513],
                              start=(c == 0), stop=(c == NCH - 1),
                              inc=([(mm, 1)] if c == NCH - 1 else None))
                    kb.wait(S, mm, mm.n)
                    kb.wait(S, ve, elu_g[fp])
                    if f == 0:
                        kb.wait(S, qS, qS.n)
                    kb.op(S, "activation", out=eex, in_=psq[:], func=AF.Exp,
                          bias=bqk_sb[:, f:f + 1], scale=1.0, inc=[(se, 1)])
                    kb.op(S, "activation", out=ere, in_=psq[:], func=AF.Relu,
                          bias=bqk_sb[:, f:f + 1], scale=1.0, inc=[(se, 1)])
                    qk_psg[fp] = se.n
                    kb.wait(V, se, se.n)
                    kb.op(V, "tensor_scalar", out=eex, in0=eex, scalar1=1.0,
                          scalar2=None, op0=ALU.min, inc=[(ve, 1)])
                    kb.op(V, "tensor_tensor", out=dst[:, fc * SL:(fc + 1) * SL], in0=eex,
                          in1=ere, op=ALU.add, inc=[(ve, 1)])
                    elu_g[fp] = ve.n
                qk_done = ve.n
                kb.wait(S, ve, qk_done)
                kb.op(S, "dma_start", out=qel_d[s], in_=qel_sb, inc=[(qS, 16)])

                for c in range(NCH):
                    kb.op(V, "tensor_reduce", out=red8_sb[:, c:c + 1],
                          in_=kel_sb[:, c * SL:(c + 1) * SL], axis=AX.X, op=ALU.add,
                          inc=([(ve, 1)] if c == NCH - 1 else None))
                kb.wait(V, ve, ve.n)
                kb.op(V, "tensor_tensor", out=ksum_sb[:, b * NCH:(b + 1) * NCH],
                      in0=ksum_sb[:, b * NCH:(b + 1) * NCH],
                      in1=red8_sb[:], op=ALU.add, inc=[(ve, 1)])

                kb.wait(V, qL, rope_ld)
                for (src, isq) in ((qel_sb, True), (kel_sb, False)):
                    dstr = qr2_sb if isq else kr2_sb
                    for jc in range(4):
                        co = cos_sb[:, jc * SL:(jc + 1) * SL]
                        si = sin_sb[:, jc * SL:(jc + 1) * SL]
                        ev = src[:, jc * SL:(jc + 1) * SL]
                        od = src[:, (4 + jc) * SL:(5 + jc) * SL]
                        kb.wait(V, qS, qS.n)
                        if not isq:
                            kb.wait(V, mm, mm.n)
                        kb.op(V, "tensor_tensor", out=eexp_sb[:], in0=co, in1=ev, op=ALU.mult, inc=[(ve, 1)])
                        kb.op(V, "tensor_tensor", out=erel_sb[:], in0=si, in1=od, op=ALU.mult, inc=[(ve, 1)])
                        kb.wait(V, ve, ve.n)
                        kb.op(V, "tensor_tensor", out=dstr[:, 0:SL],
                              in0=eexp_sb[:], in1=erel_sb[:], op=ALU.subtract, inc=[(ve, 1)])
                        kb.op(V, "tensor_tensor", out=eexp_sb[:], in0=co, in1=od, op=ALU.mult, inc=[(ve, 1)])
                        kb.op(V, "tensor_tensor", out=erel_sb[:], in0=si, in1=ev, op=ALU.mult, inc=[(ve, 1)])
                        kb.wait(V, ve, ve.n)
                        kb.op(V, "tensor_tensor", out=dstr[:, SL:2 * SL],
                              in0=eexp_sb[:], in1=erel_sb[:], op=ALU.add, inc=[(ve, 1)])
                        pair_done = ve.n
                        if isq:
                            kb.wait(S, ve, pair_done)
                            kb.op(S, "dma_start", out=qr_d[s, :, jc * SL:(jc + 1) * SL],
                                  in_=dstr[:, 0:SL], inc=[(qS, 16)])
                            kb.op(S, "dma_start", out=qr_d[s, :, (4 + jc) * SL:(5 + jc) * SL],
                                  in_=dstr[:, SL:2 * SL], inc=[(qS, 16)])
                        else:
                            kb.wait(T, ve, pair_done)
                            for half, h in ((0, jc), (1, jc + 4)):
                                for t in range(NT):
                                    kb.op(T, "transpose", ps_tr[:, t * 128:(t + 1) * 128],
                                          dstr[:, half * SL + t * 128: half * SL + (t + 1) * 128],
                                          ident_sb[:], inc=([(mm, 1)] if t == NT - 1 else None))
                                kb.wait(V, mm, mm.n)
                                for t in range(NT):
                                    kb.op(V, "tensor_copy", out=krt1_sb[:, t * 128:(t + 1) * 128],
                                          in_=ps_tr[:, t * 128:(t + 1) * 128],
                                          inc=([(ve, 1)] if t == NT - 1 else None))
                                kb.wait(T, ve, ve.n)
                                for t in range(NT):
                                    kb.op(T, "matmul", ps_kv[:, h * HD:(h + 1) * HD],
                                          krt1_sb[:, t * 128:(t + 1) * 128],
                                          xln_sb[:, t * D + h * 128:t * D + (h + 1) * 128],
                                          start=(t == 0), stop=(t == NT - 1),
                                          skip_group_check=True,
                                          inc=([(mm, 1)] if t == NT - 1 else None))
                                kb.wait(V, mm, mm.n)
                kv_mm = mm.n
                rope_g = ve.n
                xln_g = kv_mm
                kb.wait(S, mm, kv_mm)
                kb.wait(S, qG, qG.n)
                kb.op(S, "copy", kvev_sb[:, 0:SL], ps_kv[:, 0:SL], inc=[(se, 1)])
                kb.op(S, "copy", kvev_sb[:, SL:2 * SL], ps_kv[:, SL:2 * SL], inc=[(se, 1)])
                kv_ev = se.n
                kb.wait(T, se, kv_ev)
                kb.wait(G, se, kv_ev)
                kb.wait(G, qG, qG.n)
                if s % 2 == 0:
                    kb.op(G, "dma_start", out=kv_in[:, b * D:(b + 1) * D], in_=kvev_sb[:],
                          inc=[(qG, 16)])
                else:
                    kb.op(G, "dma_start", out=kv_in[:, b * D:(b + 1) * D], in_=kvev_sb[:],
                          accum_op=ALU.add, inc=[(qG, 16)])

                lep_psg = {"ps0": qk_psg[0], "ps1": qk_psg[1], "ps3": p1_psg[2], "ps4": p1_psg[3]}
                for t in range(NT):
                    pse = (ps0, ps1) if t % 2 == 0 else (ps3, ps4)
                    pk = ("ps0", "ps1") if t % 2 == 0 else ("ps3", "ps4")
                    for half in range(2):
                        pst = pse[half]
                        kb.wait(T, se, lep_psg[pk[half]])
                        for k in range(3):
                            for c in range(NCH):
                                kb.op(T, "matmul", pst[:],
                                      xlt_sb[:, c * 514 + t * 128 + k: c * 514 + t * 128 + k + 128],
                                      lepew_sb[:, (k * NCH + c) * D + half * SL:(k * NCH + c) * D + (half + 1) * SL],
                                      start=(k == 0 and c == 0), stop=False)
                        kb.op(T, "matmul", pst[:], ones_bf_sb[0:1, :],
                              lepeb_sb[0:1, half * SL:(half + 1) * SL],
                              start=False, stop=True, inc=[(mm, 1)])
                        kb.wait(S, mm, mm.n)
                        if half == 0:
                            kb.wait(S, qS, qS.n)
                        kb.op(S, "copy", lepet_sb2[t % 2][:, half * SL:(half + 1) * SL],
                              pst[:], inc=[(se, 1)])
                        lep_psg[pk[half]] = se.n
                    kb.op(S, "dma_start", out=lepe_d[s, t * 128:(t + 1) * 128, :],
                          in_=lepet_sb2[t % 2], inc=[(qS, 16)])
                p1_psg = [lep_psg["ps0"], lep_psg["ps1"], lep_psg["ps3"], lep_psg["ps4"]]
                xlt_g = mm.n

        mm_p1 = mm.n

        if max_phase >= 2:
            kb.wait(G, ve, ve.n)
            kb.wait(G, qG, qG.n)
            kb.op(G, "dma_start", out=kv_in[:, 4096:4128], in_=ksum_sb[:], inc=[(qG, 16)])
            kb.wait(G, qG, qG.n)
            kb.op(G, "collective_compute", "AllReduce", ALU.add,
                  replica_groups=[list(range(NCORES))],
                  ins=[kv_in.opt()], outs=[kv_out.opt()], inc=[(qCkv, 1)])
            kb.wait(Y, qCkv, qCkv.n)
            kb.wait(Y, mm, mm.n)
            kb.wait(Y, ve, ve.n)
            kb.wait(Y, se, se.n)
            kb.wait(Y, qS, qS.n)
            kb.op(Y, "dma_start", out=kvf_sb, in_=kv_out[:, :], inc=[(qL, 16)])
            kb.wait(V, qL, qL.n)
            kb.op(V, "tensor_scalar", out=kvbf_sb[:], in0=kvf_sb[:, 0:4096], scalar1=ISQN,
                  scalar2=None, op0=ALU.mult, inc=[(ve, 1)])
            kb.op(V, "tensor_scalar", out=kmean_sb[:], in0=kvf_sb[:, 4096:4128], scalar1=1.0 / N,
                  scalar2=None, op0=ALU.mult, inc=[(ve, 1)])
            kb.op(V, "memset", sall_sb[:], 0.0, inc=[(ve, 1)])
            kb.wait(V, ve, ve.n)
            for b in range(B):
                for c in range(NCH):
                    base = (b * NCH + c) * NH
                    h0 = 2 * (c % 4)
                    kb.op(V, "tensor_copy", out=sall_sb[0:64, base + h0: base + h0 + 1],
                          in_=kmean_sb[0:64, b * NCH + c: b * NCH + c + 1], inc=[(ve, 1)])
                    kb.op(V, "tensor_copy", out=sall_sb[64:128, base + h0 + 1: base + h0 + 2],
                          in_=kmean_sb[64:128, b * NCH + c: b * NCH + c + 1], inc=[(ve, 1)])
            p2_done = ve.n
            if dev:
                kb.wait(S, ve, p2_done)
                kb.op(S, "dma_start", out=dbg2_d[:, 0:4096], in_=kvbf_sb[:], inc=[(qS, 16)])
                kb.op(S, "dma_start", out=dbg_d[:, 0:32], in_=kmean_sb[:], inc=[(qS, 16)])
                kb.op(S, "dma_start", out=dbg2_d[:, 4096:4096 + 256], in_=sall_sb[:], inc=[(qS, 16)])

        if max_phase >= 3:
            kb.op(V, "memset", rowbase_sb[:], 0.0, inc=[(ve, 1)])
            kb.op(V, "memset", cnt1_sb[:], 0.0, inc=[(ve, 1)])
            for s in range(NSEG):
                b = s // 2
                g0 = s * NT
                kb.wait(Y, mm, mm.n)
                kb.wait(Y, ve, ve.n)
                kb.wait(Y, se, se.n)
                kb.wait(Y, qS, qS.n)
                kb.op(Y, "dma_start", out=p3qel_sb, in_=qel_d[s], inc=[(qL, 16)])
                kb.op(Y, "dma_start", out=p3qr_sb, in_=qr_d[s], inc=[(qL, 16)])
                ld_qq = qL.n
                kb.wait(T, qL, ld_qq)
                kb.wait(T, ve, ve.n)
                for c in range(NCH):
                    kb.op(T, "matmul", ps2[0:8, :],
                          sall_sb[:, (b * NCH + c) * NH:(b * NCH + c + 1) * NH],
                          p3qel_sb[:, c * SL:(c + 1) * SL],
                          start=(c == 0), stop=(c == NCH - 1),
                          inc=([(mm, 1)] if c == NCH - 1 else None))
                kb.wait(S, mm, mm.n)
                kb.op(S, "copy", zden_sb[0:8, :], ps2[0:8, :], inc=[(se, 1)])
                kb.wait(T, se, se.n)
                for t in range(NT):
                    kb.op(T, "transpose", ps2[:, 448 + t * 8: 456 + t * 8],
                          zden_sb[0:8, t * 128:(t + 1) * 128], identf_sb[0:8, 0:8],
                          inc=([(mm, 1)] if t == NT - 1 else None))
                kb.wait(V, mm, mm.n)
                kb.op(V, "tensor_scalar", out=zt_sb[:, 0:32], in0=ps2[:, 448:480], scalar1=1e-6,
                      scalar2=None, op0=ALU.add, inc=[(ve, 1)])
                kb.op(V, "reciprocal", out=z_sb[:, 0:32], in_=zt_sb[:, 0:32], inc=[(ve, 1)])
                z_done = ve.n
                kb.wait(T, ve, z_done)

                og = {0: 0, 1: 0}
                ldg = {0: 0, 1: 0}
                for t in range(NT):
                    g = g0 + t
                    pr = t % 2
                    odst = [(h, ps_kv[:, h * HD:(h + 1) * HD]) for h in range(NH)]
                    kb.wait(T, ve, og[1 - pr])
                    for h, dstp in odst:
                        kb.op(T, "matmul", dstp,
                              p3qr_sb[:, h * SL + t * 128: h * SL + (t + 1) * 128],
                              kvbf_sb[:, (b * NH + h) * HD:(b * NH + h + 1) * HD],
                              start=(h % 4 == 0), stop=(h % 4 == 3),
                              skip_group_check=True,
                              inc=([(mm, 1)] if h == NH - 1 else None))
                    o_mm = mm.n
                    kb.wait(Y, ve, ldg[pr])
                    kb.wait(Y, qX2[pr], qX2[pr].n)
                    kb.op(Y, "dma_start", out=p3x_sb[pr], in_=x_shard[s, t * 128:(t + 1) * 128, :],
                          inc=[(qL, 16)])
                    kb.op(Y, "dma_start", out=p3lep_sb[pr], in_=lepe_d[s, t * 128:(t + 1) * 128, :],
                          inc=[(qL, 16)])
                    ld_xt = qL.n
                    kb.wait(V, mm, o_mm)
                    for h, dstp in odst:
                        kb.op(V, "tensor_scalar", out=attnz_sb[pr][:, h * HD:(h + 1) * HD],
                              in0=dstp,
                              scalar1=z_sb[:, t * 8 + h: t * 8 + h + 1], scalar2=None,
                              op0=ALU.mult, inc=([(ve, 1)] if h == NH - 1 else None))
                    og[pr] = ve.n
                    kb.wait(V, qL, ld_xt)
                    kb.wait(V, qX2[pr], qX2[pr].n)
                    kb.op(V, "scalar_tensor_tensor", out=x2t3_sb[pr], in0=p3lep_sb[pr][:],
                          scalar=1.0, in1=p3x_sb[pr][:], op0=ALU.mult, op1=ALU.add, inc=[(ve, 1)])
                    kb.op(V, "tensor_tensor", out=x2t3_sb[pr], in0=x2t3_sb[pr], in1=attnz_sb[pr],
                          op=ALU.add, inc=[(ve, 1)])
                    x2_done = ve.n
                    kb.wait(S, ve, x2_done)
                    kb.op(S, "dma_start", out=x2_d[s, t * 128:(t + 1) * 128, :], in_=x2t3_sb[pr],
                          inc=[(qX2[pr], 16)])
                    bnp, stp = (bn6_sb, stat_sb) if pr == 0 else (hbn2_sb, stat2_sb)
                    kb.op(V, "bn_stats", out=bnp[:, 0:6], in_=x2t3_sb[pr][:, 0:SL], inc=[(ve, 1)])
                    kb.op(V, "bn_stats", out=bnp[:, 6:12], in_=x2t3_sb[pr][:, SL:D], inc=[(ve, 1)])
                    kb.op(V, "bn_aggr", out=stp[:, 0:2], in_=bnp[:], inc=[(ve, 1)])
                    kb.wait(S, ve, ve.n)
                    kb.op(S, "activation", out=stp[:, 2:3], in_=stp[:, 1:2], func=AF.Sqrt,
                          bias=eps_sb[:, 0:1], scale=1.0, inc=[(se, 1)])
                    kb.wait(V, se, se.n)
                    kb.op(V, "reciprocal", out=stp[:, 3:4], in_=stp[:, 2:3], inc=[(ve, 1)])
                    kb.op(V, "scalar_tensor_tensor", out=stp[:, 5:6], in0=stp[:, 0:1],
                          scalar=-1.0, in1=stp[:, 3:4], op0=ALU.mult, op1=ALU.mult, inc=[(ve, 1)])
                    kb.wait(S, ve, ve.n)
                    kb.op(S, "activation", out=xln2_sb[:, t * D:(t + 1) * D], in_=x2t3_sb[pr][:],
                          func=AF.Identity, bias=stp[:, 5:6], scale=stp[:, 3:4], inc=[(se, 1)])
                    ldg[pr] = ve.n
                    ln2_done = se.n
                    kb.op(S, "dma_start", out=x2l_d[s * SL + t * 128: s * SL + (t + 1) * 128, :],
                          in_=xln2_sb[:, t * D:(t + 1) * D], inc=[(qS, 16)])

                kb.wait(T, se, se.n)
                for c in range(NCH):
                    for t in range(NT):
                        kb.op(T, "transpose", ps_tr[:, t * 128:(t + 1) * 128],
                              xln2_sb[:, t * D + c * 128: t * D + (c + 1) * 128], ident_sb[:],
                              inc=([(mm, 1)] if t == NT - 1 else None))
                    kb.wait(V, mm, mm.n)
                    kb.op(V, "tensor_copy", out=xlt2_sb[:, c * SL:(c + 1) * SL],
                          in_=ps_tr[:], inc=[(ve, 1)])
                    kb.wait(T, ve, ve.n)
                xlt2_done = ve.n
                kb.wait(S, ve, xlt2_done)
                kb.op(S, "dma_start", out=xlt2_d[s], in_=xlt2_sb, inc=[(qS, 16)])

                for t in range(NT):
                    g = g0 + t
                    pr = t % 2
                    rcol = 0 if pr == 0 else 128
                    ccol = 256 if pr == 0 else 320
                    tcol = 384 if pr == 0 else 448
                    for c in range(NCH):
                        kb.op(T, "matmul", ps3[:, rcol:rcol + 72],
                              xlt2_sb[:, c * SL + t * 128: c * SL + (t + 1) * 128],
                              gate_sb[:, c * 72:(c + 1) * 72],
                              start=(c == 0), stop=(c == NCH - 1),
                              inc=([(mm, 1)] if c == NCH - 1 else None))
                    kb.wait(S, mm, mm.n)
                    kb.op(S, "copy", lgt_sb[pr][:, 0:72], ps3[:, rcol:rcol + 72], inc=[(se, 1)])
                    lg_ev = se.n
                    kb.op(S, "activation", out=sg_sb[:, g:g + 1], in_=lgt_sb[pr][:, 64:65],
                          func=AF.Sigmoid, bias=0.0, scale=1.0, inc=[(se, 1)])
                    if dev:
                        kb.op(S, "dma_start", out=lg_d[g][:, 0:72], in_=lgt_sb[pr][:, 0:72],
                              inc=[(qS, 16)])
                        kb.wait(S, qS, qS.n)
                    kb.wait(V, se, lg_ev)
                    kb.op(V, "max", out=m8_sb[pr], in_=lgt_sb[pr][:, 0:64], inc=[(ve, 1)])
                    kb.op(V, "match_replace", out=repl_sb, in_to_replace=m8_sb[pr],
                          in_values=lgt_sb[pr][:, 0:64], imm_value=-1e30, inc=[(ve, 1)])
                    kb.op(V, "tensor_tensor", out=selt_sb[pr], in0=lgt_sb[pr][:, 0:64], in1=repl_sb,
                          op=ALU.not_equal, inc=[(ve, 1)])
                    kb.op(V, "tensor_scalar", out=rsum_sb[:, 0:1], in0=m8_sb[pr][:, 0:1],
                          scalar1=-1.0, scalar2=None, op0=ALU.mult, inc=[(ve, 1)])
                    kb.wait(S, ve, ve.n)
                    kb.op(S, "activation", out=eall_sb, in_=lgt_sb[pr][:, 0:64], func=AF.Exp,
                          bias=rsum_sb[:, 0:1], scale=1.0, inc=[(se, 1)])
                    kb.wait(V, se, se.n)
                    kb.wait(V, qWM[pr], qWM[pr].n)
                    kb.op(V, "tensor_tensor", out=wm3_sb[pr], in0=eall_sb, in1=selt_sb[pr],
                          op=ALU.mult, inc=[(ve, 1)])
                    kb.op(V, "tensor_reduce", out=rsum_sb[:, 1:2], in_=wm3_sb[pr], axis=AX.X,
                          op=ALU.add, inc=[(ve, 1)])
                    kb.op(V, "reciprocal", out=rsum_sb[:, 2:3], in_=rsum_sb[:, 1:2], inc=[(ve, 1)])
                    kb.op(V, "tensor_scalar", out=wm3_sb[pr], in0=wm3_sb[pr],
                          scalar1=rsum_sb[:, 2:3], scalar2=None, op0=ALU.mult, inc=[(ve, 1)])
                    wsel_done = ve.n
                    kb.wait(S, ve, wsel_done)
                    kb.op(S, "dma_start", out=wa_d[s, :, t * E:(t + 1) * E], in_=wm3_sb[pr],
                          inc=[(qWM[pr], 16)])
                    kb.op(V, "tensor_tensor", out=eall_sb, in0=selt_sb[pr], in1=c64p1_sb[:],
                          op=ALU.mult, inc=[(ve, 1)])
                    kb.op(V, "max", out=ed8_sb, in_=eall_sb, inc=[(ve, 1)])
                    kb.op(V, "tensor_scalar", out=edesc_sb[:, g * 8:(g + 1) * 8], in0=ed8_sb,
                          scalar1=-1.0, scalar2=None, op0=ALU.add, inc=[(ve, 1)])
                    kb.wait(T, ve, ve.n)
                    kb.op(T, "matmul", ps4[:, ccol - 256:ccol - 192], ustrict_sb[:], selt_sb[pr],
                          start=True, stop=True, inc=[(mm, 1)])
                    kb.op(T, "matmul", ps4[0:1, tcol - 256:tcol - 192], onec_bf_sb[:], selt_sb[pr],
                          start=True, stop=True, inc=[(mm, 1)])
                    cs_mm = mm.n
                    kb.wait(S, mm, cs_mm)
                    kb.wait(S, qCL[pr], qCL[pr].n)
                    kb.op(S, "copy", clt_sb[pr], ps4[:, ccol - 256:ccol - 192], inc=[(se, 1)])
                    kb.op(S, "copy", tot_sb[0:1, pr * 64:pr * 64 + 64],
                          ps4[0:1, tcol - 256:tcol - 192], inc=[(se, 1)])
                    ev_c = se.n
                    kb.wait(V, se, ev_c)
                    kb.op(V, "tensor_copy", out=rowbase_sb[0:1, g * E:(g + 1) * E],
                          in_=cnt1_sb[0:1, s * E:(s + 1) * E], inc=[(ve, 1)])
                    kb.op(V, "tensor_tensor", out=cnt1_sb[0:1, s * E:(s + 1) * E],
                          in0=cnt1_sb[0:1, s * E:(s + 1) * E], in1=tot_sb[0:1, pr * 64:pr * 64 + 64],
                          op=ALU.add, inc=[(ve, 1)])
                    kb.wait(S, ve, ve.n)
                    kb.op(S, "dma_start", out=cl_d[s, :, t * E:(t + 1) * E], in_=clt_sb[pr],
                          inc=[(qCL[pr], 16)])

        if max_phase >= 4:
            kb.wait(G, mm, mm_p1)
            for xh in range(2):
                kb.op(G, "dma_start",
                      out=sguv.rearrange("p (c x f) -> p c x f", x=2, f=2048)[:, :, xh, :],
                      in_=sgu_w[:, xh * 2048:(xh + 1) * 2048].rearrange("(c p) f -> p c f", p=128),
                      inc=[(qW5, 16)])
            kb.op(G, "dma_start", out=wv(65536, 16384, BF16).rearrange("p (c f) -> p c f", f=SL),
                  in_=sdown_w[:, 0:SL].rearrange("(c p) f -> p c f", p=128), inc=[(qW5, 16)])
            kb.op(G, "dma_start", out=wv(81920, 16384, BF16).rearrange("p (c f) -> p c f", f=SL),
                  in_=sdown_w[:, SL:D].rearrange("(c p) f -> p c f", p=128), inc=[(qW5, 16)])
            sd0_sb = wv(65536, 16384, BF16)
            sd1_sb = wv(81920, 16384, BF16)
            p5w_ld = qW5.n

            kb.wait(S, ve, ve.n)
            kb.op(S, "dma_start", out=cnt_in.rearrange("s e -> (s e)").unsqueeze(0),
                  in_=cnt1_sb[0:1, :], inc=[(qS, 16)])
            kb.wait(G, qS, qS.n)
            kb.op(G, "collective_compute", "AllGather", ALU.bypass,
                  replica_groups=[list(range(NCORES))],
                  ins=[cnt_in.opt()], outs=[cnt_out.opt()], inc=[(qCcnt, 1)])
            kb.op(G, "collective_compute", "AllGather", ALU.bypass,
                  replica_groups=[list(range(NCORES))],
                  ins=[x2l_d.opt()], outs=[x2a_d.opt()], inc=[(qCx2, 1)])
            kb.wait(Y, qCcnt, qCcnt.n)
            kb.wait(Y, mm, mm.n)
            kb.wait(Y, ve, ve.n)
            kb.wait(Y, se, se.n)
            for q_ in (qCL[0], qCL[1], qWM[0], qWM[1], qX2[0], qX2[1]):
                kb.wait(Y, q_, q_.n)
            kb.op(Y, "dma_start", out=cnts_sb[0:64, 0:64], in_=cnt_out[:, :], inc=[(qL, 16)])
            ld_cnt = qL.n
            kb.wait(T, qL, ld_cnt)
            for s_ in range(NSEG):
                kb.op(T, "matmul", ps2[0:1, s_ * 64:(s_ + 1) * 64],
                      maskcols_sb[:, s_:s_ + 1], cnts_sb[0:64, 0:64],
                      start=True, stop=True, inc=[(mm, 1)])
            base_mm = mm.n
            kb.wait(V, mm, base_mm)
            for s_ in range(NSEG):
                kb.op(V, "tensor_tensor",
                      out=rowbase_sb[0:1, s_ * 4 * E:(s_ + 1) * 4 * E].rearrange("o (t e) -> o t e", e=E),
                      in0=rowbase_sb[0:1, s_ * 4 * E:(s_ + 1) * 4 * E].rearrange("o (t e) -> o t e", e=E),
                      in1=ps2[0:1, s_ * 64:(s_ + 1) * 64].unsqueeze(1).broadcast_to([1, 4, 64]),
                      op=ALU.add, inc=[(ve, 1)])
            rb_done = ve.n
            kb.wait(T, ve, rb_done)

            smk = {"n": 0}
            posh_guard = [0] * NPOSH
            hl_guard = [0] * NHL

            def p4_tile(g):
                s_, t = g // 4, g % 4
                par = g % NHL
                if t == 0:
                    kb.wait(Y, ve, ve.n)
                    kb.op(Y, "dma_start", out=clseg_sb, in_=cl_d[s_], inc=[(qL, 16)])
                    kb.op(Y, "dma_start", out=waseg_sb, in_=wa_d[s_], inc=[(qL, 16)])
                    kb.wait(V, qL, qL.n)
                kb.wait(T, ve, ve.n)
                kb.op(T, "matmul", ps2[:, 0:64], ones_f_sb[:],
                      rowbase_sb[0:1, g * E:(g + 1) * E], start=True, stop=True, inc=[(mm, 1)])
                kb.wait(V, mm, mm.n)
                kb.op(V, "tensor_tensor", out=cg_sb, in0=clseg_sb[:, t * E:(t + 1) * E],
                      in1=ps2[:, 0:64], op=ALU.add, inc=[(ve, 1)])
                kb.wait(T, ve, ve.n)
                kb.op(V, "tensor_tensor",
                      out=mask_sb.rearrange("p (k e) -> p k e", e=E),
                      in0=c64_sb[:].unsqueeze(1).broadcast_to([128, 8, E]),
                      in1=edesc_sb[:, g * 8:(g + 1) * 8].unsqueeze(2).broadcast_to([128, 8, E]),
                      op=ALU.is_equal, inc=[(ve, 1)])
                kb.op(V, "tensor_tensor",
                      out=wmp_sb.rearrange("p (k e) -> p k e", e=E),
                      in0=mask_sb.rearrange("p (k e) -> p k e", e=E),
                      in1=waseg_sb[:, t * E:(t + 1) * E].unsqueeze(1).broadcast_to([128, 8, E]),
                      op=ALU.mult, inc=[(ve, 1)])
                kb.wait(V, mm, hl_guard[par])
                kb.op(V, "tensor_scalar",
                      out=mhl_sb[par].rearrange("p (k x e) -> p k x e", x=2, e=E)[:, :, 0, :],
                      in0=mask_sb.rearrange("p (k e) -> p k e", e=E),
                      scalar1=tok_hi_sb[:, g:g + 1], scalar2=None, op0=ALU.mult, inc=[(ve, 1)])
                kb.op(V, "tensor_scalar",
                      out=mhl_sb[par].rearrange("p (k x e) -> p k x e", x=2, e=E)[:, :, 1, :],
                      in0=mask_sb.rearrange("p (k e) -> p k e", e=E),
                      scalar1=tok_lo_sb[:, g:g + 1], scalar2=None, op0=ALU.mult, inc=[(ve, 1)])
                kb.op(V, "tensor_copy",
                      out=whl_sb[par].rearrange("p (k x e) -> p k x e", x=2, e=E)[:, :, 0, :],
                      in_=wmp_sb.rearrange("p (k e) -> p k e", e=E), inc=[(ve, 1)])
                kb.op(V, "tensor_tensor",
                      out=whl_sb[par].rearrange("p (k x e) -> p k x e", x=2, e=E)[:, :, 1, :],
                      in0=wmp_sb.rearrange("p (k e) -> p k e", e=E),
                      in1=whl_sb[par].rearrange("p (k x e) -> p k x e", x=2, e=E)[:, :, 0, :],
                      op=ALU.subtract, inc=[(ve, 1)])
                for k in range(8):
                    kb.op(V, "scalar_tensor_tensor", out=junk_sb,
                          in0=mask_sb[:, k * E:(k + 1) * E], scalar=1.0, in1=cg_sb,
                          op0=ALU.mult, op1=ALU.mult,
                          accum_out=pos_sb[:, g * 8 + k: g * 8 + k + 1], inc=[(ve, 1)])
                kb.wait(G, ve, ve.n)
                for k in range(8):
                    pp = (g * 8 + k) % NPOSH
                    kb.wait(G, mm, posh_guard[pp])
                    kb.op(G, "tensor_scalar", out=posh_sb[pp],
                          in0=c512_sb[:], scalar1=pos_sb[:, g * 8 + k: g * 8 + k + 1],
                          scalar2=None, op0=ALU.is_equal, inc=[(gv, 1)])
                    kb.wait(T, gv, gv.n)
                    first = smk["n"] == 0
                    last = (g == 31 and k == 7)
                    kb.op(T, "matmul", ps3[:], mhl_sb[par][:, k * 128:(k + 1) * 128],
                          posh_sb[pp], start=first, stop=last, skip_group_check=True)
                    kb.op(T, "matmul", ps4[:], whl_sb[par][:, k * 128:(k + 1) * 128],
                          posh_sb[pp], start=first, stop=last, skip_group_check=True,
                          inc=[(mm, 1)])
                    posh_guard[pp] = mm.n
                    smk["n"] += 1
                hl_guard[par] = mm.n

            def p5_seg(s_):
                kb.wait(Y, mm, mm.n)
                kb.wait(Y, ve, ve.n)
                kb.wait(Y, se, se.n)
                kb.wait(Y, qS, qS.n)
                kb.op(Y, "dma_start", out=p5xlt_sb, in_=xlt2_d[s_], inc=[(qL, 16)])
                ld_xlt = qL.n
                if s_ == 0:
                    kb.wait(Y, qE[0], qE[0].n)
                    kb.wait(Y, qE[1], qE[1].n)
                    kb.wait(T, qW5, p5w_ld)
                kb.wait(T, qL, ld_xlt)
                for fc in range(16):
                    for c in range(NCH):
                        kb.op(T, "matmul", ps0[:],
                              sguv[:, c * 4096 + fc * 128: c * 4096 + (fc + 1) * 128],
                              p5xlt_sb[:, c * SL:(c + 1) * SL],
                              start=(c == 0), stop=(c == NCH - 1))
                    for c in range(NCH):
                        kb.op(T, "matmul", ps1[:],
                              sguv[:, c * 4096 + (fc + 16) * 128: c * 4096 + (fc + 17) * 128],
                              p5xlt_sb[:, c * SL:(c + 1) * SL],
                              start=(c == 0), stop=(c == NCH - 1),
                              inc=([(mm, 1)] if c == NCH - 1 else None))
                    gu_mm = mm.n
                    kb.wait(S, mm, gu_mm)
                    kb.op(S, "activation", out=silu_sb, in_=ps0[:], func=AF.Silu,
                          bias=0.0, scale=1.0, inc=[(se, 1)])
                    kb.wait(V, se, se.n)
                    kb.op(V, "tensor_tensor", out=h_sb[:, fc * SL:(fc + 1) * SL],
                          in0=silu_sb, in1=ps1[:], op=ALU.mult, inc=[(ve, 1)])
                    kb.wait(T, ve, ve.n)
                for t in range(NT):
                    g = s_ * 4 + t
                    for half, sdw, pso in ((0, sd0_sb, ps0), (1, sd1_sb, ps1)):
                        for c in range(16):
                            kb.op(T, "matmul", pso[:],
                                  h_sb[:, c * SL + t * 128: c * SL + (t + 1) * 128],
                                  sdw[:, c * SL:(c + 1) * SL],
                                  start=(c == 0), stop=(c == 15),
                                  inc=([(mm, 1)] if c == 15 else None))
                    dn_mm = mm.n
                    kb.wait(Y, ve, ve.n)
                    kb.op(Y, "dma_start", out=p5x_sb, in_=x2_d[s_, t * 128:(t + 1) * 128, :],
                          inc=[(qL, 16)])
                    kb.wait(V, qL, qL.n)
                    kb.wait(V, mm, dn_mm)
                    kb.op(V, "scalar_tensor_tensor", out=p5s_sb[:, 0:SL], in0=ps0[:],
                          scalar=sg_sb[:, g:g + 1], in1=p5x_sb[:, 0:SL],
                          op0=ALU.mult, op1=ALU.add, inc=[(ve, 1)])
                    kb.op(V, "scalar_tensor_tensor", out=p5s_sb[:, SL:2 * SL], in0=ps1[:],
                          scalar=sg_sb[:, g:g + 1], in1=p5x_sb[:, SL:D],
                          op0=ALU.mult, op1=ALU.add, inc=[(ve, 1)])
                    x2s_done = ve.n
                    kb.wait(S, ve, x2s_done)
                    kb.op(S, "dma_start", out=x2s_d[s_, t * 128:(t + 1) * 128, :],
                          in_=p5s_sb, inc=[(qXS[s_ // 2], 16)])
                    kb.wait(T, ve, x2s_done)
                    kb.wait(V, qXS[s_ // 2], qXS[s_ // 2].n)

            for s_ in range(NSEG):
                for t in range(NT):
                    p4_tile(s_ * 4 + t)
            sm_mm = mm.n

            kb.wait(V, mm, sm_mm)
            kb.op(V, "tensor_scalar", out=eh_sb[0], in0=edesc_sb[:], scalar1=8.0, scalar2=None,
                  op0=ALU.is_ge, inc=[(ve, 1)])
            for i, thr in enumerate((16.0, 24.0, 32.0, 40.0, 48.0, 56.0)):
                kb.op(V, "scalar_tensor_tensor", out=eh_sb[(i + 1) % 2], in0=edesc_sb[:],
                      scalar=thr, in1=eh_sb[i % 2], op0=ALU.is_ge, op1=ALU.add, inc=[(ve, 1)])
            ehf = eh_sb[0]
            kb.op(V, "scalar_tensor_tensor", out=a1_sb, in0=edesc_sb[:], scalar=512.0,
                  in1=ehf, op0=ALU.mult, op1=ALU.add, inc=[(ve, 1)])
            kb.op(V, "tensor_tensor", out=a1_sb, in0=a1_sb, in1=pos_sb[:], op=ALU.add, inc=[(ve, 1)])
            kb.op(V, "tensor_scalar", out=alive_sb, in0=pos_sb[:], scalar1=512.0, scalar2=None,
                  op0=ALU.is_lt, inc=[(ve, 1)])
            kb.op(V, "tensor_scalar", out=a1_sb, in0=a1_sb, scalar1=zrow_sb[:, 0:1], scalar2=None,
                  op0=ALU.subtract, inc=[(ve, 1)])
            kb.op(V, "tensor_tensor", out=a1_sb, in0=a1_sb, in1=alive_sb, op=ALU.mult, inc=[(ve, 1)])
            kb.op(V, "tensor_scalar", out=addrf_sb[:], in0=a1_sb, scalar1=zrow_sb[:, 0:1],
                  scalar2=None, op0=ALU.add, inc=[(ve, 1)])
            kb.op(V, "tensor_copy", out=addru_sb[:], in_=addrf_sb[:], inc=[(ve, 1)])
            if dev:
                kb.wait(S, ve, ve.n)
                kb.op(S, "dma_start", out=dbg_d[:, 32:288], in_=edesc_sb[:], inc=[(qS, 16)])
                kb.op(S, "dma_start", out=dbg_d[:, 288:544], in_=pos_sb[:], inc=[(qS, 16)])
                kb.op(S, "dma_start", out=dbg_d[:, 544:800], in_=addrf_sb[:], inc=[(qS, 16)])

            kb.wait(S, mm, sm_mm)
            kb.op(S, "copy", smhlb_sb2[:, 0:SL], ps3[:], inc=[(se, 1)])
            kb.op(S, "copy", smhlb_sb2[:, SL:2 * SL], ps4[:], inc=[(se, 1)])
            hl_ev = se.n
            kb.wait(T, se, hl_ev)
            kb.op(T, "matmul", ps2[0:64, 0:SL], comb_sb[:, 0:64], smhlb_sb2[:, 0:SL],
                  start=True, stop=True, inc=[(mm, 1)])
            kb.wait(S, mm, mm.n)
            kb.op(S, "copy", smv_sb[0:64, :], ps2[0:64, 0:SL], inc=[(se, 1)])
            kb.wait(T, se, se.n)
            kb.op(T, "matmul", ps2[0:64, 0:SL], comb_sb[:, 64:128], smhlb_sb2[:, SL:2 * SL],
                  start=True, stop=True, inc=[(mm, 1)])
            kb.wait(S, mm, mm.n)
            kb.op(S, "copy", wv_sb[0:64, :], ps2[0:64, 0:SL], inc=[(se, 1)])
            smw_done = se.n
            kb.wait(S, se, smw_done)
            for r in range(NCORES):
                kb.op(S, "dma_start", out=smw_in[16 * r:16 * r + 8, :],
                      in_=smv_sb[8 * r:8 * r + 8, :], inc=[(qS, 16)])
                kb.op(S, "dma_start", out=smw_in[16 * r + 8:16 * r + 16, :],
                      in_=wv_sb[8 * r:8 * r + 8, :], inc=[(qS, 16)])
            kb.wait(G, qS, qS.n)
            kb.op(G, "collective_compute", "ReduceScatter", ALU.add,
                  replica_groups=[list(range(NCORES))],
                  ins=[smw_in.opt()], outs=[smw_out.opt()], inc=[(qCsm, 1)])
            kb.wait(Y, qCsm, qCsm.n)
            kb.op(Y, "dma_start", out=smloc_sb[0:16, :], in_=smw_out[:, :], inc=[(qL, 16)])
            kb.wait(T, qL, qL.n)
            for t in range(NT):
                kb.op(T, "transpose", ps2[:, t * 16:(t + 1) * 16],
                      smloc_sb[0:16, t * 128:(t + 1) * 128], identf_sb[0:16, 0:16],
                      inc=([(mm, 1)] if t == NT - 1 else None))
            kb.wait(V, mm, mm.n)
            for t in range(NT):
                kb.op(V, "tensor_scalar", out=smT_sb[:, t * 8:(t + 1) * 8],
                      in0=ps2[:, t * 16: t * 16 + 8], scalar1=-1.0, scalar2=0.0,
                      op0=ALU.add, op1=ALU.max, inc=[(ve, 1)])
                kb.op(V, "tensor_copy", out=wt_sb[:, t * 8:(t + 1) * 8],
                      in_=ps2[:, t * 16 + 8: t * 16 + 16], inc=[(ve, 1)])
            kb.op(V, "tensor_copy", out=smTu_sb[:, 0:32], in_=smT_sb[:, 0:32], inc=[(ve, 1)])
            p4_done = ve.n
            if dev:
                kb.wait(S, ve, p4_done)
                kb.op(S, "dma_start", out=dbg_d[:, 800:832], in_=smT_sb[:, 0:32], inc=[(qS, 16)])
                kb.op(S, "dma_start", out=dbg_d[:, 832:864], in_=wt_sb[:, 0:32], inc=[(qS, 16)])

        if max_phase >= 6:
            kb.wait(V, qS, qS.n)
            kb.op(V, "memset", zero_sb[0:1, :], 0.0, inc=[(ve, 1)])
            kb.wait(S, ve, ve.n)
            kb.op(S, "dma_start", out=x4l_d[X4ROWS - 1: X4ROWS, :], in_=zero_sb[0:1, :],
                  inc=[(qS, 16)])
            kb.wait(S, qS, qS.n)
            kb.wait(G, ve, ve.n)
            kb.wait(G, qCx2, qCx2.n)
            kb.wait(G, qS, qS.n)
            kb.wait(G, qG, qG.n)

            def load_exp(e, buf):
                kb.op(G, "dma_start", out=egu_sb[buf].rearrange("p (c f) -> p c f", f=D),
                      in_=egu_w[e].rearrange("(c p) f -> p c f", p=128), inc=[(qE[buf], 16)])
                kb.op(G, "dma_start", out=edn_sb[buf].rearrange("p (c f) -> p c f", f=D),
                      in_=edown_w[e].rearrange("(c p) f -> p c f", p=128), inc=[(qE[buf], 16)])

            kb.wait(G, mm, mm.n)
            load_exp(0, 0)
            for e in range(EL):
                buf = e % 2
                w_ld = qE[buf].n
                if e + 1 < EL:
                    kb.wait(G, tE, max(0, e))
                    load_exp(e + 1, (e + 1) % 2)
                for t in range(NT):
                    kb.wait(G, mm, mm.n)
                    kb.op(G, "indirect_dma_start", out=rows_sb, out_offset=None,
                          in_=x2a_d, in_offset=IOA(smTu_sb[:, t * 8 + e: t * 8 + e + 1], 0),
                          inc=[(qG, 16)])
                    kb.wait(T, qG, qG.n)
                    for c in range(NCH):
                        kb.op(T, "transpose", ps_tr[:, (c % 4) * 128:(c % 4 + 1) * 128],
                              rows_sb[:, c * 128:(c + 1) * 128], ident_sb[:],
                              inc=[(mm, 1)] if (c % 4 == 3) else None)
                        if c % 4 == 3:
                            kb.wait(V, mm, mm.n)
                            for cc in range(c - 3, c + 1):
                                kb.op(V, "tensor_copy",
                                      out=rowsT_sb[:, cc * SL + t * 128: cc * SL + (t + 1) * 128],
                                      in_=ps_tr[:, (cc % 4) * 128:(cc % 4 + 1) * 128],
                                      inc=([(ve, 1)] if cc % 4 == 3 else None))
                            kb.wait(T, ve, ve.n)
                rows_done = ve.n
                kb.wait(T, qE[buf], w_ld)
                if e == 0:
                    kb.wait(T, se, se.n)
                for fc in range(4):
                    for c in range(NCH):
                        kb.op(T, "matmul", ps3[:],
                              egu_sb[buf][:, c * D + fc * 128: c * D + (fc + 1) * 128],
                              rowsT_sb[:, c * SL:(c + 1) * SL],
                              start=(c == 0), stop=(c == NCH - 1))
                    for c in range(NCH):
                        kb.op(T, "matmul", ps4[:],
                              egu_sb[buf][:, c * D + (fc + 4) * 128: c * D + (fc + 5) * 128],
                              rowsT_sb[:, c * SL:(c + 1) * SL],
                              start=(c == 0), stop=(c == NCH - 1),
                              inc=([(mm, 1)] if c == NCH - 1 else None))
                    kb.wait(S, mm, mm.n)
                    kb.op(S, "activation", out=silu6_sb, in_=ps3[:], func=AF.Silu,
                          bias=0.0, scale=1.0, inc=[(se, 1)])
                    kb.wait(V, se, se.n)
                    kb.op(V, "tensor_tensor", out=he_sb[:, fc * SL:(fc + 1) * SL],
                          in0=silu6_sb, in1=ps4[:], op=ALU.mult, inc=[(ve, 1)])
                    kb.wait(T, ve, ve.n)
                for t in range(NT):
                    for half, pso in ((0, ps3), (1, ps4)):
                        for c in range(4):
                            kb.op(T, "matmul", pso[:],
                                  he_sb[:, c * SL + t * 128: c * SL + (t + 1) * 128],
                                  edn_sb[buf][:, c * D + half * SL: c * D + half * SL + SL],
                                  start=(c == 0), stop=(c == 3),
                                  inc=([(mm, 1)] if c == 3 else None))
                    dn_mm = mm.n
                    kb.wait(V, mm, dn_mm)
                    kb.wait(V, qS, qS.n)
                    kb.op(V, "tensor_scalar", out=oute_sb[:, 0:SL], in0=ps3[:],
                          scalar1=wt_sb[:, t * 8 + e: t * 8 + e + 1], scalar2=None,
                          op0=ALU.mult, inc=[(ve, 1)])
                    kb.op(V, "tensor_scalar", out=oute_sb[:, SL:2 * SL], in0=ps4[:],
                          scalar1=wt_sb[:, t * 8 + e: t * 8 + e + 1], scalar2=None,
                          op0=ALU.mult, inc=[(ve, 1)])
                    kb.wait(S, ve, ve.n)
                    kb.op(S, "dma_start", out=x4l_d[e * CAP + t * 128: e * CAP + (t + 1) * 128, :],
                          in_=oute_sb, inc=[(qS, 16)])
                    kb.wait(T, ve, ve.n)
                tE.inc(1)
                kb.emit(T, lambda eng, s_=tE: eng.sem_inc(s_.h, 1))
            kb.wait(G, qS, qS.n)
            kb.wait(G, qG, qG.n)
            kb.op(G, "collective_compute", "AllGather", ALU.bypass,
                  replica_groups=[list(range(NCORES))],
                  ins=[x4l_d.opt()], outs=[x4a_d.opt()], inc=[(qCx4, 1)])
            p6_mm = mm.n
            p6_ve = ve.n
            p6_se = se.n

            for s_ in range(NSEG):
                p5_seg(s_)

        if max_phase >= 7:
            kb.wait(G, qCx4, qCx4.n)
            kb.wait(G, ve, p6_ve)
            kb.wait(G, se, p6_se)
            for grp in range(4):
                tiles = list(range(grp * 8, (grp + 1) * 8))
                kb.wait(Y, mm, p6_mm)
                kb.wait(Y, ve, p6_ve)
                kb.wait(Y, qE[0], qE[0].n)
                kb.wait(Y, qE[1], qE[1].n)
                kb.wait(Y, qG, qG.n)
                kb.wait(Y, qXS[grp], qXS[grp].n)
                kb.wait(Y, qS, qS.n)
                for i, gt in enumerate(tiles):
                    s, t = gt // 4, gt % 4
                    kb.op(Y, "dma_start", out=fin_sb[i], in_=x2s_d[s, t * 128:(t + 1) * 128, :],
                          inc=[(qL, 16)])
                kb.wait(G, qL, qL.n)
                for k in range(8):
                    if k > 0:
                        kb.wait(G, qG, qG.n)
                    for i, gt in enumerate(tiles):
                        kb.op(G, "indirect_dma_start", out=fin_sb[i], out_offset=None,
                              in_=x4a_d, in_offset=IOA(addru_sb[:, gt * 8 + k: gt * 8 + k + 1], 0),
                              compute_op=ALU.add, inc=[(qG, 16)])
                kb.wait(Y, qG, qG.n)
                for i, gt in enumerate(tiles):
                    kb.op(Y, "dma_start", out=out_d[gt * 128:(gt + 1) * 128, :], in_=fin_sb[i],
                          inc=[(qS, 16)])

        kb.wait(Y, qS, qS.n)
        kb.wait(Y, qG, qG.n)
        for _s in (qCkv, qCcnt, qCx2, qCsm, qCx4, qE[0], qE[1], qW5,
                   qX2[0], qX2[1], qCL[0], qCL[1], qWM[0], qWM[1],
                   qXS[0], qXS[1], qXS[2], qXS[3]):
            kb.wait(Y, _s, _s.n)
        kb.wait(Y, mm, mm.n)
        kb.wait(Y, ve, ve.n)
        kb.wait(Y, se, se.n)
        kb.wait(Y, qL, qL.n)

        with nc.Block() as block:
            @block.tensor
            def _(eng):
                for en, fn in kb.ops:
                    if en == T:
                        fn(eng)

            @block.vector
            def _(eng):
                for en, fn in kb.ops:
                    if en == V:
                        fn(eng)

            @block.scalar
            def _(eng):
                for en, fn in kb.ops:
                    if en == S:
                        fn(eng)

            @block.gpsimd
            def _(eng):
                for en, fn in kb.ops:
                    if en == G:
                        fn(eng)

            @block.sync
            def _(eng):
                for en, fn in kb.ops:
                    if en == Y:
                        fn(eng)

    return nc


def _perm_evenodd():
    q = np.arange(1024)
    p = np.concatenate([q[0::2], q[1::2]])
    return np.concatenate([p, 1024 + p])


def prepare_inputs(inputs):
    x = np.asarray(inputs["L0_attn_x"])
    wqk = np.asarray(inputs["L0_attn_qk_w"])
    bqk = np.asarray(inputs["L0_attn_qk_b"])
    lepe_w = np.asarray(inputs["L0_attn_lepe_w"])
    lepe_b = np.asarray(inputs["L0_attn_lepe_b"])
    gate_w = np.asarray(inputs["L0_ffn_gate_w"])
    sg_w = np.asarray(inputs["L0_ffn_shared_gate_w"])
    sgu_w = np.asarray(inputs["L0_ffn_shared_gate_up_w"])
    sdown_w = np.asarray(inputs["L0_ffn_shared_down_w"])
    egu_w = np.asarray(inputs["L0_ffn_expert_gate_up_w"])
    edown_w = np.asarray(inputs["L0_ffn_expert_down_w"])

    perm = _perm_evenodd()
    wqk_p = np.ascontiguousarray(wqk[:, perm]).astype(np.float32)
    bqk_p = bqk[perm]
    bqk_cols = np.ascontiguousarray(bqk_p.reshape(16, 128).T).astype(np.float32)

    gate_ext = np.zeros((D, 72), np.float32)
    gate_ext[:, :64] = gate_w
    gate_ext[:, 64:65] = sg_w

    theta = 1.0 / (10000.0 ** (np.arange(512, dtype=np.float32) / 512.0))
    c64c = np.tile(np.arange(E, dtype=np.float32), (128, 1))
    c512c = np.tile(np.arange(CAP, dtype=np.float32), (128, 1))
    ustrictc = np.triu(np.ones((128, 128), np.float32), 1).astype(BF)
    identc = np.eye(128, dtype=BF)
    identfc = np.eye(128, dtype=np.float32)

    in_maps = []
    for c in range(NCORES):
        segs = [(b, c + 8 * h) for b in range(B) for h in range(2)]
        x_shard = np.stack([x[b, 512 * j:512 * (j + 1), :] for (b, j) in segs])
        halo = np.zeros((NSEG, 2, D), np.float32)
        for si, (b, j) in enumerate(segs):
            if j > 0:
                halo[si, 0] = x[b, 512 * j - 1]
            if j < 15:
                halo[si, 1] = x[b, 512 * (j + 1)]
        cost = np.zeros((NSEG, 4, 128, SL), np.float32)
        sint = np.zeros((NSEG, 4, 128, SL), np.float32)
        for si, (b, j) in enumerate(segs):
            pos = np.arange(512 * j, 512 * (j + 1), dtype=np.float32)
            ang = pos[None, :] * theta[:, None]
            cost[si] = np.cos(ang).reshape(4, 128, SL)
            sint[si] = np.sin(ang).reshape(4, 128, SL)

        loc = np.arange(NSEG * SL).reshape(32, 128).T
        v = (c * NSEG * SL + loc + 1).astype(np.float32)
        tok_hic = np.floor(v / 256.0).astype(np.float32)
        tok_loc = (v - 256.0 * tok_hic).astype(np.float32)

        maskcol = np.zeros((E, 8), np.float32)
        for s_own in range(NSEG):
            b_own, h_own = s_own // 2, s_own % 2
            g_own = 16 * b_own + c + 8 * h_own
            for r in range(NCORES):
                for l in range(8):
                    bb, hh = l // 2, l % 2
                    g = 16 * bb + r + 8 * hh
                    if g < g_own:
                        maskcol[r * 8 + l, s_own] = 1.0

        comb = np.zeros((128, 128), np.float32)
        for e in range(64):
            comb[e, e] = 256.0
            comb[e + 64, e] = 1.0
            comb[e, 64 + e] = 1.0
            comb[e + 64, 64 + e] = 1.0
        in_maps.append(dict(
            x_shard=x_shard, x_halo=halo, comb_bf=comb.astype(BF),
            wqk_p=wqk_p, bqk_cols=bqk_cols,
            lepe_w=lepe_w.astype(np.float32), lepe_b=lepe_b.reshape(1, D).astype(np.float32),
            costab=cost.astype(BF), sintab=sint.astype(BF),
            gate_ext=gate_ext, sgu_w=sgu_w.astype(np.float32),
            sdown_w=sdown_w.astype(np.float32),
            egu_w=np.ascontiguousarray(egu_w[8 * c:8 * c + 8]).astype(np.float32),
            edown_w=np.ascontiguousarray(edown_w[8 * c:8 * c + 8]).astype(np.float32),
            c64=c64c, c64p1=c64c + 1.0, c512=c512c,
            ustrict=ustrictc, ident_bf=identc, ident_f=identfc,
            ones_bf=np.ones((1, 128), BF), onec_bf=np.ones((128, 1), BF),
            ones_f=np.ones((1, 128), np.float32),
            tok_hi=tok_hic, tok_lo=tok_loc, maskcols=maskcol,
            zrow_col=np.full((128, 1), float(c * X4ROWS + X4ROWS - 1), np.float32),
            eps_col=np.full((128, 1), 1e-5, np.float32),
        ))
    return in_maps


def assemble_output(results):
    out = np.zeros((B, N, D), np.float32)
    for c in range(NCORES):
        r = results[c]["out_d"].reshape(NSEG, SL, D)
        segs = [(b, c + 8 * h) for b in range(B) for h in range(2)]
        for si, (b, j) in enumerate(segs):
            out[b, 512 * j:512 * (j + 1), :] = r[si]
    return out


def kernel(**inputs):
    in_maps = prepare_inputs(inputs)
    nc = build_program()
    res = run_bass_kernel_spmd(nc, in_maps, core_ids=list(range(NCORES)))
    return assemble_output(res.results)


# version 10
# speedup vs baseline: 1.4014x; 1.4014x over previous
import numpy as np
import ml_dtypes
from contextlib import ExitStack

import concourse.bass as bass
import concourse.mybir as mybir
from concourse.bass_utils import run_bass_kernel_spmd

F32 = mybir.dt.float32
BF16 = mybir.dt.bfloat16
U8 = mybir.dt.uint8
U32 = mybir.dt.uint32
AF = mybir.ActivationFunctionType
ALU = mybir.AluOpType
AX = mybir.AxisListType
IOA = bass.IndirectOffsetOnAxis
BF = ml_dtypes.bfloat16

NCORES = 8
B, N, D = 4, 8192, 1024
NSEG, SL, NT = 8, 512, 4
NCH, NH, HD = 8, 8, 128
E, EL, CAP = 64, 8, 512
ISQN = float(1.0 / np.sqrt(np.float32(N)))
X4ROWS = EL * CAP + 1
MAX_PHASE = 7

T, V, S, G, Y = "tensor", "vector", "scalar", "gpsimd", "sync"


class Sem:
    def __init__(self, name, dma=False):
        self.name, self.n, self.h = name, 0, None
        self.dma = dma
        self.max_waited = 0
        self.last_sync = 0

    def inc(self, k=1):
        self.n += k
        return self.n


class KB:
    def __init__(self):
        self.ops = []
        self.auto = {}

    def emit(self, en, fn):
        self.ops.append((en, fn))

    def op(self, en, meth, *args, inc=None, **kw):
        aset = self.auto.get(en)
        if aset is not None:
            v = aset.n
            if v:
                self.emit(en, lambda eng, s=aset, v=v: eng.wait_ge(s.h, v))
            if meth != "dma_start":
                inc = list(inc) if inc else []
                if not any(sem is aset for sem, _ in inc):
                    inc.append((aset, 1))
        if inc:
            for sem, k in inc:
                if sem.dma and sem.max_waited > sem.last_sync:
                    v = sem.max_waited
                    sem.last_sync = v
                    self.emit(en, lambda eng, s=sem, v=v: eng.wait_ge(s.h, v))
        if inc:
            for sem, k in inc:
                sem.inc(k)

        def fn(eng, meth=meth, args=args, kw=kw, inc=inc):
            inst = getattr(eng, meth)(*args, **kw)
            if inc:
                for sem, k in inc:
                    inst = inst.then_inc(sem.h, k)
        self.emit(en, fn)

    def wait(self, en, sem, val=None):
        v = sem.n if val is None else val
        sem.max_waited = max(sem.max_waited, v)
        self.emit(en, lambda eng, s=sem, v=v: eng.wait_ge(s.h, v))


def build_program(max_phase=MAX_PHASE, dev=False, fake_cc=False):
    nc = bass.Bass("TRN2", target_bir_lowering=False)
    kb = KB()

    def din(name, shape, dt=F32):
        return nc.dram_tensor(name, shape, dt, kind="ExternalInput").ap()

    def dscr(name, shape, dt=F32):
        return nc.dram_tensor(name, shape, dt, kind="ExternalOutput" if dev else "Internal").ap()

    def dint(name, shape, dt=F32):
        return nc.dram_tensor(name, shape, dt, kind="Internal").ap()

    x_shard = din("x_shard", [NSEG, SL, D])
    x_halo = din("x_halo", [NSEG, 2, D])
    wqk_p = din("wqk_p", [D, 2 * D])
    bqk_cols = din("bqk_cols", [128, 16])
    lepe_w = din("lepe_w", [3, D, D])
    lepe_b = din("lepe_b", [1, D])
    costab = din("costab", [NSEG, 4, 128, SL], BF16)
    sintab = din("sintab", [NSEG, 4, 128, SL], BF16)
    gate_ext = din("gate_ext", [D, 72])
    sgu_w = din("sgu_w", [D, 4096])
    sdown_w = din("sdown_w", [2048, D])
    egu_w = din("egu_w", [EL, D, D])
    edown_w = din("edown_w", [EL, 512, D])
    c64 = din("c64", [128, E])
    c64p1 = din("c64p1", [128, E])
    c512 = din("c512", [128, CAP])
    ustrict = din("ustrict", [128, 128], BF16)
    ident_bf = din("ident_bf", [128, 128], BF16)
    ident_f = din("ident_f", [128, 128])
    ones_bf = din("ones_bf", [1, 128], BF16)
    onec_bf = din("onec_bf", [128, 1], BF16)
    ones_f = din("ones_f", [1, 128])
    tok_hi = din("tok_hi", [128, 32])
    tok_lo = din("tok_lo", [128, 32])
    maskcols = din("maskcols", [E, 8])
    comb_bf = din("comb_bf", [128, 128], BF16)
    zrow_col = din("zrow_col", [128, 1])
    eps_col = din("eps_col", [128, 1])

    out_d = nc.dram_tensor("out_d", [NSEG * SL, D], F32, kind="ExternalOutput").ap()

    qel_d = dscr("qel_d", [NSEG, 128, NCH * SL], BF16)
    qr_d = dscr("qr_d", [NSEG, 128, NCH * SL], BF16)
    lepe_d = dscr("lepe_d", [NSEG, SL, D], BF16)
    x2_d = dscr("x2_d", [NSEG, SL, D], F32)
    x2s_d = dscr("x2s_d", [NSEG, SL, D], F32)
    xlt2_d = dscr("xlt2_d", [NSEG, 128, NCH * SL], BF16)
    cl_d = dscr("cl_d", [NSEG, 128, NT * E], F32)
    wa_d = dscr("wa_d", [NSEG, 128, NT * E], F32)
    lg_d = dscr("lg_d", [NSEG * NT, 128, 80], F32) if dev else None
    dbg_d = dscr("dbg_d", [128, 2048], F32) if dev else None
    dbg2_d = dscr("dbg2_d", [128, 8192], BF16) if dev else None
    kv_in = dint("kv_in", [128, 4128], F32)
    kv_out = dint("kv_out", [128, 4128], F32)
    cnt_in = dint("cnt_in", [8, E], F32)
    cnt_out = dint("cnt_out", [E, E], F32)
    smw_in = dint("smw_in", [128, CAP], F32)
    smw_out = dint("smw_out", [16, CAP], F32)
    x2l_d = dint("x2l_d", [NSEG * SL, D], BF16)
    x2a_d = dint("x2a_d", [NCORES * NSEG * SL, D], BF16)
    x4l_d = dint("x4l_d", [X4ROWS, D], BF16)
    x4a_d = dint("x4a_d", [NCORES * X4ROWS, D], BF16)

    with ExitStack() as ctx:
        def sbuf(name, shape, dt=F32):
            return ctx.enter_context(nc.sbuf_tensor(name, shape, dt))

        def sem(name, dma=False):
            s = Sem(name, dma=dma)
            s.h = ctx.enter_context(nc.semaphore(name))
            return s

        c64_sb = sbuf("c64_sb", [128, E])
        c64p1_sb = sbuf("c64p1_sb", [128, E])
        c512_sb = sbuf("c512_sb", [128, CAP])
        ustrict_sb = sbuf("ustrict_sb", [128, 128], BF16)
        ident_sb = sbuf("ident_sb", [128, 128], BF16)
        identf_sb = sbuf("identf_sb", [128, 128])
        ones_bf_sb = sbuf("ones_bf_sb", [1, 128], BF16)
        onec_bf_sb = sbuf("onec_bf_sb", [128, 1], BF16)
        ones_f_sb = sbuf("ones_f_sb", [1, 128])
        tok_hi_sb = sbuf("tok_hi_sb", [128, 32])
        tok_lo_sb = sbuf("tok_lo_sb", [128, 32])
        maskcols_sb = sbuf("maskcols_sb", [E, 8])
        comb_sb = sbuf("comb_sb", [128, 128], BF16)
        zrow_sb = sbuf("zrow_sb", [128, 1])
        eps_sb = sbuf("eps_sb", [128, 1])
        bqk_sb = sbuf("bqk_sb", [128, 16])
        lepeb_sb = sbuf("lepeb_sb", [1, D], BF16)
        gate_sb = sbuf("gate_sb", [128, NCH * 72], BF16)

        kvbf_sb = sbuf("kvbf_sb", [128, B * NH * HD], BF16)
        kmean_sb = sbuf("kmean_sb", [128, B * NCH])
        sall_sb = sbuf("sall_sb", [128, B * NCH * NH], BF16)
        ksum_sb = sbuf("ksum_sb", [128, B * NCH])
        edesc_sb = sbuf("edesc_sb", [128, 256])
        pos_sb = sbuf("pos_sb", [128, 256])
        addrf_sb = sbuf("addrf_sb", [128, 256])
        addru_sb = sbuf("addru_sb", [128, 256], U32)
        sg_sb = sbuf("sg_sb", [128, 32])
        rowbase_sb = sbuf("rowbase_sb", [1, 32 * E])
        cnt1_sb = sbuf("cnt1_sb", [1, NSEG * E])
        tot_sb = sbuf("tot_sb", [1, 2 * E])
        smloc_sb = sbuf("smloc_sb", [16, CAP])
        smi_sb = sbuf("smi_sb", [16, CAP])
        smT_sb = sbuf("smT_sb", [128, 64])
        smTu_sb = sbuf("smTu_sb", [128, 64], U32)
        wt_sb = sbuf("wt_sb", [128, 32])
        stat_sb = sbuf("stat_sb", [128, 6])
        hstat_sb = sbuf("hstat_sb", [2, 6])
        bn6_sb = sbuf("bn6_sb", [128, 12])
        hbn_sb = sbuf("hbn_sb", [2, 12])
        hbn2_sb = sbuf("hbn2_sb", [128, 12])
        stat2_sb = sbuf("stat2_sb", [128, 6])
        red8_sb = sbuf("red8_sb", [128, 8])

        AAB = 68 * 1024
        AWB = 96 * 1024
        aa = sbuf("aa", [128, AAB], U8)
        aw = sbuf("aw", [128, AWB], U8)

        def av(off, nbytes, dt):
            assert off + nbytes <= AAB, (off, nbytes)
            return aa[:, off:off + nbytes].bitcast(dt)

        def wv(off, nbytes, dt):
            assert off + nbytes <= AWB, (off, nbytes)
            return aw[:, off:off + nbytes].bitcast(dt)

        xln_sb = av(0, 8192, BF16)
        xlt_sb = av(8192, 8224, BF16)
        qel_sb = av(16416, 8192, BF16)
        kel_sb = av(24608, 8192, BF16)
        qr2_sb = av(32800, 2048, BF16)
        kr2_sb = av(34848, 2048, BF16)
        krt1_sb = av(36896, 1024, BF16)
        cos_sb = av(38944, 4096, BF16)
        sin_sb = av(43040, 4096, BF16)
        eexp_sb = av(47136, 1024, BF16)
        erel_sb = av(48160, 1024, BF16)
        rexp_sb = av(49184, 1024, BF16)
        rrel_sb = av(50208, 1024, BF16)
        xt_sb = av(51232, 4096, F32)
        kvev_sb = av(59424, 4096, F32)
        lepet_sb2 = [av(63520 + 2048 * i, 2048, BF16) for i in range(2)]
        xhf_sb = sbuf("xhf_sb", [2, D])
        xlnh_sb = sbuf("xlnh_sb", [2, D], BF16)

        kvf_sb = av(0, 16512, F32)

        xln2_sb = av(0, 8192, BF16)
        xlt2_sb = av(8192, 8192, BF16)
        p3qel_sb = av(16416, 8192, BF16)
        p3qr_sb = av(24608, 8192, BF16)
        attnz_sb = [av(32800 + 4096 * i, 4096, F32) for i in range(2)]
        p3x_sb = [av(40992 + 4096 * i, 4096, F32) for i in range(2)]
        p3lep_sb = [av(49184 + 2048 * i, 2048, BF16) for i in range(2)]
        x2t3_sb = [av(53280 + 4096 * i, 4096, F32) for i in range(2)]
        zden_sb = av(61472, 2048, F32)
        zt_sb = av(63520, 128, F32)
        z_sb = av(63648, 128, F32)
        lgt_sb = [av(63776, 320, F32) for i in range(1)]
        lgt_sb = [av(63776 + 320 * i, 320, F32) for i in range(2)]
        m8_sb = [av(64416 + 32 * i, 32, F32) for i in range(2)]
        repl_sb = av(64480, 256, F32)
        selt_sb = [av(64736 + 128 * i, 128, BF16) for i in range(2)]
        eall_sb = av(64992, 256, F32)
        wm3_sb = [av(65248 + 256 * i, 256, F32) for i in range(2)]
        rsum_sb = av(65760, 16, F32)
        ed8_sb = av(65776, 32, F32)
        clt_sb = [av(65808 + 256 * i, 256, F32) for i in range(2)]
        tot2_sb = av(66320, 8, F32)

        NPOSH, NHL = 6, 3
        cnts_sb = av(0, 256, F32)
        clseg_sb = av(256, 1024, F32)
        waseg_sb = av(1280, 1024, F32)
        cg_sb = av(2304, 256, F32)
        mask_sb = av(2560, 1024, BF16)
        wmp_sb = av(3584, 2048, F32)
        junk_sb = av(5632, 256, F32)
        posh_sb = [av(5888 + 1024 * i, 1024, BF16) for i in range(NPOSH)]
        mhl_sb = [av(12032 + 2048 * i, 2048, BF16) for i in range(NHL)]
        whl_sb = [av(18176 + 2048 * i, 2048, BF16) for i in range(NHL)]
        smhlb_sb2 = av(0, 2048, BF16)
        smv_sb = av(2048, 2048, F32)
        wv_sb = av(4096, 2048, F32)
        eh_sb = [av(6144, 1024, F32), av(7168, 1024, F32)]
        a1_sb = av(8192, 1024, F32)
        alive_sb = av(9216, 1024, F32)

        h_sb = av(33792, 16384, BF16)
        sguv = wv(0, 65536, BF16)
        p5x_sb = av(50176, 4096, F32)
        p5s_sb = av(54272, 4096, F32)
        silu_sb = av(58368, 1024, BF16)
        p5xlt_sb = av(59392, 8192, BF16)

        egu_sb = [av(18432, 16384, BF16), av(34816, 16384, BF16)]
        edn_sb = [av(51200, 8192, BF16), av(59392, 8192, BF16)]
        rows_sb = av(0, 2048, BF16)
        rowsT_sb = av(2048, 8192, BF16)
        he_sb = av(10240, 4096, BF16)
        oute_sb = av(14336, 2048, BF16)
        zero_sb = av(16384, 2048, BF16)
        silu6_sb = av(17408, 1024, BF16)

        NFT = 8
        fin_sb = [av(4096 * i, 4096, F32) for i in range(NFT)]

        ps0 = ctx.enter_context(nc.psum_tensor("ps0", [128, SL], F32))
        ps1 = ctx.enter_context(nc.psum_tensor("ps1", [128, SL], F32))
        ps2 = ctx.enter_context(nc.psum_tensor("ps2", [128, SL], F32))
        ps_trF = ctx.enter_context(nc.psum_tensor("ps_trF", [128, 2 * SL], BF16))
        ps_tr = ps_trF[:, 0:SL]
        ps_tr2 = ps_trF[:, SL:2 * SL]
        ps_kv = ctx.enter_context(nc.psum_tensor("ps_kv", [128, 2 * SL], F32))
        ps3 = ctx.enter_context(nc.psum_tensor("ps3", [128, SL], F32))
        ps4 = ctx.enter_context(nc.psum_tensor("ps4", [128, SL], F32))


        qL = sem("qL", dma=True); qS = sem("qS", dma=True); qG = sem("qG", dma=True)

        qE = [sem("qE0", dma=True), sem("qE1", dma=True)]
        qX2 = [sem("qX2a", dma=True), sem("qX2b", dma=True)]
        qCL = [sem("qCLa", dma=True), sem("qCLb", dma=True)]
        qWM = [sem("qWMa", dma=True), sem("qWMb", dma=True)]
        qXS = [sem("qXS%d" % i, dma=True) for i in range(4)]
        qW5 = sem("qW5", dma=True)
        qCkv = sem("qCkv"); qCcnt = sem("qCcnt"); qCx2 = sem("qCx2")
        qCsm = sem("qCsm"); qCx4 = sem("qCx4")
        mm = sem("mm"); ve = sem("ve"); se = sem("se"); tE = sem("tE"); gv = sem("gv")
        kb.auto = {V: ve, S: se}

        for dst, src in [(c64_sb, c64), (c64p1_sb, c64p1), (c512_sb, c512),
                         (ustrict_sb, ustrict), (ident_sb, ident_bf), (identf_sb, ident_f),
                         (ones_bf_sb, ones_bf), (onec_bf_sb, onec_bf), (ones_f_sb, ones_f),
                         (tok_hi_sb, tok_hi), (tok_lo_sb, tok_lo),
                         (maskcols_sb, maskcols), (comb_sb, comb_bf),
                         (zrow_sb, zrow_col), (eps_sb, eps_col),
                         (bqk_sb, bqk_cols)]:
            kb.op(Y, "dma_start", out=dst[:], in_=src, inc=[(qL, 16)])
        P0_sync = qL.n
        kb.op(G, "dma_start", out=wv(0, 32768, BF16).rearrange("p (c f) -> p c f", f=2 * D),
              in_=wqk_p.rearrange("(c p) f -> p c f", p=128), inc=[(qG, 16)])
        kb.op(G, "dma_start", out=wv(32768, 49152, BF16).rearrange("p (k c f) -> p k c f", k=3, c=NCH),
              in_=lepe_w.rearrange("k (c p) f -> p k c f", p=128), inc=[(qG, 16)])
        kb.op(G, "dma_start", out=lepeb_sb[:], in_=lepe_b, inc=[(qG, 16)])
        kb.op(G, "dma_start", out=gate_sb[:].rearrange("p (c f) -> p c f", f=72),
              in_=gate_ext.rearrange("(c p) f -> p c f", p=128), inc=[(qG, 16)])
        P0_gp = qG.n
        wqk_sb = wv(0, 32768, BF16)
        lepew_sb = wv(32768, 49152, BF16)
        kb.op(V, "memset", ksum_sb[:], 0.0, inc=[(ve, 1)])
        kb.wait(T, qL, P0_sync)
        kb.wait(V, qL, P0_sync)
        kb.wait(S, qL, P0_sync)
        kb.wait(T, qG, P0_gp)

        def emit_ln(src_ap, dst_ap, bn, st, part=128):
            kb.op(V, "bn_stats", out=bn[0:part, 0:6], in_=src_ap[0:part, 0:SL], inc=[(ve, 1)])
            kb.op(V, "bn_stats", out=bn[0:part, 6:12], in_=src_ap[0:part, SL:D], inc=[(ve, 1)])
            kb.wait(V, ve, ve.n)
            kb.op(V, "bn_aggr", out=st[0:part, 0:2], in_=bn[0:part, :], inc=[(ve, 1)])
            kb.wait(S, ve, ve.n)
            kb.op(S, "activation", out=st[0:part, 2:3], in_=st[0:part, 1:2], func=AF.Sqrt,
                  bias=eps_sb[0:part, 0:1], scale=1.0, inc=[(se, 1)])
            kb.wait(V, se, se.n)
            kb.op(V, "reciprocal", out=st[0:part, 3:4], in_=st[0:part, 2:3], inc=[(ve, 1)])
            kb.wait(V, ve, ve.n)
            kb.op(V, "scalar_tensor_tensor", out=st[0:part, 5:6], in0=st[0:part, 0:1],
                  scalar=-1.0, in1=st[0:part, 3:4], op0=ALU.mult, op1=ALU.mult, inc=[(ve, 1)])
            kb.wait(S, ve, ve.n)
            kb.op(S, "activation", out=dst_ap, in_=src_ap[0:part, :], func=AF.Identity,
                  bias=st[0:part, 5:6], scale=st[0:part, 3:4], inc=[(se, 1)])
            return se.n

        if max_phase >= 1:
            p1_psg = [0, 0, 0, 0]
            halo_ve_g = halo_se_g = rope_g = xln_g = xlt_g = 0
            for s in range(NSEG):
                b = s // 2
                kb.wait(Y, ve, halo_ve_g)
                kb.wait(Y, se, halo_se_g)
                kb.op(Y, "dma_start", out=xhf_sb[:], in_=x_halo[s], inc=[(qL, 16)])
                kb.wait(V, qL, qL.n)
                kb.wait(S, mm, xln_g)
                emit_ln(xhf_sb, xlnh_sb[:], hbn_sb, hstat_sb, part=2)
                halo_ve_g = ve.n
                halo_se_g = se.n
                for t in range(NT):
                    kb.wait(Y, se, se.n)
                    kb.op(Y, "dma_start", out=xt_sb[:], in_=x_shard[s, t * 128:(t + 1) * 128, :],
                          inc=[(qL, 16)])
                    kb.wait(V, qL, qL.n)
                    if t == 0:
                        kb.wait(S, mm, xln_g)
                    emit_ln(xt_sb, xln_sb[:, t * D:(t + 1) * D], bn6_sb, stat_sb)
                xln_done = se.n

                kb.wait(T, se, xln_done)
                for c in range(NCH):
                    for t in range(NT):
                        kb.op(T, "transpose", ps_tr[:, t * 128:(t + 1) * 128],
                              xln_sb[:, t * D + c * 128: t * D + (c + 1) * 128], ident_sb[:],
                              inc=([(mm, 1)] if t == NT - 1 else None))
                    kb.wait(V, mm, mm.n)
                    if c == 0:
                        kb.wait(V, mm, xlt_g)
                        kb.wait(V, qS, qS.n)
                    kb.op(V, "tensor_copy", out=xlt_sb[:, c * 514 + 1: c * 514 + 513],
                          in_=ps_tr[:], inc=[(ve, 1)])
                    kb.wait(T, ve, ve.n)
                for c in range(NCH):
                    kb.op(T, "transpose", ps_tr[:, c * 2:(c + 1) * 2],
                          xlnh_sb[:, c * 128:(c + 1) * 128], ident_sb[0:2, 0:2],
                          inc=([(mm, 1)] if c == NCH - 1 else None))
                kb.wait(V, mm, mm.n)
                for c in range(NCH):
                    kb.op(V, "tensor_copy", out=xlt_sb[:, c * 514: c * 514 + 1],
                          in_=ps_tr[:, c * 2: c * 2 + 1], inc=[(ve, 1)])
                    kb.op(V, "tensor_copy", out=xlt_sb[:, c * 514 + 513: c * 514 + 514],
                          in_=ps_tr[:, c * 2 + 1: c * 2 + 2], inc=[(ve, 1)])
                kb.wait(T, ve, ve.n)

                kb.wait(Y, ve, rope_g)
                kb.op(Y, "dma_start", out=cos_sb.rearrange("p (c q) -> p c q", q=SL),
                      in_=costab[s].transpose([1, 0, 2]), inc=[(qL, 16)])
                kb.op(Y, "dma_start", out=sin_sb.rearrange("p (c q) -> p c q", q=SL),
                      in_=sintab[s].transpose([1, 0, 2]), inc=[(qL, 16)])
                rope_ld = qL.n

                qk_psg = [p1_psg[0], p1_psg[1]]
                elu_g = [0, 0]
                for f in range(16):
                    dst = qel_sb if f < 8 else kel_sb
                    fc = f % 8
                    fp = f % 2
                    psq = ps0 if fp == 0 else ps1
                    eex = eexp_sb if fp == 0 else rexp_sb
                    ere = erel_sb if fp == 0 else rrel_sb
                    kb.wait(T, se, qk_psg[fp])
                    for c in range(NCH):
                        kb.op(T, "matmul", psq[:],
                              wqk_sb[:, c * 2048 + f * 128: c * 2048 + (f + 1) * 128],
                              xlt_sb[:, c * 514 + 1: c * 514 + 513],
                              start=(c == 0), stop=(c == NCH - 1),
                              inc=([(mm, 1)] if c == NCH - 1 else None))
                    kb.wait(S, mm, mm.n)
                    kb.wait(S, ve, elu_g[fp])
                    if f == 0:
                        kb.wait(S, qS, qS.n)
                    kb.op(S, "activation", out=eex, in_=psq[:], func=AF.Exp,
                          bias=bqk_sb[:, f:f + 1], scale=1.0, inc=[(se, 1)])
                    kb.op(S, "activation", out=ere, in_=psq[:], func=AF.Relu,
                          bias=bqk_sb[:, f:f + 1], scale=1.0, inc=[(se, 1)])
                    qk_psg[fp] = se.n
                    kb.wait(V, se, se.n)
                    kb.op(V, "tensor_scalar", out=eex, in0=eex, scalar1=1.0,
                          scalar2=None, op0=ALU.min, inc=[(ve, 1)])
                    kb.op(V, "tensor_tensor", out=dst[:, fc * SL:(fc + 1) * SL], in0=eex,
                          in1=ere, op=ALU.add, inc=[(ve, 1)])
                    elu_g[fp] = ve.n
                qk_done = ve.n
                kb.wait(S, ve, qk_done)
                kb.op(S, "dma_start", out=qel_d[s], in_=qel_sb, inc=[(qS, 16)])

                for c in range(NCH):
                    kb.op(V, "tensor_reduce", out=red8_sb[:, c:c + 1],
                          in_=kel_sb[:, c * SL:(c + 1) * SL], axis=AX.X, op=ALU.add,
                          inc=([(ve, 1)] if c == NCH - 1 else None))
                kb.wait(V, ve, ve.n)
                kb.op(V, "tensor_tensor", out=ksum_sb[:, b * NCH:(b + 1) * NCH],
                      in0=ksum_sb[:, b * NCH:(b + 1) * NCH],
                      in1=red8_sb[:], op=ALU.add, inc=[(ve, 1)])

                kb.wait(V, qL, rope_ld)
                for (src, isq) in ((qel_sb, True), (kel_sb, False)):
                    dstr = qr2_sb if isq else kr2_sb
                    for jc in range(4):
                        co = cos_sb[:, jc * SL:(jc + 1) * SL]
                        si = sin_sb[:, jc * SL:(jc + 1) * SL]
                        ev = src[:, jc * SL:(jc + 1) * SL]
                        od = src[:, (4 + jc) * SL:(5 + jc) * SL]
                        kb.wait(V, qS, qS.n)
                        if not isq:
                            kb.wait(V, mm, mm.n)
                        kb.op(V, "tensor_tensor", out=eexp_sb[:], in0=co, in1=ev, op=ALU.mult, inc=[(ve, 1)])
                        kb.op(V, "tensor_tensor", out=erel_sb[:], in0=si, in1=od, op=ALU.mult, inc=[(ve, 1)])
                        kb.wait(V, ve, ve.n)
                        kb.op(V, "tensor_tensor", out=dstr[:, 0:SL],
                              in0=eexp_sb[:], in1=erel_sb[:], op=ALU.subtract, inc=[(ve, 1)])
                        kb.op(V, "tensor_tensor", out=eexp_sb[:], in0=co, in1=od, op=ALU.mult, inc=[(ve, 1)])
                        kb.op(V, "tensor_tensor", out=erel_sb[:], in0=si, in1=ev, op=ALU.mult, inc=[(ve, 1)])
                        kb.wait(V, ve, ve.n)
                        kb.op(V, "tensor_tensor", out=dstr[:, SL:2 * SL],
                              in0=eexp_sb[:], in1=erel_sb[:], op=ALU.add, inc=[(ve, 1)])
                        pair_done = ve.n
                        if isq:
                            kb.wait(S, ve, pair_done)
                            kb.op(S, "dma_start", out=qr_d[s, :, jc * SL:(jc + 1) * SL],
                                  in_=dstr[:, 0:SL], inc=[(qS, 16)])
                            kb.op(S, "dma_start", out=qr_d[s, :, (4 + jc) * SL:(5 + jc) * SL],
                                  in_=dstr[:, SL:2 * SL], inc=[(qS, 16)])
                        else:
                            kb.wait(T, ve, pair_done)
                            for half, h in ((0, jc), (1, jc + 4)):
                                for t in range(NT):
                                    kb.op(T, "transpose", ps_tr[:, t * 128:(t + 1) * 128],
                                          dstr[:, half * SL + t * 128: half * SL + (t + 1) * 128],
                                          ident_sb[:], inc=([(mm, 1)] if t == NT - 1 else None))
                                kb.wait(V, mm, mm.n)
                                for t in range(NT):
                                    kb.op(V, "tensor_copy", out=krt1_sb[:, t * 128:(t + 1) * 128],
                                          in_=ps_tr[:, t * 128:(t + 1) * 128],
                                          inc=([(ve, 1)] if t == NT - 1 else None))
                                kb.wait(T, ve, ve.n)
                                for t in range(NT):
                                    kb.op(T, "matmul", ps_kv[:, h * HD:(h + 1) * HD],
                                          krt1_sb[:, t * 128:(t + 1) * 128],
                                          xln_sb[:, t * D + h * 128:t * D + (h + 1) * 128],
                                          start=(t == 0), stop=(t == NT - 1),
                                          skip_group_check=True,
                                          inc=([(mm, 1)] if t == NT - 1 else None))
                                kb.wait(V, mm, mm.n)
                kv_mm = mm.n
                rope_g = ve.n
                xln_g = kv_mm
                kb.wait(S, mm, kv_mm)
                kb.wait(S, qG, qG.n)
                kb.op(S, "copy", kvev_sb[:, 0:SL], ps_kv[:, 0:SL], inc=[(se, 1)])
                kb.op(S, "copy", kvev_sb[:, SL:2 * SL], ps_kv[:, SL:2 * SL], inc=[(se, 1)])
                kv_ev = se.n
                kb.wait(T, se, kv_ev)
                kb.wait(G, se, kv_ev)
                kb.wait(G, qG, qG.n)
                if s % 2 == 0:
                    kb.op(G, "dma_start", out=kv_in[:, b * D:(b + 1) * D], in_=kvev_sb[:],
                          inc=[(qG, 16)])
                else:
                    kb.op(G, "dma_start", out=kv_in[:, b * D:(b + 1) * D], in_=kvev_sb[:],
                          accum_op=ALU.add, inc=[(qG, 16)])

                lep_psg = {"ps0": qk_psg[0], "ps1": qk_psg[1], "ps3": p1_psg[2], "ps4": p1_psg[3]}
                for t in range(NT):
                    pse = (ps0, ps1) if t % 2 == 0 else (ps3, ps4)
                    pk = ("ps0", "ps1") if t % 2 == 0 else ("ps3", "ps4")
                    for half in range(2):
                        pst = pse[half]
                        kb.wait(T, se, lep_psg[pk[half]])
                        for k in range(3):
                            for c in range(NCH):
                                kb.op(T, "matmul", pst[:],
                                      xlt_sb[:, c * 514 + t * 128 + k: c * 514 + t * 128 + k + 128],
                                      lepew_sb[:, (k * NCH + c) * D + half * SL:(k * NCH + c) * D + (half + 1) * SL],
                                      start=(k == 0 and c == 0), stop=False)
                        kb.op(T, "matmul", pst[:], ones_bf_sb[0:1, :],
                              lepeb_sb[0:1, half * SL:(half + 1) * SL],
                              start=False, stop=True, inc=[(mm, 1)])
                        kb.wait(S, mm, mm.n)
                        if half == 0:
                            kb.wait(S, qS, qS.n)
                        kb.op(S, "copy", lepet_sb2[t % 2][:, half * SL:(half + 1) * SL],
                              pst[:], inc=[(se, 1)])
                        lep_psg[pk[half]] = se.n
                    kb.op(S, "dma_start", out=lepe_d[s, t * 128:(t + 1) * 128, :],
                          in_=lepet_sb2[t % 2], inc=[(qS, 16)])
                p1_psg = [lep_psg["ps0"], lep_psg["ps1"], lep_psg["ps3"], lep_psg["ps4"]]
                xlt_g = mm.n

        mm_p1 = mm.n

        if max_phase >= 2:
            kb.wait(G, ve, ve.n)
            kb.wait(G, qG, qG.n)
            kb.op(G, "dma_start", out=kv_in[:, 4096:4128], in_=ksum_sb[:], inc=[(qG, 16)])
            kb.wait(G, qG, qG.n)
            kb.op(G, "collective_compute", "AllReduce", ALU.add,
                  replica_groups=[list(range(NCORES))],
                  ins=[kv_in.opt()], outs=[kv_out.opt()], inc=[(qCkv, 1)])
            kb.wait(Y, qCkv, qCkv.n)
            kb.wait(Y, mm, mm.n)
            kb.wait(Y, ve, ve.n)
            kb.wait(Y, se, se.n)
            kb.wait(Y, qS, qS.n)
            kb.op(Y, "dma_start", out=kvf_sb, in_=kv_out[:, :], inc=[(qL, 16)])
            kb.wait(V, qL, qL.n)
            kb.op(V, "tensor_scalar", out=kvbf_sb[:], in0=kvf_sb[:, 0:4096], scalar1=ISQN,
                  scalar2=None, op0=ALU.mult, inc=[(ve, 1)])
            kb.op(V, "tensor_scalar", out=kmean_sb[:], in0=kvf_sb[:, 4096:4128], scalar1=1.0 / N,
                  scalar2=None, op0=ALU.mult, inc=[(ve, 1)])
            kb.op(V, "memset", sall_sb[:], 0.0, inc=[(ve, 1)])
            kb.wait(V, ve, ve.n)
            for b in range(B):
                for c in range(NCH):
                    base = (b * NCH + c) * NH
                    h0 = 2 * (c % 4)
                    kb.op(V, "tensor_copy", out=sall_sb[0:64, base + h0: base + h0 + 1],
                          in_=kmean_sb[0:64, b * NCH + c: b * NCH + c + 1], inc=[(ve, 1)])
                    kb.op(V, "tensor_copy", out=sall_sb[64:128, base + h0 + 1: base + h0 + 2],
                          in_=kmean_sb[64:128, b * NCH + c: b * NCH + c + 1], inc=[(ve, 1)])
            p2_done = ve.n
            if dev:
                kb.wait(S, ve, p2_done)
                kb.op(S, "dma_start", out=dbg2_d[:, 0:4096], in_=kvbf_sb[:], inc=[(qS, 16)])
                kb.op(S, "dma_start", out=dbg_d[:, 0:32], in_=kmean_sb[:], inc=[(qS, 16)])
                kb.op(S, "dma_start", out=dbg2_d[:, 4096:4096 + 256], in_=sall_sb[:], inc=[(qS, 16)])

        if max_phase >= 3:
            kb.op(V, "memset", rowbase_sb[:], 0.0, inc=[(ve, 1)])
            kb.op(V, "memset", cnt1_sb[:], 0.0, inc=[(ve, 1)])
            for s in range(NSEG):
                b = s // 2
                g0 = s * NT
                kb.wait(Y, mm, mm.n)
                kb.wait(Y, ve, ve.n)
                kb.wait(Y, se, se.n)
                kb.wait(Y, qS, qS.n)
                kb.op(Y, "dma_start", out=p3qel_sb, in_=qel_d[s], inc=[(qL, 16)])
                kb.op(Y, "dma_start", out=p3qr_sb, in_=qr_d[s], inc=[(qL, 16)])
                ld_qq = qL.n
                kb.wait(T, qL, ld_qq)
                kb.wait(T, ve, ve.n)
                for c in range(NCH):
                    kb.op(T, "matmul", ps2[0:8, :],
                          sall_sb[:, (b * NCH + c) * NH:(b * NCH + c + 1) * NH],
                          p3qel_sb[:, c * SL:(c + 1) * SL],
                          start=(c == 0), stop=(c == NCH - 1),
                          inc=([(mm, 1)] if c == NCH - 1 else None))
                kb.wait(S, mm, mm.n)
                kb.op(S, "copy", zden_sb[0:8, :], ps2[0:8, :], inc=[(se, 1)])
                kb.wait(T, se, se.n)
                for t in range(NT):
                    kb.op(T, "transpose", ps2[:, 448 + t * 8: 456 + t * 8],
                          zden_sb[0:8, t * 128:(t + 1) * 128], identf_sb[0:8, 0:8],
                          inc=([(mm, 1)] if t == NT - 1 else None))
                kb.wait(V, mm, mm.n)
                kb.op(V, "tensor_scalar", out=zt_sb[:, 0:32], in0=ps2[:, 448:480], scalar1=1e-6,
                      scalar2=None, op0=ALU.add, inc=[(ve, 1)])
                kb.op(V, "reciprocal", out=z_sb[:, 0:32], in_=zt_sb[:, 0:32], inc=[(ve, 1)])
                z_done = ve.n
                kb.wait(T, ve, z_done)

                og = {0: 0, 1: 0}
                ldg = {0: 0, 1: 0}
                for t in range(NT):
                    g = g0 + t
                    pr = t % 2
                    odst = [(h, ps_kv[:, h * HD:(h + 1) * HD]) for h in range(NH)]
                    kb.wait(T, ve, og[1 - pr])
                    for h, dstp in odst:
                        kb.op(T, "matmul", dstp,
                              p3qr_sb[:, h * SL + t * 128: h * SL + (t + 1) * 128],
                              kvbf_sb[:, (b * NH + h) * HD:(b * NH + h + 1) * HD],
                              start=(h % 4 == 0), stop=(h % 4 == 3),
                              skip_group_check=True,
                              inc=([(mm, 1)] if h == NH - 1 else None))
                    o_mm = mm.n
                    kb.wait(Y, ve, ldg[pr])
                    kb.wait(Y, qX2[pr], qX2[pr].n)
                    kb.op(Y, "dma_start", out=p3x_sb[pr], in_=x_shard[s, t * 128:(t + 1) * 128, :],
                          inc=[(qL, 16)])
                    kb.op(Y, "dma_start", out=p3lep_sb[pr], in_=lepe_d[s, t * 128:(t + 1) * 128, :],
                          inc=[(qL, 16)])
                    ld_xt = qL.n
                    kb.wait(V, mm, o_mm)
                    for h, dstp in odst:
                        kb.op(V, "tensor_scalar", out=attnz_sb[pr][:, h * HD:(h + 1) * HD],
                              in0=dstp,
                              scalar1=z_sb[:, t * 8 + h: t * 8 + h + 1], scalar2=None,
                              op0=ALU.mult, inc=([(ve, 1)] if h == NH - 1 else None))
                    og[pr] = ve.n
                    kb.wait(V, qL, ld_xt)
                    kb.wait(V, qX2[pr], qX2[pr].n)
                    kb.op(V, "scalar_tensor_tensor", out=x2t3_sb[pr], in0=p3lep_sb[pr][:],
                          scalar=1.0, in1=p3x_sb[pr][:], op0=ALU.mult, op1=ALU.add, inc=[(ve, 1)])
                    kb.op(V, "tensor_tensor", out=x2t3_sb[pr], in0=x2t3_sb[pr], in1=attnz_sb[pr],
                          op=ALU.add, inc=[(ve, 1)])
                    x2_done = ve.n
                    kb.wait(S, ve, x2_done)
                    kb.op(S, "dma_start", out=x2_d[s, t * 128:(t + 1) * 128, :], in_=x2t3_sb[pr],
                          inc=[(qX2[pr], 16)])
                    bnp, stp = (bn6_sb, stat_sb) if pr == 0 else (hbn2_sb, stat2_sb)
                    kb.op(V, "bn_stats", out=bnp[:, 0:6], in_=x2t3_sb[pr][:, 0:SL], inc=[(ve, 1)])
                    kb.op(V, "bn_stats", out=bnp[:, 6:12], in_=x2t3_sb[pr][:, SL:D], inc=[(ve, 1)])
                    kb.op(V, "bn_aggr", out=stp[:, 0:2], in_=bnp[:], inc=[(ve, 1)])
                    kb.wait(S, ve, ve.n)
                    kb.op(S, "activation", out=stp[:, 2:3], in_=stp[:, 1:2], func=AF.Sqrt,
                          bias=eps_sb[:, 0:1], scale=1.0, inc=[(se, 1)])
                    kb.wait(V, se, se.n)
                    kb.op(V, "reciprocal", out=stp[:, 3:4], in_=stp[:, 2:3], inc=[(ve, 1)])
                    kb.op(V, "scalar_tensor_tensor", out=stp[:, 5:6], in0=stp[:, 0:1],
                          scalar=-1.0, in1=stp[:, 3:4], op0=ALU.mult, op1=ALU.mult, inc=[(ve, 1)])
                    kb.wait(S, ve, ve.n)
                    kb.op(S, "activation", out=xln2_sb[:, t * D:(t + 1) * D], in_=x2t3_sb[pr][:],
                          func=AF.Identity, bias=stp[:, 5:6], scale=stp[:, 3:4], inc=[(se, 1)])
                    ldg[pr] = ve.n
                    ln2_done = se.n
                    kb.op(S, "dma_start", out=x2l_d[s * SL + t * 128: s * SL + (t + 1) * 128, :],
                          in_=xln2_sb[:, t * D:(t + 1) * D], inc=[(qS, 16)])

                kb.wait(T, se, se.n)
                for c in range(NCH):
                    for t in range(NT):
                        kb.op(T, "transpose", ps_tr[:, t * 128:(t + 1) * 128],
                              xln2_sb[:, t * D + c * 128: t * D + (c + 1) * 128], ident_sb[:],
                              inc=([(mm, 1)] if t == NT - 1 else None))
                    kb.wait(V, mm, mm.n)
                    kb.op(V, "tensor_copy", out=xlt2_sb[:, c * SL:(c + 1) * SL],
                          in_=ps_tr[:], inc=[(ve, 1)])
                    kb.wait(T, ve, ve.n)
                xlt2_done = ve.n
                kb.wait(S, ve, xlt2_done)
                kb.op(S, "dma_start", out=xlt2_d[s], in_=xlt2_sb, inc=[(qS, 16)])

                for t in range(NT):
                    g = g0 + t
                    pr = t % 2
                    rcol = 0 if pr == 0 else 128
                    ccol = 256 if pr == 0 else 320
                    tcol = 384 if pr == 0 else 448
                    for c in range(NCH):
                        kb.op(T, "matmul", ps3[:, rcol:rcol + 72],
                              xlt2_sb[:, c * SL + t * 128: c * SL + (t + 1) * 128],
                              gate_sb[:, c * 72:(c + 1) * 72],
                              start=(c == 0), stop=(c == NCH - 1),
                              inc=([(mm, 1)] if c == NCH - 1 else None))
                    kb.wait(S, mm, mm.n)
                    kb.op(S, "copy", lgt_sb[pr][:, 0:72], ps3[:, rcol:rcol + 72], inc=[(se, 1)])
                    lg_ev = se.n
                    kb.op(S, "activation", out=sg_sb[:, g:g + 1], in_=lgt_sb[pr][:, 64:65],
                          func=AF.Sigmoid, bias=0.0, scale=1.0, inc=[(se, 1)])
                    if dev:
                        kb.op(S, "dma_start", out=lg_d[g][:, 0:72], in_=lgt_sb[pr][:, 0:72],
                              inc=[(qS, 16)])
                        kb.wait(S, qS, qS.n)
                    kb.wait(V, se, lg_ev)
                    kb.op(V, "max", out=m8_sb[pr], in_=lgt_sb[pr][:, 0:64], inc=[(ve, 1)])
                    kb.op(V, "match_replace", out=repl_sb, in_to_replace=m8_sb[pr],
                          in_values=lgt_sb[pr][:, 0:64], imm_value=-1e30, inc=[(ve, 1)])
                    kb.op(V, "tensor_tensor", out=selt_sb[pr], in0=lgt_sb[pr][:, 0:64], in1=repl_sb,
                          op=ALU.not_equal, inc=[(ve, 1)])
                    kb.op(V, "tensor_scalar", out=rsum_sb[:, 0:1], in0=m8_sb[pr][:, 0:1],
                          scalar1=-1.0, scalar2=None, op0=ALU.mult, inc=[(ve, 1)])
                    kb.wait(S, ve, ve.n)
                    kb.op(S, "activation", out=eall_sb, in_=lgt_sb[pr][:, 0:64], func=AF.Exp,
                          bias=rsum_sb[:, 0:1], scale=1.0, inc=[(se, 1)])
                    kb.wait(V, se, se.n)
                    kb.wait(V, qWM[pr], qWM[pr].n)
                    kb.op(V, "tensor_tensor", out=wm3_sb[pr], in0=eall_sb, in1=selt_sb[pr],
                          op=ALU.mult, inc=[(ve, 1)])
                    kb.op(V, "tensor_reduce", out=rsum_sb[:, 1:2], in_=wm3_sb[pr], axis=AX.X,
                          op=ALU.add, inc=[(ve, 1)])
                    kb.op(V, "reciprocal", out=rsum_sb[:, 2:3], in_=rsum_sb[:, 1:2], inc=[(ve, 1)])
                    kb.op(V, "tensor_scalar", out=wm3_sb[pr], in0=wm3_sb[pr],
                          scalar1=rsum_sb[:, 2:3], scalar2=None, op0=ALU.mult, inc=[(ve, 1)])
                    wsel_done = ve.n
                    kb.wait(S, ve, wsel_done)
                    kb.op(S, "dma_start", out=wa_d[s, :, t * E:(t + 1) * E], in_=wm3_sb[pr],
                          inc=[(qWM[pr], 16)])
                    kb.op(V, "tensor_tensor", out=eall_sb, in0=selt_sb[pr], in1=c64p1_sb[:],
                          op=ALU.mult, inc=[(ve, 1)])
                    kb.op(V, "max", out=ed8_sb, in_=eall_sb, inc=[(ve, 1)])
                    kb.op(V, "tensor_scalar", out=edesc_sb[:, g * 8:(g + 1) * 8], in0=ed8_sb,
                          scalar1=-1.0, scalar2=None, op0=ALU.add, inc=[(ve, 1)])
                    kb.wait(T, ve, ve.n)
                    kb.op(T, "matmul", ps4[:, ccol - 256:ccol - 192], ustrict_sb[:], selt_sb[pr],
                          start=True, stop=True, inc=[(mm, 1)])
                    kb.op(T, "matmul", ps4[0:1, tcol - 256:tcol - 192], onec_bf_sb[:], selt_sb[pr],
                          start=True, stop=True, inc=[(mm, 1)])
                    cs_mm = mm.n
                    kb.wait(S, mm, cs_mm)
                    kb.wait(S, qCL[pr], qCL[pr].n)
                    kb.op(S, "copy", clt_sb[pr], ps4[:, ccol - 256:ccol - 192], inc=[(se, 1)])
                    kb.op(S, "copy", tot_sb[0:1, pr * 64:pr * 64 + 64],
                          ps4[0:1, tcol - 256:tcol - 192], inc=[(se, 1)])
                    ev_c = se.n
                    kb.wait(V, se, ev_c)
                    kb.op(V, "tensor_copy", out=rowbase_sb[0:1, g * E:(g + 1) * E],
                          in_=cnt1_sb[0:1, s * E:(s + 1) * E], inc=[(ve, 1)])
                    kb.op(V, "tensor_tensor", out=cnt1_sb[0:1, s * E:(s + 1) * E],
                          in0=cnt1_sb[0:1, s * E:(s + 1) * E], in1=tot_sb[0:1, pr * 64:pr * 64 + 64],
                          op=ALU.add, inc=[(ve, 1)])
                    kb.wait(S, ve, ve.n)
                    kb.op(S, "dma_start", out=cl_d[s, :, t * E:(t + 1) * E], in_=clt_sb[pr],
                          inc=[(qCL[pr], 16)])

        if max_phase >= 4:
            kb.wait(G, mm, mm_p1)
            for xh in range(2):
                kb.op(G, "dma_start",
                      out=sguv.rearrange("p (c x f) -> p c x f", x=2, f=2048)[:, :, xh, :],
                      in_=sgu_w[:, xh * 2048:(xh + 1) * 2048].rearrange("(c p) f -> p c f", p=128),
                      inc=[(qW5, 16)])
            kb.op(G, "dma_start", out=wv(65536, 16384, BF16).rearrange("p (c f) -> p c f", f=SL),
                  in_=sdown_w[:, 0:SL].rearrange("(c p) f -> p c f", p=128), inc=[(qW5, 16)])
            kb.op(G, "dma_start", out=wv(81920, 16384, BF16).rearrange("p (c f) -> p c f", f=SL),
                  in_=sdown_w[:, SL:D].rearrange("(c p) f -> p c f", p=128), inc=[(qW5, 16)])
            sd0_sb = wv(65536, 16384, BF16)
            sd1_sb = wv(81920, 16384, BF16)
            p5w_ld = qW5.n

            kb.wait(S, ve, ve.n)
            kb.op(S, "dma_start", out=cnt_in.rearrange("s e -> (s e)").unsqueeze(0),
                  in_=cnt1_sb[0:1, :], inc=[(qS, 16)])
            kb.wait(G, qS, qS.n)
            kb.op(G, "collective_compute", "AllGather", ALU.bypass,
                  replica_groups=[list(range(NCORES))],
                  ins=[cnt_in.opt()], outs=[cnt_out.opt()], inc=[(qCcnt, 1)])
            kb.op(G, "collective_compute", "AllGather", ALU.bypass,
                  replica_groups=[list(range(NCORES))],
                  ins=[x2l_d.opt()], outs=[x2a_d.opt()], inc=[(qCx2, 1)])
            kb.wait(Y, qCcnt, qCcnt.n)
            kb.wait(Y, mm, mm.n)
            kb.wait(Y, ve, ve.n)
            kb.wait(Y, se, se.n)
            for q_ in (qCL[0], qCL[1], qWM[0], qWM[1], qX2[0], qX2[1]):
                kb.wait(Y, q_, q_.n)
            kb.op(Y, "dma_start", out=cnts_sb[0:64, 0:64], in_=cnt_out[:, :], inc=[(qL, 16)])
            ld_cnt = qL.n
            kb.wait(T, qL, ld_cnt)
            for s_ in range(NSEG):
                kb.op(T, "matmul", ps2[0:1, s_ * 64:(s_ + 1) * 64],
                      maskcols_sb[:, s_:s_ + 1], cnts_sb[0:64, 0:64],
                      start=True, stop=True, inc=[(mm, 1)])
            base_mm = mm.n
            kb.wait(V, mm, base_mm)
            for s_ in range(NSEG):
                kb.op(V, "tensor_tensor",
                      out=rowbase_sb[0:1, s_ * 4 * E:(s_ + 1) * 4 * E].rearrange("o (t e) -> o t e", e=E),
                      in0=rowbase_sb[0:1, s_ * 4 * E:(s_ + 1) * 4 * E].rearrange("o (t e) -> o t e", e=E),
                      in1=ps2[0:1, s_ * 64:(s_ + 1) * 64].unsqueeze(1).broadcast_to([1, 4, 64]),
                      op=ALU.add, inc=[(ve, 1)])
            rb_done = ve.n
            kb.wait(T, ve, rb_done)

            smk = {"n": 0}
            posh_guard = [0] * NPOSH
            hl_guard = [0] * NHL

            def p4_tile(g):
                s_, t = g // 4, g % 4
                par = g % NHL
                if t == 0:
                    kb.wait(Y, ve, ve.n)
                    kb.op(Y, "dma_start", out=clseg_sb, in_=cl_d[s_], inc=[(qL, 16)])
                    kb.op(Y, "dma_start", out=waseg_sb, in_=wa_d[s_], inc=[(qL, 16)])
                    kb.wait(V, qL, qL.n)
                kb.wait(T, ve, ve.n)
                kb.op(T, "matmul", ps2[:, 0:64], ones_f_sb[:],
                      rowbase_sb[0:1, g * E:(g + 1) * E], start=True, stop=True, inc=[(mm, 1)])
                kb.wait(V, mm, mm.n)
                kb.op(V, "tensor_tensor", out=cg_sb, in0=clseg_sb[:, t * E:(t + 1) * E],
                      in1=ps2[:, 0:64], op=ALU.add, inc=[(ve, 1)])
                kb.wait(T, ve, ve.n)
                kb.op(V, "tensor_tensor",
                      out=mask_sb.rearrange("p (k e) -> p k e", e=E),
                      in0=c64_sb[:].unsqueeze(1).broadcast_to([128, 8, E]),
                      in1=edesc_sb[:, g * 8:(g + 1) * 8].unsqueeze(2).broadcast_to([128, 8, E]),
                      op=ALU.is_equal, inc=[(ve, 1)])
                kb.op(V, "tensor_tensor",
                      out=wmp_sb.rearrange("p (k e) -> p k e", e=E),
                      in0=mask_sb.rearrange("p (k e) -> p k e", e=E),
                      in1=waseg_sb[:, t * E:(t + 1) * E].unsqueeze(1).broadcast_to([128, 8, E]),
                      op=ALU.mult, inc=[(ve, 1)])
                kb.wait(V, mm, hl_guard[par])
                kb.op(V, "tensor_scalar",
                      out=mhl_sb[par].rearrange("p (k x e) -> p k x e", x=2, e=E)[:, :, 0, :],
                      in0=mask_sb.rearrange("p (k e) -> p k e", e=E),
                      scalar1=tok_hi_sb[:, g:g + 1], scalar2=None, op0=ALU.mult, inc=[(ve, 1)])
                kb.op(V, "tensor_scalar",
                      out=mhl_sb[par].rearrange("p (k x e) -> p k x e", x=2, e=E)[:, :, 1, :],
                      in0=mask_sb.rearrange("p (k e) -> p k e", e=E),
                      scalar1=tok_lo_sb[:, g:g + 1], scalar2=None, op0=ALU.mult, inc=[(ve, 1)])
                kb.op(V, "tensor_copy",
                      out=whl_sb[par].rearrange("p (k x e) -> p k x e", x=2, e=E)[:, :, 0, :],
                      in_=wmp_sb.rearrange("p (k e) -> p k e", e=E), inc=[(ve, 1)])
                kb.op(V, "tensor_tensor",
                      out=whl_sb[par].rearrange("p (k x e) -> p k x e", x=2, e=E)[:, :, 1, :],
                      in0=wmp_sb.rearrange("p (k e) -> p k e", e=E),
                      in1=whl_sb[par].rearrange("p (k x e) -> p k x e", x=2, e=E)[:, :, 0, :],
                      op=ALU.subtract, inc=[(ve, 1)])
                for k in range(8):
                    kb.op(V, "scalar_tensor_tensor", out=junk_sb,
                          in0=mask_sb[:, k * E:(k + 1) * E], scalar=1.0, in1=cg_sb,
                          op0=ALU.mult, op1=ALU.mult,
                          accum_out=pos_sb[:, g * 8 + k: g * 8 + k + 1], inc=[(ve, 1)])
                for k in range(8):
                    pp = (g * 8 + k) % NPOSH
                    kb.wait(V, mm, posh_guard[pp])
                    kb.op(V, "tensor_scalar", out=posh_sb[pp],
                          in0=c512_sb[:], scalar1=pos_sb[:, g * 8 + k: g * 8 + k + 1],
                          scalar2=None, op0=ALU.is_equal, inc=[(ve, 1)])
                    kb.wait(T, ve, ve.n)
                    first = smk["n"] == 0
                    last = (g == 31 and k == 7)
                    kb.op(T, "matmul", ps3[:], mhl_sb[par][:, k * 128:(k + 1) * 128],
                          posh_sb[pp], start=first, stop=last, skip_group_check=True)
                    kb.op(T, "matmul", ps4[:], whl_sb[par][:, k * 128:(k + 1) * 128],
                          posh_sb[pp], start=first, stop=last, skip_group_check=True,
                          inc=[(mm, 1)])
                    posh_guard[pp] = mm.n
                    smk["n"] += 1
                hl_guard[par] = mm.n

            def p5_seg(s_):
                kb.wait(Y, mm, mm.n)
                kb.wait(Y, ve, ve.n)
                kb.wait(Y, se, se.n)
                kb.wait(Y, qS, qS.n)
                kb.op(Y, "dma_start", out=p5xlt_sb, in_=xlt2_d[s_], inc=[(qL, 16)])
                ld_xlt = qL.n
                if s_ == 0:
                    kb.wait(Y, qE[0], qE[0].n)
                    kb.wait(Y, qE[1], qE[1].n)
                    kb.wait(T, qW5, p5w_ld)
                kb.wait(T, qL, ld_xlt)
                for fc in range(16):
                    for c in range(NCH):
                        kb.op(T, "matmul", ps0[:],
                              sguv[:, c * 4096 + fc * 128: c * 4096 + (fc + 1) * 128],
                              p5xlt_sb[:, c * SL:(c + 1) * SL],
                              start=(c == 0), stop=(c == NCH - 1))
                    for c in range(NCH):
                        kb.op(T, "matmul", ps1[:],
                              sguv[:, c * 4096 + (fc + 16) * 128: c * 4096 + (fc + 17) * 128],
                              p5xlt_sb[:, c * SL:(c + 1) * SL],
                              start=(c == 0), stop=(c == NCH - 1),
                              inc=([(mm, 1)] if c == NCH - 1 else None))
                    gu_mm = mm.n
                    kb.wait(S, mm, gu_mm)
                    kb.op(S, "activation", out=silu_sb, in_=ps0[:], func=AF.Silu,
                          bias=0.0, scale=1.0, inc=[(se, 1)])
                    kb.wait(V, se, se.n)
                    kb.op(V, "tensor_tensor", out=h_sb[:, fc * SL:(fc + 1) * SL],
                          in0=silu_sb, in1=ps1[:], op=ALU.mult, inc=[(ve, 1)])
                    kb.wait(T, ve, ve.n)
                for t in range(NT):
                    g = s_ * 4 + t
                    for half, sdw, pso in ((0, sd0_sb, ps0), (1, sd1_sb, ps1)):
                        for c in range(16):
                            kb.op(T, "matmul", pso[:],
                                  h_sb[:, c * SL + t * 128: c * SL + (t + 1) * 128],
                                  sdw[:, c * SL:(c + 1) * SL],
                                  start=(c == 0), stop=(c == 15),
                                  inc=([(mm, 1)] if c == 15 else None))
                    dn_mm = mm.n
                    kb.wait(Y, ve, ve.n)
                    kb.op(Y, "dma_start", out=p5x_sb, in_=x2_d[s_, t * 128:(t + 1) * 128, :],
                          inc=[(qL, 16)])
                    kb.wait(V, qL, qL.n)
                    kb.wait(V, mm, dn_mm)
                    kb.op(V, "scalar_tensor_tensor", out=p5s_sb[:, 0:SL], in0=ps0[:],
                          scalar=sg_sb[:, g:g + 1], in1=p5x_sb[:, 0:SL],
                          op0=ALU.mult, op1=ALU.add, inc=[(ve, 1)])
                    kb.op(V, "scalar_tensor_tensor", out=p5s_sb[:, SL:2 * SL], in0=ps1[:],
                          scalar=sg_sb[:, g:g + 1], in1=p5x_sb[:, SL:D],
                          op0=ALU.mult, op1=ALU.add, inc=[(ve, 1)])
                    x2s_done = ve.n
                    kb.wait(S, ve, x2s_done)
                    kb.op(S, "dma_start", out=x2s_d[s_, t * 128:(t + 1) * 128, :],
                          in_=p5s_sb, inc=[(qXS[s_ // 2], 16)])
                    kb.wait(T, ve, x2s_done)
                    kb.wait(V, qXS[s_ // 2], qXS[s_ // 2].n)

            for s_ in range(NSEG):
                for t in range(NT):
                    p4_tile(s_ * 4 + t)
            sm_mm = mm.n

            kb.wait(V, mm, sm_mm)
            kb.op(V, "tensor_scalar", out=eh_sb[0], in0=edesc_sb[:], scalar1=8.0, scalar2=None,
                  op0=ALU.is_ge, inc=[(ve, 1)])
            for i, thr in enumerate((16.0, 24.0, 32.0, 40.0, 48.0, 56.0)):
                kb.op(V, "scalar_tensor_tensor", out=eh_sb[(i + 1) % 2], in0=edesc_sb[:],
                      scalar=thr, in1=eh_sb[i % 2], op0=ALU.is_ge, op1=ALU.add, inc=[(ve, 1)])
            ehf = eh_sb[0]
            kb.op(V, "scalar_tensor_tensor", out=a1_sb, in0=edesc_sb[:], scalar=512.0,
                  in1=ehf, op0=ALU.mult, op1=ALU.add, inc=[(ve, 1)])
            kb.op(V, "tensor_tensor", out=a1_sb, in0=a1_sb, in1=pos_sb[:], op=ALU.add, inc=[(ve, 1)])
            kb.op(V, "tensor_scalar", out=alive_sb, in0=pos_sb[:], scalar1=512.0, scalar2=None,
                  op0=ALU.is_lt, inc=[(ve, 1)])
            kb.op(V, "tensor_scalar", out=a1_sb, in0=a1_sb, scalar1=zrow_sb[:, 0:1], scalar2=None,
                  op0=ALU.subtract, inc=[(ve, 1)])
            kb.op(V, "tensor_tensor", out=a1_sb, in0=a1_sb, in1=alive_sb, op=ALU.mult, inc=[(ve, 1)])
            kb.op(V, "tensor_scalar", out=addrf_sb[:], in0=a1_sb, scalar1=zrow_sb[:, 0:1],
                  scalar2=None, op0=ALU.add, inc=[(ve, 1)])
            kb.op(V, "tensor_copy", out=addru_sb[:], in_=addrf_sb[:], inc=[(ve, 1)])
            if dev:
                kb.wait(S, ve, ve.n)
                kb.op(S, "dma_start", out=dbg_d[:, 32:288], in_=edesc_sb[:], inc=[(qS, 16)])
                kb.op(S, "dma_start", out=dbg_d[:, 288:544], in_=pos_sb[:], inc=[(qS, 16)])
                kb.op(S, "dma_start", out=dbg_d[:, 544:800], in_=addrf_sb[:], inc=[(qS, 16)])

            kb.wait(S, mm, sm_mm)
            kb.op(S, "copy", smhlb_sb2[:, 0:SL], ps3[:], inc=[(se, 1)])
            kb.op(S, "copy", smhlb_sb2[:, SL:2 * SL], ps4[:], inc=[(se, 1)])
            hl_ev = se.n
            kb.wait(T, se, hl_ev)
            kb.op(T, "matmul", ps2[0:64, 0:SL], comb_sb[:, 0:64], smhlb_sb2[:, 0:SL],
                  start=True, stop=True, inc=[(mm, 1)])
            kb.wait(S, mm, mm.n)
            kb.op(S, "copy", smv_sb[0:64, :], ps2[0:64, 0:SL], inc=[(se, 1)])
            kb.wait(T, se, se.n)
            kb.op(T, "matmul", ps2[0:64, 0:SL], comb_sb[:, 64:128], smhlb_sb2[:, SL:2 * SL],
                  start=True, stop=True, inc=[(mm, 1)])
            kb.wait(S, mm, mm.n)
            kb.op(S, "copy", wv_sb[0:64, :], ps2[0:64, 0:SL], inc=[(se, 1)])
            smw_done = se.n
            kb.wait(S, se, smw_done)
            for r in range(NCORES):
                kb.op(S, "dma_start", out=smw_in[16 * r:16 * r + 8, :],
                      in_=smv_sb[8 * r:8 * r + 8, :], inc=[(qS, 16)])
                kb.op(S, "dma_start", out=smw_in[16 * r + 8:16 * r + 16, :],
                      in_=wv_sb[8 * r:8 * r + 8, :], inc=[(qS, 16)])
            kb.wait(G, qS, qS.n)
            kb.op(G, "collective_compute", "ReduceScatter", ALU.add,
                  replica_groups=[list(range(NCORES))],
                  ins=[smw_in.opt()], outs=[smw_out.opt()], inc=[(qCsm, 1)])
            kb.wait(Y, qCsm, qCsm.n)
            kb.op(Y, "dma_start", out=smloc_sb[0:16, :], in_=smw_out[:, :], inc=[(qL, 16)])
            kb.wait(T, qL, qL.n)
            for t in range(NT):
                kb.op(T, "transpose", ps2[:, t * 16:(t + 1) * 16],
                      smloc_sb[0:16, t * 128:(t + 1) * 128], identf_sb[0:16, 0:16],
                      inc=([(mm, 1)] if t == NT - 1 else None))
            kb.wait(V, mm, mm.n)
            for t in range(NT):
                kb.op(V, "tensor_scalar", out=smT_sb[:, t * 8:(t + 1) * 8],
                      in0=ps2[:, t * 16: t * 16 + 8], scalar1=-1.0, scalar2=0.0,
                      op0=ALU.add, op1=ALU.max, inc=[(ve, 1)])
                kb.op(V, "tensor_copy", out=wt_sb[:, t * 8:(t + 1) * 8],
                      in_=ps2[:, t * 16 + 8: t * 16 + 16], inc=[(ve, 1)])
            kb.op(V, "tensor_copy", out=smTu_sb[:, 0:32], in_=smT_sb[:, 0:32], inc=[(ve, 1)])
            p4_done = ve.n
            if dev:
                kb.wait(S, ve, p4_done)
                kb.op(S, "dma_start", out=dbg_d[:, 800:832], in_=smT_sb[:, 0:32], inc=[(qS, 16)])
                kb.op(S, "dma_start", out=dbg_d[:, 832:864], in_=wt_sb[:, 0:32], inc=[(qS, 16)])

        if max_phase >= 6:
            kb.wait(V, qS, qS.n)
            kb.op(V, "memset", zero_sb[0:1, :], 0.0, inc=[(ve, 1)])
            kb.wait(S, ve, ve.n)
            kb.op(S, "dma_start", out=x4l_d[X4ROWS - 1: X4ROWS, :], in_=zero_sb[0:1, :],
                  inc=[(qS, 16)])
            kb.wait(S, qS, qS.n)
            kb.wait(G, ve, ve.n)
            kb.wait(G, qCx2, qCx2.n)
            kb.wait(G, qS, qS.n)
            kb.wait(G, qG, qG.n)

            def load_exp(e, buf):
                kb.op(G, "dma_start", out=egu_sb[buf].rearrange("p (c f) -> p c f", f=D),
                      in_=egu_w[e].rearrange("(c p) f -> p c f", p=128), inc=[(qE[buf], 16)])
                kb.op(G, "dma_start", out=edn_sb[buf].rearrange("p (c f) -> p c f", f=D),
                      in_=edown_w[e].rearrange("(c p) f -> p c f", p=128), inc=[(qE[buf], 16)])

            kb.wait(G, mm, mm.n)
            load_exp(0, 0)
            for e in range(EL):
                buf = e % 2
                w_ld = qE[buf].n
                if e + 1 < EL:
                    kb.wait(G, tE, max(0, e))
                    load_exp(e + 1, (e + 1) % 2)
                for t in range(NT):
                    kb.wait(G, mm, mm.n)
                    kb.op(G, "indirect_dma_start", out=rows_sb, out_offset=None,
                          in_=x2a_d, in_offset=IOA(smTu_sb[:, t * 8 + e: t * 8 + e + 1], 0),
                          inc=[(qG, 16)])
                    kb.wait(T, qG, qG.n)
                    for c in range(NCH):
                        kb.op(T, "transpose", ps_tr[:, (c % 4) * 128:(c % 4 + 1) * 128],
                              rows_sb[:, c * 128:(c + 1) * 128], ident_sb[:],
                              inc=[(mm, 1)] if (c % 4 == 3) else None)
                        if c % 4 == 3:
                            kb.wait(V, mm, mm.n)
                            for cc in range(c - 3, c + 1):
                                kb.op(V, "tensor_copy",
                                      out=rowsT_sb[:, cc * SL + t * 128: cc * SL + (t + 1) * 128],
                                      in_=ps_tr[:, (cc % 4) * 128:(cc % 4 + 1) * 128],
                                      inc=([(ve, 1)] if cc % 4 == 3 else None))
                            kb.wait(T, ve, ve.n)
                rows_done = ve.n
                kb.wait(T, qE[buf], w_ld)
                if e == 0:
                    kb.wait(T, se, se.n)
                for fc in range(4):
                    for c in range(NCH):
                        kb.op(T, "matmul", ps3[:],
                              egu_sb[buf][:, c * D + fc * 128: c * D + (fc + 1) * 128],
                              rowsT_sb[:, c * SL:(c + 1) * SL],
                              start=(c == 0), stop=(c == NCH - 1))
                    for c in range(NCH):
                        kb.op(T, "matmul", ps4[:],
                              egu_sb[buf][:, c * D + (fc + 4) * 128: c * D + (fc + 5) * 128],
                              rowsT_sb[:, c * SL:(c + 1) * SL],
                              start=(c == 0), stop=(c == NCH - 1),
                              inc=([(mm, 1)] if c == NCH - 1 else None))
                    kb.wait(S, mm, mm.n)
                    kb.op(S, "activation", out=silu6_sb, in_=ps3[:], func=AF.Silu,
                          bias=0.0, scale=1.0, inc=[(se, 1)])
                    kb.wait(V, se, se.n)
                    kb.op(V, "tensor_tensor", out=he_sb[:, fc * SL:(fc + 1) * SL],
                          in0=silu6_sb, in1=ps4[:], op=ALU.mult, inc=[(ve, 1)])
                    kb.wait(T, ve, ve.n)
                for t in range(NT):
                    for half, pso in ((0, ps3), (1, ps4)):
                        for c in range(4):
                            kb.op(T, "matmul", pso[:],
                                  he_sb[:, c * SL + t * 128: c * SL + (t + 1) * 128],
                                  edn_sb[buf][:, c * D + half * SL: c * D + half * SL + SL],
                                  start=(c == 0), stop=(c == 3),
                                  inc=([(mm, 1)] if c == 3 else None))
                    dn_mm = mm.n
                    kb.wait(V, mm, dn_mm)
                    kb.wait(V, qS, qS.n)
                    kb.op(V, "tensor_scalar", out=oute_sb[:, 0:SL], in0=ps3[:],
                          scalar1=wt_sb[:, t * 8 + e: t * 8 + e + 1], scalar2=None,
                          op0=ALU.mult, inc=[(ve, 1)])
                    kb.op(V, "tensor_scalar", out=oute_sb[:, SL:2 * SL], in0=ps4[:],
                          scalar1=wt_sb[:, t * 8 + e: t * 8 + e + 1], scalar2=None,
                          op0=ALU.mult, inc=[(ve, 1)])
                    kb.wait(S, ve, ve.n)
                    kb.op(S, "dma_start", out=x4l_d[e * CAP + t * 128: e * CAP + (t + 1) * 128, :],
                          in_=oute_sb, inc=[(qS, 16)])
                    kb.wait(T, ve, ve.n)
                tE.inc(1)
                kb.emit(T, lambda eng, s_=tE: eng.sem_inc(s_.h, 1))
            kb.wait(G, qS, qS.n)
            kb.wait(G, qG, qG.n)
            kb.op(G, "collective_compute", "AllGather", ALU.bypass,
                  replica_groups=[list(range(NCORES))],
                  ins=[x4l_d.opt()], outs=[x4a_d.opt()], inc=[(qCx4, 1)])
            p6_mm = mm.n
            p6_ve = ve.n
            p6_se = se.n

            for s_ in range(NSEG):
                p5_seg(s_)

        if max_phase >= 7:
            kb.wait(G, qCx4, qCx4.n)
            kb.wait(G, ve, p6_ve)
            kb.wait(G, se, p6_se)
            for grp in range(4):
                tiles = list(range(grp * 8, (grp + 1) * 8))
                kb.wait(Y, mm, p6_mm)
                kb.wait(Y, ve, p6_ve)
                kb.wait(Y, qE[0], qE[0].n)
                kb.wait(Y, qE[1], qE[1].n)
                kb.wait(Y, qG, qG.n)
                kb.wait(Y, qXS[grp], qXS[grp].n)
                kb.wait(Y, qS, qS.n)
                for i, gt in enumerate(tiles):
                    s, t = gt // 4, gt % 4
                    kb.op(Y, "dma_start", out=fin_sb[i], in_=x2s_d[s, t * 128:(t + 1) * 128, :],
                          inc=[(qL, 16)])
                kb.wait(G, qL, qL.n)
                for k in range(8):
                    if k > 0:
                        kb.wait(G, qG, qG.n)
                    for i, gt in enumerate(tiles):
                        kb.op(G, "indirect_dma_start", out=fin_sb[i], out_offset=None,
                              in_=x4a_d, in_offset=IOA(addru_sb[:, gt * 8 + k: gt * 8 + k + 1], 0),
                              compute_op=ALU.add, inc=[(qG, 16)])
                kb.wait(Y, qG, qG.n)
                for i, gt in enumerate(tiles):
                    kb.op(Y, "dma_start", out=out_d[gt * 128:(gt + 1) * 128, :], in_=fin_sb[i],
                          inc=[(qS, 16)])

        kb.wait(Y, qS, qS.n)
        kb.wait(Y, qG, qG.n)
        for _s in (qCkv, qCcnt, qCx2, qCsm, qCx4, qE[0], qE[1], qW5,
                   qX2[0], qX2[1], qCL[0], qCL[1], qWM[0], qWM[1],
                   qXS[0], qXS[1], qXS[2], qXS[3]):
            kb.wait(Y, _s, _s.n)
        kb.wait(Y, mm, mm.n)
        kb.wait(Y, ve, ve.n)
        kb.wait(Y, se, se.n)
        kb.wait(Y, qL, qL.n)

        with nc.Block() as block:
            @block.tensor
            def _(eng):
                for en, fn in kb.ops:
                    if en == T:
                        fn(eng)

            @block.vector
            def _(eng):
                for en, fn in kb.ops:
                    if en == V:
                        fn(eng)

            @block.scalar
            def _(eng):
                for en, fn in kb.ops:
                    if en == S:
                        fn(eng)

            @block.gpsimd
            def _(eng):
                for en, fn in kb.ops:
                    if en == G:
                        fn(eng)

            @block.sync
            def _(eng):
                for en, fn in kb.ops:
                    if en == Y:
                        fn(eng)

    return nc


def _perm_evenodd():
    q = np.arange(1024)
    p = np.concatenate([q[0::2], q[1::2]])
    return np.concatenate([p, 1024 + p])


def prepare_inputs(inputs):
    x = np.asarray(inputs["L0_attn_x"])
    wqk = np.asarray(inputs["L0_attn_qk_w"])
    bqk = np.asarray(inputs["L0_attn_qk_b"])
    lepe_w = np.asarray(inputs["L0_attn_lepe_w"])
    lepe_b = np.asarray(inputs["L0_attn_lepe_b"])
    gate_w = np.asarray(inputs["L0_ffn_gate_w"])
    sg_w = np.asarray(inputs["L0_ffn_shared_gate_w"])
    sgu_w = np.asarray(inputs["L0_ffn_shared_gate_up_w"])
    sdown_w = np.asarray(inputs["L0_ffn_shared_down_w"])
    egu_w = np.asarray(inputs["L0_ffn_expert_gate_up_w"])
    edown_w = np.asarray(inputs["L0_ffn_expert_down_w"])

    perm = _perm_evenodd()
    wqk_p = np.ascontiguousarray(wqk[:, perm]).astype(np.float32)
    bqk_p = bqk[perm]
    bqk_cols = np.ascontiguousarray(bqk_p.reshape(16, 128).T).astype(np.float32)

    gate_ext = np.zeros((D, 72), np.float32)
    gate_ext[:, :64] = gate_w
    gate_ext[:, 64:65] = sg_w

    theta = 1.0 / (10000.0 ** (np.arange(512, dtype=np.float32) / 512.0))
    c64c = np.tile(np.arange(E, dtype=np.float32), (128, 1))
    c512c = np.tile(np.arange(CAP, dtype=np.float32), (128, 1))
    ustrictc = np.triu(np.ones((128, 128), np.float32), 1).astype(BF)
    identc = np.eye(128, dtype=BF)
    identfc = np.eye(128, dtype=np.float32)

    in_maps = []
    for c in range(NCORES):
        segs = [(b, c + 8 * h) for b in range(B) for h in range(2)]
        x_shard = np.stack([x[b, 512 * j:512 * (j + 1), :] for (b, j) in segs])
        halo = np.zeros((NSEG, 2, D), np.float32)
        for si, (b, j) in enumerate(segs):
            if j > 0:
                halo[si, 0] = x[b, 512 * j - 1]
            if j < 15:
                halo[si, 1] = x[b, 512 * (j + 1)]
        cost = np.zeros((NSEG, 4, 128, SL), np.float32)
        sint = np.zeros((NSEG, 4, 128, SL), np.float32)
        for si, (b, j) in enumerate(segs):
            pos = np.arange(512 * j, 512 * (j + 1), dtype=np.float32)
            ang = pos[None, :] * theta[:, None]
            cost[si] = np.cos(ang).reshape(4, 128, SL)
            sint[si] = np.sin(ang).reshape(4, 128, SL)

        loc = np.arange(NSEG * SL).reshape(32, 128).T
        v = (c * NSEG * SL + loc + 1).astype(np.float32)
        tok_hic = np.floor(v / 256.0).astype(np.float32)
        tok_loc = (v - 256.0 * tok_hic).astype(np.float32)

        maskcol = np.zeros((E, 8), np.float32)
        for s_own in range(NSEG):
            b_own, h_own = s_own // 2, s_own % 2
            g_own = 16 * b_own + c + 8 * h_own
            for r in range(NCORES):
                for l in range(8):
                    bb, hh = l // 2, l % 2
                    g = 16 * bb + r + 8 * hh
                    if g < g_own:
                        maskcol[r * 8 + l, s_own] = 1.0

        comb = np.zeros((128, 128), np.float32)
        for e in range(64):
            comb[e, e] = 256.0
            comb[e + 64, e] = 1.0
            comb[e, 64 + e] = 1.0
            comb[e + 64, 64 + e] = 1.0
        in_maps.append(dict(
            x_shard=x_shard, x_halo=halo, comb_bf=comb.astype(BF),
            wqk_p=wqk_p, bqk_cols=bqk_cols,
            lepe_w=lepe_w.astype(np.float32), lepe_b=lepe_b.reshape(1, D).astype(np.float32),
            costab=cost.astype(BF), sintab=sint.astype(BF),
            gate_ext=gate_ext, sgu_w=sgu_w.astype(np.float32),
            sdown_w=sdown_w.astype(np.float32),
            egu_w=np.ascontiguousarray(egu_w[8 * c:8 * c + 8]).astype(np.float32),
            edown_w=np.ascontiguousarray(edown_w[8 * c:8 * c + 8]).astype(np.float32),
            c64=c64c, c64p1=c64c + 1.0, c512=c512c,
            ustrict=ustrictc, ident_bf=identc, ident_f=identfc,
            ones_bf=np.ones((1, 128), BF), onec_bf=np.ones((128, 1), BF),
            ones_f=np.ones((1, 128), np.float32),
            tok_hi=tok_hic, tok_lo=tok_loc, maskcols=maskcol,
            zrow_col=np.full((128, 1), float(c * X4ROWS + X4ROWS - 1), np.float32),
            eps_col=np.full((128, 1), 1e-5, np.float32),
        ))
    return in_maps


def assemble_output(results):
    out = np.zeros((B, N, D), np.float32)
    for c in range(NCORES):
        r = results[c]["out_d"].reshape(NSEG, SL, D)
        segs = [(b, c + 8 * h) for b in range(B) for h in range(2)]
        for si, (b, j) in enumerate(segs):
            out[b, 512 * j:512 * (j + 1), :] = r[si]
    return out


def kernel(**inputs):
    in_maps = prepare_inputs(inputs)
    nc = build_program()
    res = run_bass_kernel_spmd(nc, in_maps, core_ids=list(range(NCORES)))
    return assemble_output(res.results)


# version 11 (final)
# speedup vs baseline: 1.4525x; 1.0364x over previous
import numpy as np
import ml_dtypes
from contextlib import ExitStack

import concourse.bass as bass
import concourse.mybir as mybir
from concourse.bass_utils import run_bass_kernel_spmd

F32 = mybir.dt.float32
BF16 = mybir.dt.bfloat16
U8 = mybir.dt.uint8
U32 = mybir.dt.uint32
AF = mybir.ActivationFunctionType
ALU = mybir.AluOpType
AX = mybir.AxisListType
IOA = bass.IndirectOffsetOnAxis
BF = ml_dtypes.bfloat16

NCORES = 8
B, N, D = 4, 8192, 1024
NSEG, SL, NT = 8, 512, 4
NCH, NH, HD = 8, 8, 128
E, EL, CAP = 64, 8, 512
ISQN = float(1.0 / np.sqrt(np.float32(N)))
X4ROWS = EL * CAP + 1
MAX_PHASE = 7

T, V, S, G, Y = "tensor", "vector", "scalar", "gpsimd", "sync"


class Sem:
    def __init__(self, name, dma=False):
        self.name, self.n, self.h = name, 0, None
        self.dma = dma
        self.max_waited = 0
        self.last_sync = 0

    def inc(self, k=1):
        self.n += k
        return self.n


class KB:
    def __init__(self):
        self.ops = []
        self.auto = {}

    def emit(self, en, fn):
        self.ops.append((en, fn))

    def op(self, en, meth, *args, inc=None, **kw):
        aset = self.auto.get(en)
        if aset is not None:
            v = aset.n
            if v:
                self.emit(en, lambda eng, s=aset, v=v: eng.wait_ge(s.h, v))
            if meth != "dma_start":
                inc = list(inc) if inc else []
                if not any(sem is aset for sem, _ in inc):
                    inc.append((aset, 1))
        if inc:
            for sem, k in inc:
                if sem.dma and sem.max_waited > sem.last_sync:
                    v = sem.max_waited
                    sem.last_sync = v
                    self.emit(en, lambda eng, s=sem, v=v: eng.wait_ge(s.h, v))
        if inc:
            for sem, k in inc:
                sem.inc(k)

        def fn(eng, meth=meth, args=args, kw=kw, inc=inc):
            inst = getattr(eng, meth)(*args, **kw)
            if inc:
                for sem, k in inc:
                    inst = inst.then_inc(sem.h, k)
        self.emit(en, fn)

    def wait(self, en, sem, val=None):
        v = sem.n if val is None else val
        sem.max_waited = max(sem.max_waited, v)
        self.emit(en, lambda eng, s=sem, v=v: eng.wait_ge(s.h, v))


def build_program(max_phase=MAX_PHASE, dev=False, fake_cc=False):
    nc = bass.Bass("TRN2", target_bir_lowering=False)
    kb = KB()

    def din(name, shape, dt=F32):
        return nc.dram_tensor(name, shape, dt, kind="ExternalInput").ap()

    def dscr(name, shape, dt=F32):
        return nc.dram_tensor(name, shape, dt, kind="ExternalOutput" if dev else "Internal").ap()

    def dint(name, shape, dt=F32):
        return nc.dram_tensor(name, shape, dt, kind="Internal").ap()

    x_shard = din("x_shard", [NSEG, SL, D])
    x_halo = din("x_halo", [NSEG, 2, D])
    wqk_p = din("wqk_p", [D, 2 * D])
    bqk_cols = din("bqk_cols", [128, 16])
    lepe_w = din("lepe_w", [3, D, D])
    lepe_b = din("lepe_b", [1, D])
    costab = din("costab", [NSEG, 4, 128, SL], BF16)
    sintab = din("sintab", [NSEG, 4, 128, SL], BF16)
    gate_ext = din("gate_ext", [D, 72])
    sgu_w = din("sgu_w", [D, 4096])
    sdown_w = din("sdown_w", [2048, D])
    egu_w = din("egu_w", [EL, D, D])
    edown_w = din("edown_w", [EL, 512, D])
    c64 = din("c64", [128, E])
    c64p1 = din("c64p1", [128, E])
    c512 = din("c512", [128, CAP])
    ustrict = din("ustrict", [128, 128], BF16)
    ident_bf = din("ident_bf", [128, 128], BF16)
    ident_f = din("ident_f", [128, 128])
    ones_bf = din("ones_bf", [1, 128], BF16)
    onec_bf = din("onec_bf", [128, 1], BF16)
    ones_f = din("ones_f", [1, 128])
    tok_hi = din("tok_hi", [128, 32])
    tok_lo = din("tok_lo", [128, 32])
    maskcols = din("maskcols", [E, 8])
    comb_bf = din("comb_bf", [128, 128], BF16)
    zrow_col = din("zrow_col", [128, 1])
    eps_col = din("eps_col", [128, 1])

    out_d = nc.dram_tensor("out_d", [NSEG * SL, D], F32, kind="ExternalOutput").ap()

    qel_d = dscr("qel_d", [NSEG, 128, NCH * SL], BF16)
    qr_d = dscr("qr_d", [NSEG, 128, NCH * SL], BF16)
    lepe_d = dscr("lepe_d", [NSEG, SL, D], BF16)
    x2_d = dscr("x2_d", [NSEG, SL, D], F32)
    x2s_d = dscr("x2s_d", [NSEG, SL, D], F32)
    xlt2_d = dscr("xlt2_d", [NSEG, 128, NCH * SL], BF16)
    cl_d = dscr("cl_d", [NSEG, 128, NT * E], F32)
    wa_d = dscr("wa_d", [NSEG, 128, NT * E], F32)
    lg_d = dscr("lg_d", [NSEG * NT, 128, 80], F32) if dev else None
    dbg_d = dscr("dbg_d", [128, 2048], F32) if dev else None
    dbg2_d = dscr("dbg2_d", [128, 8192], BF16) if dev else None
    kv_in = dint("kv_in", [128, 4128], F32)
    kv_out = nc.dram_tensor("kv_out", [128, 4128], F32, kind="Internal", addr_space="Shared").ap()
    cnt_in = dint("cnt_in", [8, E], F32)
    cnt_out = dint("cnt_out", [E, E], F32)
    smw_in = dint("smw_in", [128, CAP], F32)
    smw_out = dint("smw_out", [16, CAP], F32)
    x2l_d = dint("x2l_d", [NSEG * SL, D], BF16)
    x2a_d = nc.dram_tensor("x2a_d", [NCORES * NSEG * SL, D], BF16, kind="Internal", addr_space="Shared").ap()
    x4l_d = dint("x4l_d", [X4ROWS, D], BF16)
    x4a_d = nc.dram_tensor("x4a_d", [NCORES * X4ROWS, D], BF16, kind="Internal", addr_space="Shared").ap()

    with ExitStack() as ctx:
        def sbuf(name, shape, dt=F32):
            return ctx.enter_context(nc.sbuf_tensor(name, shape, dt))

        def sem(name, dma=False):
            s = Sem(name, dma=dma)
            s.h = ctx.enter_context(nc.semaphore(name))
            return s

        c64_sb = sbuf("c64_sb", [128, E])
        c64p1_sb = sbuf("c64p1_sb", [128, E])
        c512_sb = sbuf("c512_sb", [128, CAP])
        ustrict_sb = sbuf("ustrict_sb", [128, 128], BF16)
        ident_sb = sbuf("ident_sb", [128, 128], BF16)
        identf_sb = sbuf("identf_sb", [128, 128])
        ones_bf_sb = sbuf("ones_bf_sb", [1, 128], BF16)
        onec_bf_sb = sbuf("onec_bf_sb", [128, 1], BF16)
        ones_f_sb = sbuf("ones_f_sb", [1, 128])
        tok_hi_sb = sbuf("tok_hi_sb", [128, 32])
        tok_lo_sb = sbuf("tok_lo_sb", [128, 32])
        maskcols_sb = sbuf("maskcols_sb", [E, 8])
        comb_sb = sbuf("comb_sb", [128, 128], BF16)
        zrow_sb = sbuf("zrow_sb", [128, 1])
        eps_sb = sbuf("eps_sb", [128, 1])
        bqk_sb = sbuf("bqk_sb", [128, 16])
        lepeb_sb = sbuf("lepeb_sb", [1, D], BF16)
        gate_sb = sbuf("gate_sb", [128, NCH * 72], BF16)

        kvbf_sb = sbuf("kvbf_sb", [128, B * NH * HD], BF16)
        kmean_sb = sbuf("kmean_sb", [128, B * NCH])
        sall_sb = sbuf("sall_sb", [128, B * NCH * NH], BF16)
        ksum_sb = sbuf("ksum_sb", [128, B * NCH])
        edesc_sb = sbuf("edesc_sb", [128, 256])
        pos_sb = sbuf("pos_sb", [128, 256])
        addrf_sb = sbuf("addrf_sb", [128, 256])
        addru_sb = sbuf("addru_sb", [128, 256], U32)
        sg_sb = sbuf("sg_sb", [128, 32])
        rowbase_sb = sbuf("rowbase_sb", [1, 32 * E])
        cnt1_sb = sbuf("cnt1_sb", [1, NSEG * E])
        tot_sb = sbuf("tot_sb", [1, 2 * E])
        smloc_sb = sbuf("smloc_sb", [16, CAP])
        smi_sb = sbuf("smi_sb", [16, CAP])
        smT_sb = sbuf("smT_sb", [128, 64])
        smTu_sb = sbuf("smTu_sb", [128, 64], U32)
        wt_sb = sbuf("wt_sb", [128, 32])
        stat_sb = sbuf("stat_sb", [128, 6])
        hstat_sb = sbuf("hstat_sb", [2, 6])
        bn6_sb = sbuf("bn6_sb", [128, 12])
        hbn_sb = sbuf("hbn_sb", [2, 12])
        hbn2_sb = sbuf("hbn2_sb", [128, 12])
        stat2_sb = sbuf("stat2_sb", [128, 6])
        red8_sb = sbuf("red8_sb", [128, 8])

        AAB = 68 * 1024
        AWB = 96 * 1024
        aa = sbuf("aa", [128, AAB], U8)
        aw = sbuf("aw", [128, AWB], U8)

        def av(off, nbytes, dt):
            assert off + nbytes <= AAB, (off, nbytes)
            return aa[:, off:off + nbytes].bitcast(dt)

        def wv(off, nbytes, dt):
            assert off + nbytes <= AWB, (off, nbytes)
            return aw[:, off:off + nbytes].bitcast(dt)

        xln_sb = av(0, 8192, BF16)
        xlt_sb = av(8192, 8224, BF16)
        qel_sb = av(16416, 8192, BF16)
        kel_sb = av(24608, 8192, BF16)
        qr2_sb = av(32800, 2048, BF16)
        kr2_sb = av(34848, 2048, BF16)
        krt1_sb = av(36896, 1024, BF16)
        cos_sb = av(38944, 4096, BF16)
        sin_sb = av(43040, 4096, BF16)
        eexp_sb = av(47136, 1024, BF16)
        erel_sb = av(48160, 1024, BF16)
        rexp_sb = av(49184, 1024, BF16)
        rrel_sb = av(50208, 1024, BF16)
        xt_sb = av(51232, 4096, F32)
        kvev_sb = av(59424, 4096, F32)
        lepet_sb2 = [av(63520 + 2048 * i, 2048, BF16) for i in range(2)]
        xhf_sb = sbuf("xhf_sb", [2, D])
        xlnh_sb = sbuf("xlnh_sb", [2, D], BF16)

        kvf_sb = av(0, 16512, F32)

        xln2_sb = av(0, 8192, BF16)
        xlt2_sb = av(8192, 8192, BF16)
        p3qel_sb = av(16416, 8192, BF16)
        p3qr_sb = av(24608, 8192, BF16)
        attnz_sb = [av(32800 + 4096 * i, 4096, F32) for i in range(2)]
        p3x_sb = [av(40992 + 4096 * i, 4096, F32) for i in range(2)]
        p3lep_sb = [av(49184 + 2048 * i, 2048, BF16) for i in range(2)]
        x2t3_sb = [av(53280 + 4096 * i, 4096, F32) for i in range(2)]
        zden_sb = av(61472, 2048, F32)
        zt_sb = av(63520, 128, F32)
        z_sb = av(63648, 128, F32)
        lgt_sb = [av(63776, 320, F32) for i in range(1)]
        lgt_sb = [av(63776 + 320 * i, 320, F32) for i in range(2)]
        m8_sb = [av(64416 + 32 * i, 32, F32) for i in range(2)]
        repl_sb = av(64480, 256, F32)
        selt_sb = [av(64736 + 128 * i, 128, BF16) for i in range(2)]
        eall_sb = av(64992, 256, F32)
        wm3_sb = [av(65248 + 256 * i, 256, F32) for i in range(2)]
        rsum_sb = av(65760, 16, F32)
        ed8_sb = av(65776, 32, F32)
        clt_sb = [av(65808 + 256 * i, 256, F32) for i in range(2)]
        tot2_sb = av(66320, 8, F32)

        NPOSH, NHL = 6, 3
        cnts_sb = av(0, 256, F32)
        clseg_sb = av(256, 1024, F32)
        waseg_sb = av(1280, 1024, F32)
        cg_sb = av(2304, 256, F32)
        mask_sb = av(2560, 1024, BF16)
        wmp_sb = av(3584, 2048, F32)
        junk_sb = av(5632, 256, F32)
        posh_sb = [av(5888 + 1024 * i, 1024, BF16) for i in range(NPOSH)]
        mhl_sb = [av(12032 + 2048 * i, 2048, BF16) for i in range(NHL)]
        whl_sb = [av(18176 + 2048 * i, 2048, BF16) for i in range(NHL)]
        smhlb_sb2 = av(0, 2048, BF16)
        smv_sb = av(2048, 2048, F32)
        wv_sb = av(4096, 2048, F32)
        eh_sb = [av(6144, 1024, F32), av(7168, 1024, F32)]
        a1_sb = av(8192, 1024, F32)
        alive_sb = av(9216, 1024, F32)

        h_sb = av(33792, 16384, BF16)
        sguv = wv(0, 65536, BF16)
        p5x_sb = av(50176, 4096, F32)
        p5s_sb = av(54272, 4096, F32)
        silu_sb = av(58368, 1024, BF16)
        p5xlt_sb = av(59392, 8192, BF16)

        egu_sb = [av(18432, 16384, BF16), av(34816, 16384, BF16)]
        edn_sb = [av(51200, 8192, BF16), av(59392, 8192, BF16)]
        rows_sb = av(0, 2048, BF16)
        rowsT_sb = av(2048, 8192, BF16)
        he_sb = av(10240, 4096, BF16)
        oute_sb = av(14336, 2048, BF16)
        zero_sb = av(16384, 2048, BF16)
        silu6_sb = av(17408, 1024, BF16)

        NFT = 8
        fin_sb = [av(4096 * i, 4096, F32) for i in range(NFT)]

        ps0 = ctx.enter_context(nc.psum_tensor("ps0", [128, SL], F32))
        ps1 = ctx.enter_context(nc.psum_tensor("ps1", [128, SL], F32))
        ps2 = ctx.enter_context(nc.psum_tensor("ps2", [128, SL], F32))
        ps_trF = ctx.enter_context(nc.psum_tensor("ps_trF", [128, 2 * SL], BF16))
        ps_tr = ps_trF[:, 0:SL]
        ps_tr2 = ps_trF[:, SL:2 * SL]
        ps_kv = ctx.enter_context(nc.psum_tensor("ps_kv", [128, 2 * SL], F32))
        ps3 = ctx.enter_context(nc.psum_tensor("ps3", [128, SL], F32))
        ps4 = ctx.enter_context(nc.psum_tensor("ps4", [128, SL], F32))


        qL = sem("qL", dma=True); qS = sem("qS", dma=True); qG = sem("qG", dma=True)

        qE = [sem("qE0", dma=True), sem("qE1", dma=True)]
        qX2 = [sem("qX2a", dma=True), sem("qX2b", dma=True)]
        qCL = [sem("qCLa", dma=True), sem("qCLb", dma=True)]
        qWM = [sem("qWMa", dma=True), sem("qWMb", dma=True)]
        qXS = [sem("qXS%d" % i, dma=True) for i in range(4)]
        qW5 = sem("qW5", dma=True)
        qCkv = sem("qCkv"); qCcnt = sem("qCcnt"); qCx2 = sem("qCx2")
        qCsm = sem("qCsm"); qCx4 = sem("qCx4")
        mm = sem("mm"); ve = sem("ve"); se = sem("se"); tE = sem("tE"); gv = sem("gv")
        kb.auto = {V: ve, S: se}

        for dst, src in [(c64_sb, c64), (c64p1_sb, c64p1), (c512_sb, c512),
                         (ustrict_sb, ustrict), (ident_sb, ident_bf), (identf_sb, ident_f),
                         (ones_bf_sb, ones_bf), (onec_bf_sb, onec_bf), (ones_f_sb, ones_f),
                         (tok_hi_sb, tok_hi), (tok_lo_sb, tok_lo),
                         (maskcols_sb, maskcols), (comb_sb, comb_bf),
                         (zrow_sb, zrow_col), (eps_sb, eps_col),
                         (bqk_sb, bqk_cols)]:
            kb.op(Y, "dma_start", out=dst[:], in_=src, inc=[(qL, 16)])
        P0_sync = qL.n
        kb.op(G, "dma_start", out=wv(0, 32768, BF16).rearrange("p (c f) -> p c f", f=2 * D),
              in_=wqk_p.rearrange("(c p) f -> p c f", p=128), inc=[(qG, 16)])
        kb.op(G, "dma_start", out=wv(32768, 49152, BF16).rearrange("p (k c f) -> p k c f", k=3, c=NCH),
              in_=lepe_w.rearrange("k (c p) f -> p k c f", p=128), inc=[(qG, 16)])
        kb.op(G, "dma_start", out=lepeb_sb[:], in_=lepe_b, inc=[(qG, 16)])
        kb.op(G, "dma_start", out=gate_sb[:].rearrange("p (c f) -> p c f", f=72),
              in_=gate_ext.rearrange("(c p) f -> p c f", p=128), inc=[(qG, 16)])
        P0_gp = qG.n
        wqk_sb = wv(0, 32768, BF16)
        lepew_sb = wv(32768, 49152, BF16)
        kb.op(V, "memset", ksum_sb[:], 0.0, inc=[(ve, 1)])
        kb.wait(T, qL, P0_sync)
        kb.wait(V, qL, P0_sync)
        kb.wait(S, qL, P0_sync)
        kb.wait(T, qG, P0_gp)

        def emit_ln(src_ap, dst_ap, bn, st, part=128):
            kb.op(V, "bn_stats", out=bn[0:part, 0:6], in_=src_ap[0:part, 0:SL], inc=[(ve, 1)])
            kb.op(V, "bn_stats", out=bn[0:part, 6:12], in_=src_ap[0:part, SL:D], inc=[(ve, 1)])
            kb.wait(V, ve, ve.n)
            kb.op(V, "bn_aggr", out=st[0:part, 0:2], in_=bn[0:part, :], inc=[(ve, 1)])
            kb.wait(S, ve, ve.n)
            kb.op(S, "activation", out=st[0:part, 2:3], in_=st[0:part, 1:2], func=AF.Sqrt,
                  bias=eps_sb[0:part, 0:1], scale=1.0, inc=[(se, 1)])
            kb.wait(V, se, se.n)
            kb.op(V, "reciprocal", out=st[0:part, 3:4], in_=st[0:part, 2:3], inc=[(ve, 1)])
            kb.wait(V, ve, ve.n)
            kb.op(V, "scalar_tensor_tensor", out=st[0:part, 5:6], in0=st[0:part, 0:1],
                  scalar=-1.0, in1=st[0:part, 3:4], op0=ALU.mult, op1=ALU.mult, inc=[(ve, 1)])
            kb.wait(S, ve, ve.n)
            kb.op(S, "activation", out=dst_ap, in_=src_ap[0:part, :], func=AF.Identity,
                  bias=st[0:part, 5:6], scale=st[0:part, 3:4], inc=[(se, 1)])
            return se.n

        if max_phase >= 1:
            p1_psg = [0, 0, 0, 0]
            halo_ve_g = halo_se_g = rope_g = xln_g = xlt_g = 0
            for s in range(NSEG):
                b = s // 2
                kb.wait(Y, ve, halo_ve_g)
                kb.wait(Y, se, halo_se_g)
                kb.op(Y, "dma_start", out=xhf_sb[:], in_=x_halo[s], inc=[(qL, 16)])
                kb.wait(V, qL, qL.n)
                kb.wait(S, mm, xln_g)
                emit_ln(xhf_sb, xlnh_sb[:], hbn_sb, hstat_sb, part=2)
                halo_ve_g = ve.n
                halo_se_g = se.n
                for t in range(NT):
                    kb.wait(Y, se, se.n)
                    kb.op(Y, "dma_start", out=xt_sb[:], in_=x_shard[s, t * 128:(t + 1) * 128, :],
                          inc=[(qL, 16)])
                    kb.wait(V, qL, qL.n)
                    if t == 0:
                        kb.wait(S, mm, xln_g)
                    emit_ln(xt_sb, xln_sb[:, t * D:(t + 1) * D], bn6_sb, stat_sb)
                xln_done = se.n

                kb.wait(T, se, xln_done)
                for c in range(NCH):
                    for t in range(NT):
                        kb.op(T, "transpose", ps_tr[:, t * 128:(t + 1) * 128],
                              xln_sb[:, t * D + c * 128: t * D + (c + 1) * 128], ident_sb[:],
                              inc=([(mm, 1)] if t == NT - 1 else None))
                    kb.wait(V, mm, mm.n)
                    if c == 0:
                        kb.wait(V, mm, xlt_g)
                        kb.wait(V, qS, qS.n)
                    kb.op(V, "tensor_copy", out=xlt_sb[:, c * 514 + 1: c * 514 + 513],
                          in_=ps_tr[:], inc=[(ve, 1)])
                    kb.wait(T, ve, ve.n)
                for c in range(NCH):
                    kb.op(T, "transpose", ps_tr[:, c * 2:(c + 1) * 2],
                          xlnh_sb[:, c * 128:(c + 1) * 128], ident_sb[0:2, 0:2],
                          inc=([(mm, 1)] if c == NCH - 1 else None))
                kb.wait(V, mm, mm.n)
                for c in range(NCH):
                    kb.op(V, "tensor_copy", out=xlt_sb[:, c * 514: c * 514 + 1],
                          in_=ps_tr[:, c * 2: c * 2 + 1], inc=[(ve, 1)])
                    kb.op(V, "tensor_copy", out=xlt_sb[:, c * 514 + 513: c * 514 + 514],
                          in_=ps_tr[:, c * 2 + 1: c * 2 + 2], inc=[(ve, 1)])
                kb.wait(T, ve, ve.n)

                kb.wait(Y, ve, rope_g)
                kb.op(Y, "dma_start", out=cos_sb.rearrange("p (c q) -> p c q", q=SL),
                      in_=costab[s].transpose([1, 0, 2]), inc=[(qL, 16)])
                kb.op(Y, "dma_start", out=sin_sb.rearrange("p (c q) -> p c q", q=SL),
                      in_=sintab[s].transpose([1, 0, 2]), inc=[(qL, 16)])
                rope_ld = qL.n

                qk_psg = [p1_psg[0], p1_psg[1]]
                elu_g = [0, 0]
                for f in range(16):
                    dst = qel_sb if f < 8 else kel_sb
                    fc = f % 8
                    fp = f % 2
                    psq = ps0 if fp == 0 else ps1
                    eex = eexp_sb if fp == 0 else rexp_sb
                    ere = erel_sb if fp == 0 else rrel_sb
                    kb.wait(T, se, qk_psg[fp])
                    for c in range(NCH):
                        kb.op(T, "matmul", psq[:],
                              wqk_sb[:, c * 2048 + f * 128: c * 2048 + (f + 1) * 128],
                              xlt_sb[:, c * 514 + 1: c * 514 + 513],
                              start=(c == 0), stop=(c == NCH - 1),
                              inc=([(mm, 1)] if c == NCH - 1 else None))
                    kb.wait(S, mm, mm.n)
                    kb.wait(S, ve, elu_g[fp])
                    if f == 0:
                        kb.wait(S, qS, qS.n)
                    kb.op(S, "activation", out=eex, in_=psq[:], func=AF.Exp,
                          bias=bqk_sb[:, f:f + 1], scale=1.0, inc=[(se, 1)])
                    kb.op(S, "activation", out=ere, in_=psq[:], func=AF.Relu,
                          bias=bqk_sb[:, f:f + 1], scale=1.0, inc=[(se, 1)])
                    qk_psg[fp] = se.n
                    kb.wait(V, se, se.n)
                    kb.op(V, "tensor_scalar", out=eex, in0=eex, scalar1=1.0,
                          scalar2=None, op0=ALU.min, inc=[(ve, 1)])
                    kb.op(V, "tensor_tensor", out=dst[:, fc * SL:(fc + 1) * SL], in0=eex,
                          in1=ere, op=ALU.add, inc=[(ve, 1)])
                    elu_g[fp] = ve.n
                qk_done = ve.n
                kb.wait(S, ve, qk_done)
                kb.op(S, "dma_start", out=qel_d[s], in_=qel_sb, inc=[(qS, 16)])

                for c in range(NCH):
                    kb.op(V, "tensor_reduce", out=red8_sb[:, c:c + 1],
                          in_=kel_sb[:, c * SL:(c + 1) * SL], axis=AX.X, op=ALU.add,
                          inc=([(ve, 1)] if c == NCH - 1 else None))
                kb.wait(V, ve, ve.n)
                kb.op(V, "tensor_tensor", out=ksum_sb[:, b * NCH:(b + 1) * NCH],
                      in0=ksum_sb[:, b * NCH:(b + 1) * NCH],
                      in1=red8_sb[:], op=ALU.add, inc=[(ve, 1)])

                kb.wait(V, qL, rope_ld)
                for (src, isq) in ((qel_sb, True), (kel_sb, False)):
                    dstr = qr2_sb if isq else kr2_sb
                    for jc in range(4):
                        co = cos_sb[:, jc * SL:(jc + 1) * SL]
                        si = sin_sb[:, jc * SL:(jc + 1) * SL]
                        ev = src[:, jc * SL:(jc + 1) * SL]
                        od = src[:, (4 + jc) * SL:(5 + jc) * SL]
                        kb.wait(V, qS, qS.n)
                        if not isq:
                            kb.wait(V, mm, mm.n)
                        kb.op(V, "tensor_tensor", out=eexp_sb[:], in0=co, in1=ev, op=ALU.mult, inc=[(ve, 1)])
                        kb.op(V, "tensor_tensor", out=erel_sb[:], in0=si, in1=od, op=ALU.mult, inc=[(ve, 1)])
                        kb.wait(V, ve, ve.n)
                        kb.op(V, "tensor_tensor", out=dstr[:, 0:SL],
                              in0=eexp_sb[:], in1=erel_sb[:], op=ALU.subtract, inc=[(ve, 1)])
                        kb.op(V, "tensor_tensor", out=eexp_sb[:], in0=co, in1=od, op=ALU.mult, inc=[(ve, 1)])
                        kb.op(V, "tensor_tensor", out=erel_sb[:], in0=si, in1=ev, op=ALU.mult, inc=[(ve, 1)])
                        kb.wait(V, ve, ve.n)
                        kb.op(V, "tensor_tensor", out=dstr[:, SL:2 * SL],
                              in0=eexp_sb[:], in1=erel_sb[:], op=ALU.add, inc=[(ve, 1)])
                        pair_done = ve.n
                        if isq:
                            kb.wait(S, ve, pair_done)
                            kb.op(S, "dma_start", out=qr_d[s, :, jc * SL:(jc + 1) * SL],
                                  in_=dstr[:, 0:SL], inc=[(qS, 16)])
                            kb.op(S, "dma_start", out=qr_d[s, :, (4 + jc) * SL:(5 + jc) * SL],
                                  in_=dstr[:, SL:2 * SL], inc=[(qS, 16)])
                        else:
                            kb.wait(T, ve, pair_done)
                            for half, h in ((0, jc), (1, jc + 4)):
                                for t in range(NT):
                                    kb.op(T, "transpose", ps_tr[:, t * 128:(t + 1) * 128],
                                          dstr[:, half * SL + t * 128: half * SL + (t + 1) * 128],
                                          ident_sb[:], inc=([(mm, 1)] if t == NT - 1 else None))
                                kb.wait(V, mm, mm.n)
                                for t in range(NT):
                                    kb.op(V, "tensor_copy", out=krt1_sb[:, t * 128:(t + 1) * 128],
                                          in_=ps_tr[:, t * 128:(t + 1) * 128],
                                          inc=([(ve, 1)] if t == NT - 1 else None))
                                kb.wait(T, ve, ve.n)
                                for t in range(NT):
                                    kb.op(T, "matmul", ps_kv[:, h * HD:(h + 1) * HD],
                                          krt1_sb[:, t * 128:(t + 1) * 128],
                                          xln_sb[:, t * D + h * 128:t * D + (h + 1) * 128],
                                          start=(t == 0), stop=(t == NT - 1),
                                          skip_group_check=True,
                                          inc=([(mm, 1)] if t == NT - 1 else None))
                                kb.wait(V, mm, mm.n)
                kv_mm = mm.n
                rope_g = ve.n
                xln_g = kv_mm
                kb.wait(S, mm, kv_mm)
                kb.wait(S, qG, qG.n)
                kb.op(S, "copy", kvev_sb[:, 0:SL], ps_kv[:, 0:SL], inc=[(se, 1)])
                kb.op(S, "copy", kvev_sb[:, SL:2 * SL], ps_kv[:, SL:2 * SL], inc=[(se, 1)])
                kv_ev = se.n
                kb.wait(T, se, kv_ev)
                kb.wait(G, se, kv_ev)
                kb.wait(G, qG, qG.n)
                if s % 2 == 0:
                    kb.op(G, "dma_start", out=kv_in[:, b * D:(b + 1) * D], in_=kvev_sb[:],
                          inc=[(qG, 16)])
                else:
                    kb.op(G, "dma_start", out=kv_in[:, b * D:(b + 1) * D], in_=kvev_sb[:],
                          accum_op=ALU.add, inc=[(qG, 16)])

                lep_psg = {"ps0": qk_psg[0], "ps1": qk_psg[1], "ps3": p1_psg[2], "ps4": p1_psg[3]}
                for t in range(NT):
                    pse = (ps0, ps1) if t % 2 == 0 else (ps3, ps4)
                    pk = ("ps0", "ps1") if t % 2 == 0 else ("ps3", "ps4")
                    for half in range(2):
                        pst = pse[half]
                        kb.wait(T, se, lep_psg[pk[half]])
                        for k in range(3):
                            for c in range(NCH):
                                kb.op(T, "matmul", pst[:],
                                      xlt_sb[:, c * 514 + t * 128 + k: c * 514 + t * 128 + k + 128],
                                      lepew_sb[:, (k * NCH + c) * D + half * SL:(k * NCH + c) * D + (half + 1) * SL],
                                      start=(k == 0 and c == 0), stop=False)
                        kb.op(T, "matmul", pst[:], ones_bf_sb[0:1, :],
                              lepeb_sb[0:1, half * SL:(half + 1) * SL],
                              start=False, stop=True, inc=[(mm, 1)])
                        kb.wait(S, mm, mm.n)
                        if half == 0:
                            kb.wait(S, qS, qS.n)
                        kb.op(S, "copy", lepet_sb2[t % 2][:, half * SL:(half + 1) * SL],
                              pst[:], inc=[(se, 1)])
                        lep_psg[pk[half]] = se.n
                    kb.op(S, "dma_start", out=lepe_d[s, t * 128:(t + 1) * 128, :],
                          in_=lepet_sb2[t % 2], inc=[(qS, 16)])
                p1_psg = [lep_psg["ps0"], lep_psg["ps1"], lep_psg["ps3"], lep_psg["ps4"]]
                xlt_g = mm.n

        mm_p1 = mm.n

        if max_phase >= 2:
            kb.wait(G, ve, ve.n)
            kb.wait(G, qG, qG.n)
            kb.op(G, "dma_start", out=kv_in[:, 4096:4128], in_=ksum_sb[:], inc=[(qG, 16)])
            kb.wait(G, qG, qG.n)
            kb.op(G, "collective_compute", "AllReduce", ALU.add,
                  replica_groups=[list(range(NCORES))],
                  ins=[kv_in.opt()], outs=[kv_out.opt()], inc=[(qCkv, 1)])
            kb.wait(Y, qCkv, qCkv.n)
            kb.wait(Y, mm, mm.n)
            kb.wait(Y, ve, ve.n)
            kb.wait(Y, se, se.n)
            kb.wait(Y, qS, qS.n)
            kb.op(Y, "dma_start", out=kvf_sb, in_=kv_out[:, :], inc=[(qL, 16)])
            kb.wait(V, qL, qL.n)
            kb.op(V, "tensor_scalar", out=kvbf_sb[:], in0=kvf_sb[:, 0:4096], scalar1=ISQN,
                  scalar2=None, op0=ALU.mult, inc=[(ve, 1)])
            kb.op(V, "tensor_scalar", out=kmean_sb[:], in0=kvf_sb[:, 4096:4128], scalar1=1.0 / N,
                  scalar2=None, op0=ALU.mult, inc=[(ve, 1)])
            kb.op(V, "memset", sall_sb[:], 0.0, inc=[(ve, 1)])
            kb.wait(V, ve, ve.n)
            for b in range(B):
                for c in range(NCH):
                    base = (b * NCH + c) * NH
                    h0 = 2 * (c % 4)
                    kb.op(V, "tensor_copy", out=sall_sb[0:64, base + h0: base + h0 + 1],
                          in_=kmean_sb[0:64, b * NCH + c: b * NCH + c + 1], inc=[(ve, 1)])
                    kb.op(V, "tensor_copy", out=sall_sb[64:128, base + h0 + 1: base + h0 + 2],
                          in_=kmean_sb[64:128, b * NCH + c: b * NCH + c + 1], inc=[(ve, 1)])
            p2_done = ve.n
            if dev:
                kb.wait(S, ve, p2_done)
                kb.op(S, "dma_start", out=dbg2_d[:, 0:4096], in_=kvbf_sb[:], inc=[(qS, 16)])
                kb.op(S, "dma_start", out=dbg_d[:, 0:32], in_=kmean_sb[:], inc=[(qS, 16)])
                kb.op(S, "dma_start", out=dbg2_d[:, 4096:4096 + 256], in_=sall_sb[:], inc=[(qS, 16)])

        if max_phase >= 3:
            kb.op(V, "memset", rowbase_sb[:], 0.0, inc=[(ve, 1)])
            kb.op(V, "memset", cnt1_sb[:], 0.0, inc=[(ve, 1)])
            for s in range(NSEG):
                b = s // 2
                g0 = s * NT
                kb.wait(Y, mm, mm.n)
                kb.wait(Y, ve, ve.n)
                kb.wait(Y, se, se.n)
                kb.wait(Y, qS, qS.n)
                kb.op(Y, "dma_start", out=p3qel_sb, in_=qel_d[s], inc=[(qL, 16)])
                kb.op(Y, "dma_start", out=p3qr_sb, in_=qr_d[s], inc=[(qL, 16)])
                ld_qq = qL.n
                kb.wait(T, qL, ld_qq)
                kb.wait(T, ve, ve.n)
                for c in range(NCH):
                    kb.op(T, "matmul", ps2[0:8, :],
                          sall_sb[:, (b * NCH + c) * NH:(b * NCH + c + 1) * NH],
                          p3qel_sb[:, c * SL:(c + 1) * SL],
                          start=(c == 0), stop=(c == NCH - 1),
                          inc=([(mm, 1)] if c == NCH - 1 else None))
                kb.wait(S, mm, mm.n)
                kb.op(S, "copy", zden_sb[0:8, :], ps2[0:8, :], inc=[(se, 1)])
                kb.wait(T, se, se.n)
                for t in range(NT):
                    kb.op(T, "transpose", ps2[:, 448 + t * 8: 456 + t * 8],
                          zden_sb[0:8, t * 128:(t + 1) * 128], identf_sb[0:8, 0:8],
                          inc=([(mm, 1)] if t == NT - 1 else None))
                kb.wait(V, mm, mm.n)
                kb.op(V, "tensor_scalar", out=zt_sb[:, 0:32], in0=ps2[:, 448:480], scalar1=1e-6,
                      scalar2=None, op0=ALU.add, inc=[(ve, 1)])
                kb.op(V, "reciprocal", out=z_sb[:, 0:32], in_=zt_sb[:, 0:32], inc=[(ve, 1)])
                z_done = ve.n
                kb.wait(T, ve, z_done)

                og = {0: 0, 1: 0}
                ldg = {0: 0, 1: 0}
                for t in range(NT):
                    g = g0 + t
                    pr = t % 2
                    odst = [(h, ps_kv[:, h * HD:(h + 1) * HD]) for h in range(NH)]
                    kb.wait(T, ve, og[1 - pr])
                    for h, dstp in odst:
                        kb.op(T, "matmul", dstp,
                              p3qr_sb[:, h * SL + t * 128: h * SL + (t + 1) * 128],
                              kvbf_sb[:, (b * NH + h) * HD:(b * NH + h + 1) * HD],
                              start=(h % 4 == 0), stop=(h % 4 == 3),
                              skip_group_check=True,
                              inc=([(mm, 1)] if h == NH - 1 else None))
                    o_mm = mm.n
                    kb.wait(Y, ve, ldg[pr])
                    kb.wait(Y, qX2[pr], qX2[pr].n)
                    kb.op(Y, "dma_start", out=p3x_sb[pr], in_=x_shard[s, t * 128:(t + 1) * 128, :],
                          inc=[(qL, 16)])
                    kb.op(Y, "dma_start", out=p3lep_sb[pr], in_=lepe_d[s, t * 128:(t + 1) * 128, :],
                          inc=[(qL, 16)])
                    ld_xt = qL.n
                    kb.wait(V, mm, o_mm)
                    for h, dstp in odst:
                        kb.op(V, "tensor_scalar", out=attnz_sb[pr][:, h * HD:(h + 1) * HD],
                              in0=dstp,
                              scalar1=z_sb[:, t * 8 + h: t * 8 + h + 1], scalar2=None,
                              op0=ALU.mult, inc=([(ve, 1)] if h == NH - 1 else None))
                    og[pr] = ve.n
                    kb.wait(V, qL, ld_xt)
                    kb.wait(V, qX2[pr], qX2[pr].n)
                    kb.op(V, "scalar_tensor_tensor", out=x2t3_sb[pr], in0=p3lep_sb[pr][:],
                          scalar=1.0, in1=p3x_sb[pr][:], op0=ALU.mult, op1=ALU.add, inc=[(ve, 1)])
                    kb.op(V, "tensor_tensor", out=x2t3_sb[pr], in0=x2t3_sb[pr], in1=attnz_sb[pr],
                          op=ALU.add, inc=[(ve, 1)])
                    x2_done = ve.n
                    kb.wait(S, ve, x2_done)
                    kb.op(S, "dma_start", out=x2_d[s, t * 128:(t + 1) * 128, :], in_=x2t3_sb[pr],
                          inc=[(qX2[pr], 16)])
                    bnp, stp = (bn6_sb, stat_sb) if pr == 0 else (hbn2_sb, stat2_sb)
                    kb.op(V, "bn_stats", out=bnp[:, 0:6], in_=x2t3_sb[pr][:, 0:SL], inc=[(ve, 1)])
                    kb.op(V, "bn_stats", out=bnp[:, 6:12], in_=x2t3_sb[pr][:, SL:D], inc=[(ve, 1)])
                    kb.op(V, "bn_aggr", out=stp[:, 0:2], in_=bnp[:], inc=[(ve, 1)])
                    kb.wait(S, ve, ve.n)
                    kb.op(S, "activation", out=stp[:, 2:3], in_=stp[:, 1:2], func=AF.Sqrt,
                          bias=eps_sb[:, 0:1], scale=1.0, inc=[(se, 1)])
                    kb.wait(V, se, se.n)
                    kb.op(V, "reciprocal", out=stp[:, 3:4], in_=stp[:, 2:3], inc=[(ve, 1)])
                    kb.op(V, "scalar_tensor_tensor", out=stp[:, 5:6], in0=stp[:, 0:1],
                          scalar=-1.0, in1=stp[:, 3:4], op0=ALU.mult, op1=ALU.mult, inc=[(ve, 1)])
                    kb.wait(S, ve, ve.n)
                    kb.op(S, "activation", out=xln2_sb[:, t * D:(t + 1) * D], in_=x2t3_sb[pr][:],
                          func=AF.Identity, bias=stp[:, 5:6], scale=stp[:, 3:4], inc=[(se, 1)])
                    ldg[pr] = ve.n
                    ln2_done = se.n
                    kb.op(S, "dma_start", out=x2l_d[s * SL + t * 128: s * SL + (t + 1) * 128, :],
                          in_=xln2_sb[:, t * D:(t + 1) * D], inc=[(qS, 16)])

                kb.wait(T, se, se.n)
                for c in range(NCH):
                    for t in range(NT):
                        kb.op(T, "transpose", ps_tr[:, t * 128:(t + 1) * 128],
                              xln2_sb[:, t * D + c * 128: t * D + (c + 1) * 128], ident_sb[:],
                              inc=([(mm, 1)] if t == NT - 1 else None))
                    kb.wait(V, mm, mm.n)
                    kb.op(V, "tensor_copy", out=xlt2_sb[:, c * SL:(c + 1) * SL],
                          in_=ps_tr[:], inc=[(ve, 1)])
                    kb.wait(T, ve, ve.n)
                xlt2_done = ve.n
                kb.wait(S, ve, xlt2_done)
                kb.op(S, "dma_start", out=xlt2_d[s], in_=xlt2_sb, inc=[(qS, 16)])

                for t in range(NT):
                    g = g0 + t
                    pr = t % 2
                    rcol = 0 if pr == 0 else 128
                    ccol = 256 if pr == 0 else 320
                    tcol = 384 if pr == 0 else 448
                    for c in range(NCH):
                        kb.op(T, "matmul", ps3[:, rcol:rcol + 72],
                              xlt2_sb[:, c * SL + t * 128: c * SL + (t + 1) * 128],
                              gate_sb[:, c * 72:(c + 1) * 72],
                              start=(c == 0), stop=(c == NCH - 1),
                              inc=([(mm, 1)] if c == NCH - 1 else None))
                    kb.wait(S, mm, mm.n)
                    kb.op(S, "copy", lgt_sb[pr][:, 0:72], ps3[:, rcol:rcol + 72], inc=[(se, 1)])
                    lg_ev = se.n
                    kb.op(S, "activation", out=sg_sb[:, g:g + 1], in_=lgt_sb[pr][:, 64:65],
                          func=AF.Sigmoid, bias=0.0, scale=1.0, inc=[(se, 1)])
                    if dev:
                        kb.op(S, "dma_start", out=lg_d[g][:, 0:72], in_=lgt_sb[pr][:, 0:72],
                              inc=[(qS, 16)])
                        kb.wait(S, qS, qS.n)
                    kb.wait(V, se, lg_ev)
                    kb.op(V, "max", out=m8_sb[pr], in_=lgt_sb[pr][:, 0:64], inc=[(ve, 1)])
                    kb.op(V, "match_replace", out=repl_sb, in_to_replace=m8_sb[pr],
                          in_values=lgt_sb[pr][:, 0:64], imm_value=-1e30, inc=[(ve, 1)])
                    kb.op(V, "tensor_tensor", out=selt_sb[pr], in0=lgt_sb[pr][:, 0:64], in1=repl_sb,
                          op=ALU.not_equal, inc=[(ve, 1)])
                    kb.op(V, "tensor_scalar", out=rsum_sb[:, 0:1], in0=m8_sb[pr][:, 0:1],
                          scalar1=-1.0, scalar2=None, op0=ALU.mult, inc=[(ve, 1)])
                    kb.wait(S, ve, ve.n)
                    kb.op(S, "activation", out=eall_sb, in_=lgt_sb[pr][:, 0:64], func=AF.Exp,
                          bias=rsum_sb[:, 0:1], scale=1.0, inc=[(se, 1)])
                    kb.wait(V, se, se.n)
                    kb.wait(V, qWM[pr], qWM[pr].n)
                    kb.op(V, "tensor_tensor", out=wm3_sb[pr], in0=eall_sb, in1=selt_sb[pr],
                          op=ALU.mult, inc=[(ve, 1)])
                    kb.op(V, "tensor_reduce", out=rsum_sb[:, 1:2], in_=wm3_sb[pr], axis=AX.X,
                          op=ALU.add, inc=[(ve, 1)])
                    kb.op(V, "reciprocal", out=rsum_sb[:, 2:3], in_=rsum_sb[:, 1:2], inc=[(ve, 1)])
                    kb.op(V, "tensor_scalar", out=wm3_sb[pr], in0=wm3_sb[pr],
                          scalar1=rsum_sb[:, 2:3], scalar2=None, op0=ALU.mult, inc=[(ve, 1)])
                    wsel_done = ve.n
                    kb.wait(S, ve, wsel_done)
                    kb.op(S, "dma_start", out=wa_d[s, :, t * E:(t + 1) * E], in_=wm3_sb[pr],
                          inc=[(qWM[pr], 16)])
                    kb.op(V, "tensor_tensor", out=eall_sb, in0=selt_sb[pr], in1=c64p1_sb[:],
                          op=ALU.mult, inc=[(ve, 1)])
                    kb.op(V, "max", out=ed8_sb, in_=eall_sb, inc=[(ve, 1)])
                    kb.op(V, "tensor_scalar", out=edesc_sb[:, g * 8:(g + 1) * 8], in0=ed8_sb,
                          scalar1=-1.0, scalar2=None, op0=ALU.add, inc=[(ve, 1)])
                    kb.wait(T, ve, ve.n)
                    kb.op(T, "matmul", ps4[:, ccol - 256:ccol - 192], ustrict_sb[:], selt_sb[pr],
                          start=True, stop=True, inc=[(mm, 1)])
                    kb.op(T, "matmul", ps4[0:1, tcol - 256:tcol - 192], onec_bf_sb[:], selt_sb[pr],
                          start=True, stop=True, inc=[(mm, 1)])
                    cs_mm = mm.n
                    kb.wait(S, mm, cs_mm)
                    kb.wait(S, qCL[pr], qCL[pr].n)
                    kb.op(S, "copy", clt_sb[pr], ps4[:, ccol - 256:ccol - 192], inc=[(se, 1)])
                    kb.op(S, "copy", tot_sb[0:1, pr * 64:pr * 64 + 64],
                          ps4[0:1, tcol - 256:tcol - 192], inc=[(se, 1)])
                    ev_c = se.n
                    kb.wait(V, se, ev_c)
                    kb.op(V, "tensor_copy", out=rowbase_sb[0:1, g * E:(g + 1) * E],
                          in_=cnt1_sb[0:1, s * E:(s + 1) * E], inc=[(ve, 1)])
                    kb.op(V, "tensor_tensor", out=cnt1_sb[0:1, s * E:(s + 1) * E],
                          in0=cnt1_sb[0:1, s * E:(s + 1) * E], in1=tot_sb[0:1, pr * 64:pr * 64 + 64],
                          op=ALU.add, inc=[(ve, 1)])
                    kb.wait(S, ve, ve.n)
                    kb.op(S, "dma_start", out=cl_d[s, :, t * E:(t + 1) * E], in_=clt_sb[pr],
                          inc=[(qCL[pr], 16)])

        if max_phase >= 4:
            kb.wait(G, mm, mm_p1)
            for xh in range(2):
                kb.op(G, "dma_start",
                      out=sguv.rearrange("p (c x f) -> p c x f", x=2, f=2048)[:, :, xh, :],
                      in_=sgu_w[:, xh * 2048:(xh + 1) * 2048].rearrange("(c p) f -> p c f", p=128),
                      inc=[(qW5, 16)])
            kb.op(G, "dma_start", out=wv(65536, 16384, BF16).rearrange("p (c f) -> p c f", f=SL),
                  in_=sdown_w[:, 0:SL].rearrange("(c p) f -> p c f", p=128), inc=[(qW5, 16)])
            kb.op(G, "dma_start", out=wv(81920, 16384, BF16).rearrange("p (c f) -> p c f", f=SL),
                  in_=sdown_w[:, SL:D].rearrange("(c p) f -> p c f", p=128), inc=[(qW5, 16)])
            sd0_sb = wv(65536, 16384, BF16)
            sd1_sb = wv(81920, 16384, BF16)
            p5w_ld = qW5.n

            kb.wait(S, ve, ve.n)
            kb.op(S, "dma_start", out=cnt_in.rearrange("s e -> (s e)").unsqueeze(0),
                  in_=cnt1_sb[0:1, :], inc=[(qS, 16)])
            kb.wait(G, qS, qS.n)
            kb.op(G, "collective_compute", "AllGather", ALU.bypass,
                  replica_groups=[list(range(NCORES))],
                  ins=[cnt_in.opt()], outs=[cnt_out.opt()], inc=[(qCcnt, 1)])
            kb.op(G, "collective_compute", "AllGather", ALU.bypass,
                  replica_groups=[list(range(NCORES))],
                  ins=[x2l_d.opt()], outs=[x2a_d.opt()], inc=[(qCx2, 1)])
            kb.wait(Y, qCcnt, qCcnt.n)
            kb.wait(Y, mm, mm.n)
            kb.wait(Y, ve, ve.n)
            kb.wait(Y, se, se.n)
            for q_ in (qCL[0], qCL[1], qWM[0], qWM[1], qX2[0], qX2[1]):
                kb.wait(Y, q_, q_.n)
            kb.op(Y, "dma_start", out=cnts_sb[0:64, 0:64], in_=cnt_out[:, :], inc=[(qL, 16)])
            ld_cnt = qL.n
            kb.wait(T, qL, ld_cnt)
            for s_ in range(NSEG):
                kb.op(T, "matmul", ps2[0:1, s_ * 64:(s_ + 1) * 64],
                      maskcols_sb[:, s_:s_ + 1], cnts_sb[0:64, 0:64],
                      start=True, stop=True, inc=[(mm, 1)])
            base_mm = mm.n
            kb.wait(V, mm, base_mm)
            for s_ in range(NSEG):
                kb.op(V, "tensor_tensor",
                      out=rowbase_sb[0:1, s_ * 4 * E:(s_ + 1) * 4 * E].rearrange("o (t e) -> o t e", e=E),
                      in0=rowbase_sb[0:1, s_ * 4 * E:(s_ + 1) * 4 * E].rearrange("o (t e) -> o t e", e=E),
                      in1=ps2[0:1, s_ * 64:(s_ + 1) * 64].unsqueeze(1).broadcast_to([1, 4, 64]),
                      op=ALU.add, inc=[(ve, 1)])
            rb_done = ve.n
            kb.wait(T, ve, rb_done)

            smk = {"n": 0}
            posh_guard = [0] * NPOSH
            hl_guard = [0] * NHL

            def p4_tile(g):
                s_, t = g // 4, g % 4
                par = g % NHL
                if t == 0:
                    kb.wait(Y, ve, ve.n)
                    kb.op(Y, "dma_start", out=clseg_sb, in_=cl_d[s_], inc=[(qL, 16)])
                    kb.op(Y, "dma_start", out=waseg_sb, in_=wa_d[s_], inc=[(qL, 16)])
                    kb.wait(V, qL, qL.n)
                kb.wait(T, ve, ve.n)
                kb.op(T, "matmul", ps2[:, 0:64], ones_f_sb[:],
                      rowbase_sb[0:1, g * E:(g + 1) * E], start=True, stop=True, inc=[(mm, 1)])
                kb.wait(V, mm, mm.n)
                kb.op(V, "tensor_tensor", out=cg_sb, in0=clseg_sb[:, t * E:(t + 1) * E],
                      in1=ps2[:, 0:64], op=ALU.add, inc=[(ve, 1)])
                kb.wait(T, ve, ve.n)
                kb.op(V, "tensor_tensor",
                      out=mask_sb.rearrange("p (k e) -> p k e", e=E),
                      in0=c64_sb[:].unsqueeze(1).broadcast_to([128, 8, E]),
                      in1=edesc_sb[:, g * 8:(g + 1) * 8].unsqueeze(2).broadcast_to([128, 8, E]),
                      op=ALU.is_equal, inc=[(ve, 1)])
                kb.op(V, "tensor_tensor",
                      out=wmp_sb.rearrange("p (k e) -> p k e", e=E),
                      in0=mask_sb.rearrange("p (k e) -> p k e", e=E),
                      in1=waseg_sb[:, t * E:(t + 1) * E].unsqueeze(1).broadcast_to([128, 8, E]),
                      op=ALU.mult, inc=[(ve, 1)])
                kb.wait(V, mm, hl_guard[par])
                kb.op(V, "tensor_scalar",
                      out=mhl_sb[par].rearrange("p (k x e) -> p k x e", x=2, e=E)[:, :, 0, :],
                      in0=mask_sb.rearrange("p (k e) -> p k e", e=E),
                      scalar1=tok_hi_sb[:, g:g + 1], scalar2=None, op0=ALU.mult, inc=[(ve, 1)])
                kb.op(V, "tensor_scalar",
                      out=mhl_sb[par].rearrange("p (k x e) -> p k x e", x=2, e=E)[:, :, 1, :],
                      in0=mask_sb.rearrange("p (k e) -> p k e", e=E),
                      scalar1=tok_lo_sb[:, g:g + 1], scalar2=None, op0=ALU.mult, inc=[(ve, 1)])
                kb.op(V, "tensor_copy",
                      out=whl_sb[par].rearrange("p (k x e) -> p k x e", x=2, e=E)[:, :, 0, :],
                      in_=wmp_sb.rearrange("p (k e) -> p k e", e=E), inc=[(ve, 1)])
                kb.op(V, "tensor_tensor",
                      out=whl_sb[par].rearrange("p (k x e) -> p k x e", x=2, e=E)[:, :, 1, :],
                      in0=wmp_sb.rearrange("p (k e) -> p k e", e=E),
                      in1=whl_sb[par].rearrange("p (k x e) -> p k x e", x=2, e=E)[:, :, 0, :],
                      op=ALU.subtract, inc=[(ve, 1)])
                for k in range(8):
                    kb.op(V, "scalar_tensor_tensor", out=junk_sb,
                          in0=mask_sb[:, k * E:(k + 1) * E], scalar=1.0, in1=cg_sb,
                          op0=ALU.mult, op1=ALU.mult,
                          accum_out=pos_sb[:, g * 8 + k: g * 8 + k + 1], inc=[(ve, 1)])
                for k in range(8):
                    pp = (g * 8 + k) % NPOSH
                    kb.wait(V, mm, posh_guard[pp])
                    kb.op(V, "tensor_scalar", out=posh_sb[pp],
                          in0=c512_sb[:], scalar1=pos_sb[:, g * 8 + k: g * 8 + k + 1],
                          scalar2=None, op0=ALU.is_equal, inc=[(ve, 1)])
                    kb.wait(T, ve, ve.n)
                    first = smk["n"] == 0
                    last = (g == 31 and k == 7)
                    kb.op(T, "matmul", ps3[:], mhl_sb[par][:, k * 128:(k + 1) * 128],
                          posh_sb[pp], start=first, stop=last, skip_group_check=True)
                    kb.op(T, "matmul", ps4[:], whl_sb[par][:, k * 128:(k + 1) * 128],
                          posh_sb[pp], start=first, stop=last, skip_group_check=True,
                          inc=[(mm, 1)])
                    posh_guard[pp] = mm.n
                    smk["n"] += 1
                hl_guard[par] = mm.n

            def p5_seg(s_):
                kb.wait(Y, mm, mm.n)
                kb.wait(Y, ve, ve.n)
                kb.wait(Y, se, se.n)
                kb.wait(Y, qS, qS.n)
                kb.op(Y, "dma_start", out=p5xlt_sb, in_=xlt2_d[s_], inc=[(qL, 16)])
                ld_xlt = qL.n
                if s_ == 0:
                    kb.wait(Y, qE[0], qE[0].n)
                    kb.wait(Y, qE[1], qE[1].n)
                    kb.wait(T, qW5, p5w_ld)
                kb.wait(T, qL, ld_xlt)
                for fc in range(16):
                    for c in range(NCH):
                        kb.op(T, "matmul", ps0[:],
                              sguv[:, c * 4096 + fc * 128: c * 4096 + (fc + 1) * 128],
                              p5xlt_sb[:, c * SL:(c + 1) * SL],
                              start=(c == 0), stop=(c == NCH - 1))
                    for c in range(NCH):
                        kb.op(T, "matmul", ps1[:],
                              sguv[:, c * 4096 + (fc + 16) * 128: c * 4096 + (fc + 17) * 128],
                              p5xlt_sb[:, c * SL:(c + 1) * SL],
                              start=(c == 0), stop=(c == NCH - 1),
                              inc=([(mm, 1)] if c == NCH - 1 else None))
                    gu_mm = mm.n
                    kb.wait(S, mm, gu_mm)
                    kb.op(S, "activation", out=silu_sb, in_=ps0[:], func=AF.Silu,
                          bias=0.0, scale=1.0, inc=[(se, 1)])
                    kb.wait(V, se, se.n)
                    kb.op(V, "tensor_tensor", out=h_sb[:, fc * SL:(fc + 1) * SL],
                          in0=silu_sb, in1=ps1[:], op=ALU.mult, inc=[(ve, 1)])
                    kb.wait(T, ve, ve.n)
                for t in range(NT):
                    g = s_ * 4 + t
                    for half, sdw, pso in ((0, sd0_sb, ps0), (1, sd1_sb, ps1)):
                        for c in range(16):
                            kb.op(T, "matmul", pso[:],
                                  h_sb[:, c * SL + t * 128: c * SL + (t + 1) * 128],
                                  sdw[:, c * SL:(c + 1) * SL],
                                  start=(c == 0), stop=(c == 15),
                                  inc=([(mm, 1)] if c == 15 else None))
                    dn_mm = mm.n
                    kb.wait(Y, ve, ve.n)
                    kb.op(Y, "dma_start", out=p5x_sb, in_=x2_d[s_, t * 128:(t + 1) * 128, :],
                          inc=[(qL, 16)])
                    kb.wait(V, qL, qL.n)
                    kb.wait(V, mm, dn_mm)
                    kb.op(V, "scalar_tensor_tensor", out=p5s_sb[:, 0:SL], in0=ps0[:],
                          scalar=sg_sb[:, g:g + 1], in1=p5x_sb[:, 0:SL],
                          op0=ALU.mult, op1=ALU.add, inc=[(ve, 1)])
                    kb.op(V, "scalar_tensor_tensor", out=p5s_sb[:, SL:2 * SL], in0=ps1[:],
                          scalar=sg_sb[:, g:g + 1], in1=p5x_sb[:, SL:D],
                          op0=ALU.mult, op1=ALU.add, inc=[(ve, 1)])
                    x2s_done = ve.n
                    kb.wait(S, ve, x2s_done)
                    kb.op(S, "dma_start", out=x2s_d[s_, t * 128:(t + 1) * 128, :],
                          in_=p5s_sb, inc=[(qXS[s_ // 2], 16)])
                    kb.wait(T, ve, x2s_done)
                    kb.wait(V, qXS[s_ // 2], qXS[s_ // 2].n)

            for s_ in range(NSEG):
                for t in range(NT):
                    p4_tile(s_ * 4 + t)
            sm_mm = mm.n

            kb.wait(V, mm, sm_mm)
            kb.op(V, "tensor_scalar", out=eh_sb[0], in0=edesc_sb[:], scalar1=8.0, scalar2=None,
                  op0=ALU.is_ge, inc=[(ve, 1)])
            for i, thr in enumerate((16.0, 24.0, 32.0, 40.0, 48.0, 56.0)):
                kb.op(V, "scalar_tensor_tensor", out=eh_sb[(i + 1) % 2], in0=edesc_sb[:],
                      scalar=thr, in1=eh_sb[i % 2], op0=ALU.is_ge, op1=ALU.add, inc=[(ve, 1)])
            ehf = eh_sb[0]
            kb.op(V, "scalar_tensor_tensor", out=a1_sb, in0=edesc_sb[:], scalar=512.0,
                  in1=ehf, op0=ALU.mult, op1=ALU.add, inc=[(ve, 1)])
            kb.op(V, "tensor_tensor", out=a1_sb, in0=a1_sb, in1=pos_sb[:], op=ALU.add, inc=[(ve, 1)])
            kb.op(V, "tensor_scalar", out=alive_sb, in0=pos_sb[:], scalar1=512.0, scalar2=None,
                  op0=ALU.is_lt, inc=[(ve, 1)])
            kb.op(V, "tensor_scalar", out=a1_sb, in0=a1_sb, scalar1=zrow_sb[:, 0:1], scalar2=None,
                  op0=ALU.subtract, inc=[(ve, 1)])
            kb.op(V, "tensor_tensor", out=a1_sb, in0=a1_sb, in1=alive_sb, op=ALU.mult, inc=[(ve, 1)])
            kb.op(V, "tensor_scalar", out=addrf_sb[:], in0=a1_sb, scalar1=zrow_sb[:, 0:1],
                  scalar2=None, op0=ALU.add, inc=[(ve, 1)])
            kb.op(V, "tensor_copy", out=addru_sb[:], in_=addrf_sb[:], inc=[(ve, 1)])
            if dev:
                kb.wait(S, ve, ve.n)
                kb.op(S, "dma_start", out=dbg_d[:, 32:288], in_=edesc_sb[:], inc=[(qS, 16)])
                kb.op(S, "dma_start", out=dbg_d[:, 288:544], in_=pos_sb[:], inc=[(qS, 16)])
                kb.op(S, "dma_start", out=dbg_d[:, 544:800], in_=addrf_sb[:], inc=[(qS, 16)])

            kb.wait(S, mm, sm_mm)
            kb.op(S, "copy", smhlb_sb2[:, 0:SL], ps3[:], inc=[(se, 1)])
            kb.op(S, "copy", smhlb_sb2[:, SL:2 * SL], ps4[:], inc=[(se, 1)])
            hl_ev = se.n
            kb.wait(T, se, hl_ev)
            kb.op(T, "matmul", ps2[0:64, 0:SL], comb_sb[:, 0:64], smhlb_sb2[:, 0:SL],
                  start=True, stop=True, inc=[(mm, 1)])
            kb.wait(S, mm, mm.n)
            kb.op(S, "copy", smv_sb[0:64, :], ps2[0:64, 0:SL], inc=[(se, 1)])
            kb.wait(T, se, se.n)
            kb.op(T, "matmul", ps2[0:64, 0:SL], comb_sb[:, 64:128], smhlb_sb2[:, SL:2 * SL],
                  start=True, stop=True, inc=[(mm, 1)])
            kb.wait(S, mm, mm.n)
            kb.op(S, "copy", wv_sb[0:64, :], ps2[0:64, 0:SL], inc=[(se, 1)])
            smw_done = se.n
            kb.wait(S, se, smw_done)
            for r in range(NCORES):
                kb.op(S, "dma_start", out=smw_in[16 * r:16 * r + 8, :],
                      in_=smv_sb[8 * r:8 * r + 8, :], inc=[(qS, 16)])
                kb.op(S, "dma_start", out=smw_in[16 * r + 8:16 * r + 16, :],
                      in_=wv_sb[8 * r:8 * r + 8, :], inc=[(qS, 16)])
            kb.wait(G, qS, qS.n)
            kb.op(G, "collective_compute", "ReduceScatter", ALU.add,
                  replica_groups=[list(range(NCORES))],
                  ins=[smw_in.opt()], outs=[smw_out.opt()], inc=[(qCsm, 1)])
            kb.wait(Y, qCsm, qCsm.n)
            kb.op(Y, "dma_start", out=smloc_sb[0:16, :], in_=smw_out[:, :], inc=[(qL, 16)])
            kb.wait(T, qL, qL.n)
            for t in range(NT):
                kb.op(T, "transpose", ps2[:, t * 16:(t + 1) * 16],
                      smloc_sb[0:16, t * 128:(t + 1) * 128], identf_sb[0:16, 0:16],
                      inc=([(mm, 1)] if t == NT - 1 else None))
            kb.wait(V, mm, mm.n)
            for t in range(NT):
                kb.op(V, "tensor_scalar", out=smT_sb[:, t * 8:(t + 1) * 8],
                      in0=ps2[:, t * 16: t * 16 + 8], scalar1=-1.0, scalar2=0.0,
                      op0=ALU.add, op1=ALU.max, inc=[(ve, 1)])
                kb.op(V, "tensor_copy", out=wt_sb[:, t * 8:(t + 1) * 8],
                      in_=ps2[:, t * 16 + 8: t * 16 + 16], inc=[(ve, 1)])
            kb.op(V, "tensor_copy", out=smTu_sb[:, 0:32], in_=smT_sb[:, 0:32], inc=[(ve, 1)])
            p4_done = ve.n
            if dev:
                kb.wait(S, ve, p4_done)
                kb.op(S, "dma_start", out=dbg_d[:, 800:832], in_=smT_sb[:, 0:32], inc=[(qS, 16)])
                kb.op(S, "dma_start", out=dbg_d[:, 832:864], in_=wt_sb[:, 0:32], inc=[(qS, 16)])

        if max_phase >= 6:
            kb.wait(V, qS, qS.n)
            kb.op(V, "memset", zero_sb[0:1, :], 0.0, inc=[(ve, 1)])
            kb.wait(S, ve, ve.n)
            kb.op(S, "dma_start", out=x4l_d[X4ROWS - 1: X4ROWS, :], in_=zero_sb[0:1, :],
                  inc=[(qS, 16)])
            kb.wait(S, qS, qS.n)
            kb.wait(G, ve, ve.n)
            kb.wait(G, qCx2, qCx2.n)
            kb.wait(G, qS, qS.n)
            kb.wait(G, qG, qG.n)

            def load_exp(e, buf):
                kb.op(G, "dma_start", out=egu_sb[buf].rearrange("p (c f) -> p c f", f=D),
                      in_=egu_w[e].rearrange("(c p) f -> p c f", p=128), inc=[(qE[buf], 16)])
                kb.op(G, "dma_start", out=edn_sb[buf].rearrange("p (c f) -> p c f", f=D),
                      in_=edown_w[e].rearrange("(c p) f -> p c f", p=128), inc=[(qE[buf], 16)])

            kb.wait(G, mm, mm.n)
            load_exp(0, 0)
            for e in range(EL):
                buf = e % 2
                w_ld = qE[buf].n
                if e + 1 < EL:
                    kb.wait(G, tE, max(0, e))
                    load_exp(e + 1, (e + 1) % 2)
                for t in range(NT):
                    kb.wait(G, mm, mm.n)
                    kb.op(G, "indirect_dma_start", out=rows_sb, out_offset=None,
                          in_=x2a_d, in_offset=IOA(smTu_sb[:, t * 8 + e: t * 8 + e + 1], 0),
                          inc=[(qG, 16)])
                    kb.wait(T, qG, qG.n)
                    for c in range(NCH):
                        kb.op(T, "transpose", ps_tr[:, (c % 4) * 128:(c % 4 + 1) * 128],
                              rows_sb[:, c * 128:(c + 1) * 128], ident_sb[:],
                              inc=[(mm, 1)] if (c % 4 == 3) else None)
                        if c % 4 == 3:
                            kb.wait(V, mm, mm.n)
                            for cc in range(c - 3, c + 1):
                                kb.op(V, "tensor_copy",
                                      out=rowsT_sb[:, cc * SL + t * 128: cc * SL + (t + 1) * 128],
                                      in_=ps_tr[:, (cc % 4) * 128:(cc % 4 + 1) * 128],
                                      inc=([(ve, 1)] if cc % 4 == 3 else None))
                            kb.wait(T, ve, ve.n)
                rows_done = ve.n
                kb.wait(T, qE[buf], w_ld)
                if e == 0:
                    kb.wait(T, se, se.n)
                for fc in range(4):
                    for c in range(NCH):
                        kb.op(T, "matmul", ps3[:],
                              egu_sb[buf][:, c * D + fc * 128: c * D + (fc + 1) * 128],
                              rowsT_sb[:, c * SL:(c + 1) * SL],
                              start=(c == 0), stop=(c == NCH - 1))
                    for c in range(NCH):
                        kb.op(T, "matmul", ps4[:],
                              egu_sb[buf][:, c * D + (fc + 4) * 128: c * D + (fc + 5) * 128],
                              rowsT_sb[:, c * SL:(c + 1) * SL],
                              start=(c == 0), stop=(c == NCH - 1),
                              inc=([(mm, 1)] if c == NCH - 1 else None))
                    kb.wait(S, mm, mm.n)
                    kb.op(S, "activation", out=silu6_sb, in_=ps3[:], func=AF.Silu,
                          bias=0.0, scale=1.0, inc=[(se, 1)])
                    kb.wait(V, se, se.n)
                    kb.op(V, "tensor_tensor", out=he_sb[:, fc * SL:(fc + 1) * SL],
                          in0=silu6_sb, in1=ps4[:], op=ALU.mult, inc=[(ve, 1)])
                    kb.wait(T, ve, ve.n)
                for t in range(NT):
                    for half, pso in ((0, ps3), (1, ps4)):
                        for c in range(4):
                            kb.op(T, "matmul", pso[:],
                                  he_sb[:, c * SL + t * 128: c * SL + (t + 1) * 128],
                                  edn_sb[buf][:, c * D + half * SL: c * D + half * SL + SL],
                                  start=(c == 0), stop=(c == 3),
                                  inc=([(mm, 1)] if c == 3 else None))
                    dn_mm = mm.n
                    kb.wait(V, mm, dn_mm)
                    kb.wait(V, qS, qS.n)
                    kb.op(V, "tensor_scalar", out=oute_sb[:, 0:SL], in0=ps3[:],
                          scalar1=wt_sb[:, t * 8 + e: t * 8 + e + 1], scalar2=None,
                          op0=ALU.mult, inc=[(ve, 1)])
                    kb.op(V, "tensor_scalar", out=oute_sb[:, SL:2 * SL], in0=ps4[:],
                          scalar1=wt_sb[:, t * 8 + e: t * 8 + e + 1], scalar2=None,
                          op0=ALU.mult, inc=[(ve, 1)])
                    kb.wait(S, ve, ve.n)
                    kb.op(S, "dma_start", out=x4l_d[e * CAP + t * 128: e * CAP + (t + 1) * 128, :],
                          in_=oute_sb, inc=[(qS, 16)])
                    kb.wait(T, ve, ve.n)
                tE.inc(1)
                kb.emit(T, lambda eng, s_=tE: eng.sem_inc(s_.h, 1))
            kb.wait(G, qS, qS.n)
            kb.wait(G, qG, qG.n)
            kb.op(G, "collective_compute", "AllGather", ALU.bypass,
                  replica_groups=[list(range(NCORES))],
                  ins=[x4l_d.opt()], outs=[x4a_d.opt()], inc=[(qCx4, 1)])
            p6_mm = mm.n
            p6_ve = ve.n
            p6_se = se.n

            for s_ in range(NSEG):
                p5_seg(s_)

        if max_phase >= 7:
            kb.wait(G, qCx4, qCx4.n)
            kb.wait(G, ve, p6_ve)
            kb.wait(G, se, p6_se)
            for grp in range(4):
                tiles = list(range(grp * 8, (grp + 1) * 8))
                kb.wait(Y, mm, p6_mm)
                kb.wait(Y, ve, p6_ve)
                kb.wait(Y, qE[0], qE[0].n)
                kb.wait(Y, qE[1], qE[1].n)
                kb.wait(Y, qG, qG.n)
                kb.wait(Y, qXS[grp], qXS[grp].n)
                kb.wait(Y, qS, qS.n)
                for i, gt in enumerate(tiles):
                    s, t = gt // 4, gt % 4
                    kb.op(Y, "dma_start", out=fin_sb[i], in_=x2s_d[s, t * 128:(t + 1) * 128, :],
                          inc=[(qL, 16)])
                kb.wait(G, qL, qL.n)
                for k in range(8):
                    if k > 0:
                        kb.wait(G, qG, qG.n)
                    for i, gt in enumerate(tiles):
                        kb.op(G, "indirect_dma_start", out=fin_sb[i], out_offset=None,
                              in_=x4a_d, in_offset=IOA(addru_sb[:, gt * 8 + k: gt * 8 + k + 1], 0),
                              compute_op=ALU.add, inc=[(qG, 16)])
                kb.wait(Y, qG, qG.n)
                for i, gt in enumerate(tiles):
                    kb.op(Y, "dma_start", out=out_d[gt * 128:(gt + 1) * 128, :], in_=fin_sb[i],
                          inc=[(qS, 16)])

        kb.wait(Y, qS, qS.n)
        kb.wait(Y, qG, qG.n)
        for _s in (qCkv, qCcnt, qCx2, qCsm, qCx4, qE[0], qE[1], qW5,
                   qX2[0], qX2[1], qCL[0], qCL[1], qWM[0], qWM[1],
                   qXS[0], qXS[1], qXS[2], qXS[3]):
            kb.wait(Y, _s, _s.n)
        kb.wait(Y, mm, mm.n)
        kb.wait(Y, ve, ve.n)
        kb.wait(Y, se, se.n)
        kb.wait(Y, qL, qL.n)

        with nc.Block() as block:
            @block.tensor
            def _(eng):
                for en, fn in kb.ops:
                    if en == T:
                        fn(eng)

            @block.vector
            def _(eng):
                for en, fn in kb.ops:
                    if en == V:
                        fn(eng)

            @block.scalar
            def _(eng):
                for en, fn in kb.ops:
                    if en == S:
                        fn(eng)

            @block.gpsimd
            def _(eng):
                for en, fn in kb.ops:
                    if en == G:
                        fn(eng)

            @block.sync
            def _(eng):
                for en, fn in kb.ops:
                    if en == Y:
                        fn(eng)

    return nc


def _perm_evenodd():
    q = np.arange(1024)
    p = np.concatenate([q[0::2], q[1::2]])
    return np.concatenate([p, 1024 + p])


def prepare_inputs(inputs):
    x = np.asarray(inputs["L0_attn_x"])
    wqk = np.asarray(inputs["L0_attn_qk_w"])
    bqk = np.asarray(inputs["L0_attn_qk_b"])
    lepe_w = np.asarray(inputs["L0_attn_lepe_w"])
    lepe_b = np.asarray(inputs["L0_attn_lepe_b"])
    gate_w = np.asarray(inputs["L0_ffn_gate_w"])
    sg_w = np.asarray(inputs["L0_ffn_shared_gate_w"])
    sgu_w = np.asarray(inputs["L0_ffn_shared_gate_up_w"])
    sdown_w = np.asarray(inputs["L0_ffn_shared_down_w"])
    egu_w = np.asarray(inputs["L0_ffn_expert_gate_up_w"])
    edown_w = np.asarray(inputs["L0_ffn_expert_down_w"])

    perm = _perm_evenodd()
    wqk_p = np.ascontiguousarray(wqk[:, perm]).astype(np.float32)
    bqk_p = bqk[perm]
    bqk_cols = np.ascontiguousarray(bqk_p.reshape(16, 128).T).astype(np.float32)

    gate_ext = np.zeros((D, 72), np.float32)
    gate_ext[:, :64] = gate_w
    gate_ext[:, 64:65] = sg_w

    theta = 1.0 / (10000.0 ** (np.arange(512, dtype=np.float32) / 512.0))
    c64c = np.tile(np.arange(E, dtype=np.float32), (128, 1))
    c512c = np.tile(np.arange(CAP, dtype=np.float32), (128, 1))
    ustrictc = np.triu(np.ones((128, 128), np.float32), 1).astype(BF)
    identc = np.eye(128, dtype=BF)
    identfc = np.eye(128, dtype=np.float32)

    in_maps = []
    for c in range(NCORES):
        segs = [(b, c + 8 * h) for b in range(B) for h in range(2)]
        x_shard = np.stack([x[b, 512 * j:512 * (j + 1), :] for (b, j) in segs])
        halo = np.zeros((NSEG, 2, D), np.float32)
        for si, (b, j) in enumerate(segs):
            if j > 0:
                halo[si, 0] = x[b, 512 * j - 1]
            if j < 15:
                halo[si, 1] = x[b, 512 * (j + 1)]
        cost = np.zeros((NSEG, 4, 128, SL), np.float32)
        sint = np.zeros((NSEG, 4, 128, SL), np.float32)
        for si, (b, j) in enumerate(segs):
            pos = np.arange(512 * j, 512 * (j + 1), dtype=np.float32)
            ang = pos[None, :] * theta[:, None]
            cost[si] = np.cos(ang).reshape(4, 128, SL)
            sint[si] = np.sin(ang).reshape(4, 128, SL)

        loc = np.arange(NSEG * SL).reshape(32, 128).T
        v = (c * NSEG * SL + loc + 1).astype(np.float32)
        tok_hic = np.floor(v / 256.0).astype(np.float32)
        tok_loc = (v - 256.0 * tok_hic).astype(np.float32)

        maskcol = np.zeros((E, 8), np.float32)
        for s_own in range(NSEG):
            b_own, h_own = s_own // 2, s_own % 2
            g_own = 16 * b_own + c + 8 * h_own
            for r in range(NCORES):
                for l in range(8):
                    bb, hh = l // 2, l % 2
                    g = 16 * bb + r + 8 * hh
                    if g < g_own:
                        maskcol[r * 8 + l, s_own] = 1.0

        comb = np.zeros((128, 128), np.float32)
        for e in range(64):
            comb[e, e] = 256.0
            comb[e + 64, e] = 1.0
            comb[e, 64 + e] = 1.0
            comb[e + 64, 64 + e] = 1.0
        in_maps.append(dict(
            x_shard=x_shard, x_halo=halo, comb_bf=comb.astype(BF),
            wqk_p=wqk_p, bqk_cols=bqk_cols,
            lepe_w=lepe_w.astype(np.float32), lepe_b=lepe_b.reshape(1, D).astype(np.float32),
            costab=cost.astype(BF), sintab=sint.astype(BF),
            gate_ext=gate_ext, sgu_w=sgu_w.astype(np.float32),
            sdown_w=sdown_w.astype(np.float32),
            egu_w=np.ascontiguousarray(egu_w[8 * c:8 * c + 8]).astype(np.float32),
            edown_w=np.ascontiguousarray(edown_w[8 * c:8 * c + 8]).astype(np.float32),
            c64=c64c, c64p1=c64c + 1.0, c512=c512c,
            ustrict=ustrictc, ident_bf=identc, ident_f=identfc,
            ones_bf=np.ones((1, 128), BF), onec_bf=np.ones((128, 1), BF),
            ones_f=np.ones((1, 128), np.float32),
            tok_hi=tok_hic, tok_lo=tok_loc, maskcols=maskcol,
            zrow_col=np.full((128, 1), float(c * X4ROWS + X4ROWS - 1), np.float32),
            eps_col=np.full((128, 1), 1e-5, np.float32),
        ))
    return in_maps


def assemble_output(results):
    out = np.zeros((B, N, D), np.float32)
    for c in range(NCORES):
        r = results[c]["out_d"].reshape(NSEG, SL, D)
        segs = [(b, c + 8 * h) for b in range(B) for h in range(2)]
        for si, (b, j) in enumerate(segs):
            out[b, 512 * j:512 * (j + 1), :] = r[si]
    return out


def kernel(**inputs):
    in_maps = prepare_inputs(inputs)
    nc = build_program()
    res = run_bass_kernel_spmd(nc, in_maps, core_ids=list(range(NCORES)))
    return assemble_output(res.results)


# version 12
# speedup vs baseline: 1.4814x; 1.0199x over previous
import numpy as np
import ml_dtypes
from contextlib import ExitStack

import concourse.bass as bass
import concourse.mybir as mybir
from concourse.bass_utils import run_bass_kernel_spmd

F32 = mybir.dt.float32
BF16 = mybir.dt.bfloat16
U8 = mybir.dt.uint8
U32 = mybir.dt.uint32
AF = mybir.ActivationFunctionType
ALU = mybir.AluOpType
AX = mybir.AxisListType
IOA = bass.IndirectOffsetOnAxis
BF = ml_dtypes.bfloat16

NCORES = 8
B, N, D = 4, 8192, 1024
NSEG, SL, NT = 8, 512, 4
NCH, NH, HD = 8, 8, 128
E, EL, CAP = 64, 8, 512
ISQN = float(1.0 / np.sqrt(np.float32(N)))
X4ROWS = EL * CAP + 1
MAX_PHASE = 7

T, V, S, G, Y = "tensor", "vector", "scalar", "gpsimd", "sync"


class Sem:
    def __init__(self, name, dma=False):
        self.name, self.n, self.h = name, 0, None
        self.dma = dma
        self.max_waited = 0
        self.last_sync = 0

    def inc(self, k=1):
        self.n += k
        return self.n


class KB:
    def __init__(self):
        self.ops = []
        self.auto = {}

    def emit(self, en, fn):
        self.ops.append((en, fn))

    def op(self, en, meth, *args, inc=None, **kw):
        aset = self.auto.get(en)
        if aset is not None:
            v = aset.n
            if v:
                self.emit(en, lambda eng, s=aset, v=v: eng.wait_ge(s.h, v))
            if meth != "dma_start":
                inc = list(inc) if inc else []
                if not any(sem is aset for sem, _ in inc):
                    inc.append((aset, 1))
        if inc:
            for sem, k in inc:
                if sem.dma and sem.max_waited > sem.last_sync:
                    v = sem.max_waited
                    sem.last_sync = v
                    self.emit(en, lambda eng, s=sem, v=v: eng.wait_ge(s.h, v))
        if inc:
            for sem, k in inc:
                sem.inc(k)

        def fn(eng, meth=meth, args=args, kw=kw, inc=inc):
            inst = getattr(eng, meth)(*args, **kw)
            if inc:
                for sem, k in inc:
                    inst = inst.then_inc(sem.h, k)
        self.emit(en, fn)

    def wait(self, en, sem, val=None):
        v = sem.n if val is None else val
        sem.max_waited = max(sem.max_waited, v)
        self.emit(en, lambda eng, s=sem, v=v: eng.wait_ge(s.h, v))


def build_program(max_phase=MAX_PHASE, dev=False, fake_cc=False):
    nc = bass.Bass("TRN2", target_bir_lowering=False)
    kb = KB()

    def din(name, shape, dt=F32):
        return nc.dram_tensor(name, shape, dt, kind="ExternalInput").ap()

    def dscr(name, shape, dt=F32):
        return nc.dram_tensor(name, shape, dt, kind="ExternalOutput" if dev else "Internal").ap()

    def dint(name, shape, dt=F32):
        return nc.dram_tensor(name, shape, dt, kind="Internal").ap()

    x_shard = din("x_shard", [NSEG, SL, D])
    x_halo = din("x_halo", [NSEG, 2, D])
    wqk_p = din("wqk_p", [D, 2 * D])
    bqk_cols = din("bqk_cols", [128, 16])
    lepe_w = din("lepe_w", [3, D, D])
    lepe_b = din("lepe_b", [1, D])
    costab = din("costab", [NSEG, 4, 128, SL], BF16)
    sintab = din("sintab", [NSEG, 4, 128, SL], BF16)
    gate_ext = din("gate_ext", [D, 72])
    sgu_w = din("sgu_w", [D, 4096])
    sdown_w = din("sdown_w", [2048, D])
    egu_w = din("egu_w", [EL, D, D])
    edown_w = din("edown_w", [EL, 512, D])
    c64 = din("c64", [128, E])
    c64p1 = din("c64p1", [128, E])
    c512 = din("c512", [128, CAP])
    ustrict = din("ustrict", [128, 128], BF16)
    ident_bf = din("ident_bf", [128, 128], BF16)
    ident_f = din("ident_f", [128, 128])
    ones_bf = din("ones_bf", [1, 128], BF16)
    onec_bf = din("onec_bf", [128, 1], BF16)
    ones_f = din("ones_f", [1, 128])
    tok_hi = din("tok_hi", [128, 32])
    tok_lo = din("tok_lo", [128, 32])
    maskcols = din("maskcols", [E, 8])
    comb_bf = din("comb_bf", [128, 128], BF16)
    zrow_col = din("zrow_col", [128, 1])
    eps_col = din("eps_col", [128, 1])

    out_d = nc.dram_tensor("out_d", [NSEG * SL, D], F32, kind="ExternalOutput").ap()

    qel_d = dscr("qel_d", [NSEG, 128, NCH * SL], BF16)
    qr_d = dscr("qr_d", [NSEG, 128, NCH * SL], BF16)
    lepe_d = dscr("lepe_d", [NSEG, SL, D], BF16)
    x2_d = dscr("x2_d", [NSEG, SL, D], F32)
    x2s_d = dscr("x2s_d", [NSEG, SL, D], F32)
    xlt2_d = dscr("xlt2_d", [NSEG, 128, NCH * SL], BF16)
    cl_d = dscr("cl_d", [NSEG, 128, NT * E], F32)
    wa_d = dscr("wa_d", [NSEG, 128, NT * E], F32)
    lg_d = dscr("lg_d", [NSEG * NT, 128, 80], F32) if dev else None
    dbg_d = dscr("dbg_d", [128, 2048], F32) if dev else None
    dbg2_d = dscr("dbg2_d", [128, 8192], BF16) if dev else None
    kv_in = dint("kv_in", [128, 4128], F32)
    kv_out = nc.dram_tensor("kv_out", [128, 4128], F32, kind="Internal", addr_space="Shared").ap()
    cnt_in = dint("cnt_in", [8, E], F32)
    cnt_out = dint("cnt_out", [E, E], F32)
    smw_in = dint("smw_in", [128, CAP], F32)
    smw_out = dint("smw_out", [16, CAP], F32)
    x2l_d = dint("x2l_d", [NSEG * SL, D], BF16)
    x2a_d = nc.dram_tensor("x2a_d", [NCORES * NSEG * SL, D], BF16, kind="Internal", addr_space="Shared").ap()
    x4l_d = dint("x4l_d", [X4ROWS, D], BF16)
    x4a_d = nc.dram_tensor("x4a_d", [NCORES * X4ROWS, D], BF16, kind="Internal", addr_space="Shared").ap()

    with ExitStack() as ctx:
        def sbuf(name, shape, dt=F32):
            return ctx.enter_context(nc.sbuf_tensor(name, shape, dt))

        def sem(name, dma=False):
            s = Sem(name, dma=dma)
            s.h = ctx.enter_context(nc.semaphore(name))
            return s

        c64_sb = sbuf("c64_sb", [128, E])
        c64p1_sb = sbuf("c64p1_sb", [128, E])
        c512_sb = sbuf("c512_sb", [128, CAP])
        ustrict_sb = sbuf("ustrict_sb", [128, 128], BF16)
        ident_sb = sbuf("ident_sb", [128, 128], BF16)
        identf_sb = sbuf("identf_sb", [128, 128])
        ones_bf_sb = sbuf("ones_bf_sb", [1, 128], BF16)
        onec_bf_sb = sbuf("onec_bf_sb", [128, 1], BF16)
        ones_f_sb = sbuf("ones_f_sb", [1, 128])
        tok_hi_sb = sbuf("tok_hi_sb", [128, 32])
        tok_lo_sb = sbuf("tok_lo_sb", [128, 32])
        maskcols_sb = sbuf("maskcols_sb", [E, 8])
        comb_sb = sbuf("comb_sb", [128, 128], BF16)
        zrow_sb = sbuf("zrow_sb", [128, 1])
        eps_sb = sbuf("eps_sb", [128, 1])
        bqk_sb = sbuf("bqk_sb", [128, 16])
        lepeb_sb = sbuf("lepeb_sb", [1, D], BF16)
        gate_sb = sbuf("gate_sb", [128, NCH * 72], BF16)

        kvbf_sb = sbuf("kvbf_sb", [128, B * NH * HD], BF16)
        kmean_sb = sbuf("kmean_sb", [128, B * NCH])
        sall_sb = sbuf("sall_sb", [128, B * NCH * NH], BF16)
        ksum_sb = sbuf("ksum_sb", [128, B * NCH])
        edesc_sb = sbuf("edesc_sb", [128, 256])
        pos_sb = sbuf("pos_sb", [128, 256])
        addrf_sb = sbuf("addrf_sb", [128, 256])
        addru_sb = sbuf("addru_sb", [128, 256], U32)
        sg_sb = sbuf("sg_sb", [128, 32])
        rowbase_sb = sbuf("rowbase_sb", [1, 32 * E])
        cnt1_sb = sbuf("cnt1_sb", [1, NSEG * E])
        tot_sb = sbuf("tot_sb", [1, 2 * E])
        smloc_sb = sbuf("smloc_sb", [16, CAP])
        smi_sb = sbuf("smi_sb", [16, CAP])
        smT_sb = sbuf("smT_sb", [128, 64])
        smTu_sb = sbuf("smTu_sb", [128, 64], U32)
        wt_sb = sbuf("wt_sb", [128, 32])
        stat_sb = sbuf("stat_sb", [128, 6])
        hstat_sb = sbuf("hstat_sb", [2, 6])
        bn6_sb = sbuf("bn6_sb", [128, 12])
        hbn_sb = sbuf("hbn_sb", [2, 12])
        hbn2_sb = sbuf("hbn2_sb", [128, 12])
        stat2_sb = sbuf("stat2_sb", [128, 6])
        red8_sb = sbuf("red8_sb", [128, 8])

        AAB = 68 * 1024
        AWB = 96 * 1024
        aa = sbuf("aa", [128, AAB], U8)
        aw = sbuf("aw", [128, AWB], U8)

        def av(off, nbytes, dt):
            assert off + nbytes <= AAB, (off, nbytes)
            return aa[:, off:off + nbytes].bitcast(dt)

        def wv(off, nbytes, dt):
            assert off + nbytes <= AWB, (off, nbytes)
            return aw[:, off:off + nbytes].bitcast(dt)

        xln_sb = av(0, 8192, BF16)
        xlt_sb = av(8192, 8224, BF16)
        qel_sb = av(16416, 8192, BF16)
        kel_sb = av(24608, 8192, BF16)
        qr2_sb = av(32800, 2048, BF16)
        kr2_sb = av(34848, 2048, BF16)
        krt2_sb = [av(36896 + 1024 * i, 1024, BF16) for i in range(2)]
        cos_sb = av(38944, 4096, BF16)
        sin_sb = av(43040, 4096, BF16)
        eexp_sb = av(47136, 1024, BF16)
        erel_sb = av(48160, 1024, BF16)
        rexp_sb = av(49184, 1024, BF16)
        rrel_sb = av(50208, 1024, BF16)
        xt_sb = av(51232, 4096, F32)
        kvev_sb = av(59424, 4096, F32)
        lepet_sb2 = [av(63520 + 2048 * i, 2048, BF16) for i in range(2)]
        xhf_sb = sbuf("xhf_sb", [2, D])
        xlnh_sb = sbuf("xlnh_sb", [2, D], BF16)

        kvf_sb = av(0, 16512, F32)

        xln2_sb = av(0, 8192, BF16)
        xlt2_sb = av(8192, 8192, BF16)
        p3qel_sb = av(16416, 8192, BF16)
        p3qr_sb = av(24608, 8192, BF16)
        attnz_sb = [av(32800 + 4096 * i, 4096, F32) for i in range(2)]
        p3x_sb = [av(40992 + 4096 * i, 4096, F32) for i in range(2)]
        p3lep_sb = [av(49184 + 2048 * i, 2048, BF16) for i in range(2)]
        x2t3_sb = [av(53280 + 4096 * i, 4096, F32) for i in range(2)]
        zden_sb = av(61472, 2048, F32)
        zt_sb = av(63520, 128, F32)
        z_sb = av(63648, 128, F32)
        lgt_sb = [av(63776, 320, F32) for i in range(1)]
        lgt_sb = [av(63776 + 320 * i, 320, F32) for i in range(2)]
        m8_sb = [av(64416 + 32 * i, 32, F32) for i in range(2)]
        repl_sb = av(64480, 256, F32)
        selt_sb = [av(64736 + 128 * i, 128, BF16) for i in range(2)]
        eall_sb = av(64992, 256, F32)
        wm3_sb = [av(65248 + 256 * i, 256, F32) for i in range(2)]
        rsum_sb = av(65760, 16, F32)
        ed8_sb = av(65776, 32, F32)
        clt_sb = [av(65808 + 256 * i, 256, F32) for i in range(2)]
        tot2_sb = av(66320, 8, F32)

        NPOSH, NHL = 6, 3
        cnts_sb = av(0, 256, F32)
        clseg_sb = av(256, 1024, F32)
        waseg_sb = av(1280, 1024, F32)
        cg_sb = av(2304, 256, F32)
        mask_sb = av(2560, 1024, BF16)
        wmp_sb = av(3584, 2048, F32)
        junk_sb = av(5632, 256, F32)
        posh_sb = [av(5888 + 1024 * i, 1024, BF16) for i in range(NPOSH)]
        mhl_sb = [av(12032 + 2048 * i, 2048, BF16) for i in range(NHL)]
        whl_sb = [av(18176 + 2048 * i, 2048, BF16) for i in range(NHL)]
        smhlb_sb2 = av(0, 2048, BF16)
        smv_sb = av(2048, 2048, F32)
        wv_sb = av(4096, 2048, F32)
        eh_sb = [av(6144, 1024, F32), av(7168, 1024, F32)]
        a1_sb = av(8192, 1024, F32)
        alive_sb = av(9216, 1024, F32)

        h_sb = av(33792, 16384, BF16)
        sguv = wv(0, 65536, BF16)
        p5x_sb = av(50176, 4096, F32)
        p5s_sb = av(54272, 4096, F32)
        silu_sb = av(58368, 1024, BF16)
        p5xlt_sb = av(59392, 8192, BF16)

        egu_sb = [av(18432, 16384, BF16), av(34816, 16384, BF16)]
        edn_sb = [av(51200, 8192, BF16), av(59392, 8192, BF16)]
        rows_sb = av(0, 2048, BF16)
        rowsT_sb = av(2048, 8192, BF16)
        he_sb = av(10240, 4096, BF16)
        oute_sb = av(14336, 2048, BF16)
        zero_sb = av(16384, 2048, BF16)
        silu6_sb = av(17408, 1024, BF16)

        NFT = 8
        fin_sb = [av(4096 * i, 4096, F32) for i in range(NFT)]

        ps0 = ctx.enter_context(nc.psum_tensor("ps0", [128, SL], F32))
        ps1 = ctx.enter_context(nc.psum_tensor("ps1", [128, SL], F32))
        ps2 = ctx.enter_context(nc.psum_tensor("ps2", [128, SL], F32))
        ps_trF = ctx.enter_context(nc.psum_tensor("ps_trF", [128, 2 * SL], BF16))
        ps_tr = ps_trF[:, 0:SL]
        ps_tr2 = ps_trF[:, SL:2 * SL]
        ps_kv = ctx.enter_context(nc.psum_tensor("ps_kv", [128, 2 * SL], F32))
        ps3 = ctx.enter_context(nc.psum_tensor("ps3", [128, SL], F32))
        ps4 = ctx.enter_context(nc.psum_tensor("ps4", [128, SL], F32))


        qL = sem("qL", dma=True); qS = sem("qS", dma=True); qG = sem("qG", dma=True)

        qE = [sem("qE0", dma=True), sem("qE1", dma=True)]
        qX2 = [sem("qX2a", dma=True), sem("qX2b", dma=True)]
        qCL = [sem("qCLa", dma=True), sem("qCLb", dma=True)]
        qWM = [sem("qWMa", dma=True), sem("qWMb", dma=True)]
        qXS = [sem("qXS%d" % i, dma=True) for i in range(4)]
        qW5 = sem("qW5", dma=True)
        qCkv = sem("qCkv"); qCcnt = sem("qCcnt"); qCx2 = sem("qCx2")
        qCsm = sem("qCsm"); qCx4 = sem("qCx4")
        mm = sem("mm"); ve = sem("ve"); se = sem("se"); tE = sem("tE"); gv = sem("gv")
        kb.auto = {V: ve, S: se}

        for dst, src in [(c64_sb, c64), (c64p1_sb, c64p1), (c512_sb, c512),
                         (ustrict_sb, ustrict), (ident_sb, ident_bf), (identf_sb, ident_f),
                         (ones_bf_sb, ones_bf), (onec_bf_sb, onec_bf), (ones_f_sb, ones_f),
                         (tok_hi_sb, tok_hi), (tok_lo_sb, tok_lo),
                         (maskcols_sb, maskcols), (comb_sb, comb_bf),
                         (zrow_sb, zrow_col), (eps_sb, eps_col),
                         (bqk_sb, bqk_cols)]:
            kb.op(Y, "dma_start", out=dst[:], in_=src, inc=[(qL, 16)])
        P0_sync = qL.n
        kb.op(G, "dma_start", out=wv(0, 32768, BF16).rearrange("p (c f) -> p c f", f=2 * D),
              in_=wqk_p.rearrange("(c p) f -> p c f", p=128), inc=[(qG, 16)])
        kb.op(G, "dma_start", out=wv(32768, 49152, BF16).rearrange("p (k c f) -> p k c f", k=3, c=NCH),
              in_=lepe_w.rearrange("k (c p) f -> p k c f", p=128), inc=[(qG, 16)])
        kb.op(G, "dma_start", out=lepeb_sb[:], in_=lepe_b, inc=[(qG, 16)])
        kb.op(G, "dma_start", out=gate_sb[:].rearrange("p (c f) -> p c f", f=72),
              in_=gate_ext.rearrange("(c p) f -> p c f", p=128), inc=[(qG, 16)])
        P0_gp = qG.n
        wqk_sb = wv(0, 32768, BF16)
        lepew_sb = wv(32768, 49152, BF16)
        kb.op(V, "memset", ksum_sb[:], 0.0, inc=[(ve, 1)])
        kb.wait(T, qL, P0_sync)
        kb.wait(V, qL, P0_sync)
        kb.wait(S, qL, P0_sync)
        kb.wait(T, qG, P0_gp)

        def emit_ln(src_ap, dst_ap, bn, st, part=128):
            kb.op(V, "bn_stats", out=bn[0:part, 0:6], in_=src_ap[0:part, 0:SL], inc=[(ve, 1)])
            kb.op(V, "bn_stats", out=bn[0:part, 6:12], in_=src_ap[0:part, SL:D], inc=[(ve, 1)])
            kb.wait(V, ve, ve.n)
            kb.op(V, "bn_aggr", out=st[0:part, 0:2], in_=bn[0:part, :], inc=[(ve, 1)])
            kb.wait(S, ve, ve.n)
            kb.op(S, "activation", out=st[0:part, 2:3], in_=st[0:part, 1:2], func=AF.Sqrt,
                  bias=eps_sb[0:part, 0:1], scale=1.0, inc=[(se, 1)])
            kb.wait(V, se, se.n)
            kb.op(V, "reciprocal", out=st[0:part, 3:4], in_=st[0:part, 2:3], inc=[(ve, 1)])
            kb.wait(V, ve, ve.n)
            kb.op(V, "scalar_tensor_tensor", out=st[0:part, 5:6], in0=st[0:part, 0:1],
                  scalar=-1.0, in1=st[0:part, 3:4], op0=ALU.mult, op1=ALU.mult, inc=[(ve, 1)])
            kb.wait(S, ve, ve.n)
            kb.op(S, "activation", out=dst_ap, in_=src_ap[0:part, :], func=AF.Identity,
                  bias=st[0:part, 5:6], scale=st[0:part, 3:4], inc=[(se, 1)])
            return se.n

        if max_phase >= 1:
            p1_psg = [0, 0, 0, 0]
            halo_ve_g = halo_se_g = rope_g = xln_g = xlt_g = 0
            ktr_g = [0, 0]
            kcp_g = [0, 0]
            for s in range(NSEG):
                b = s // 2
                kb.wait(Y, ve, halo_ve_g)
                kb.wait(Y, se, halo_se_g)
                kb.op(Y, "dma_start", out=xhf_sb[:], in_=x_halo[s], inc=[(qL, 16)])
                kb.wait(V, qL, qL.n)
                kb.wait(S, mm, xln_g)
                emit_ln(xhf_sb, xlnh_sb[:], hbn_sb, hstat_sb, part=2)
                halo_ve_g = ve.n
                halo_se_g = se.n
                for t in range(NT):
                    kb.wait(Y, se, se.n)
                    kb.op(Y, "dma_start", out=xt_sb[:], in_=x_shard[s, t * 128:(t + 1) * 128, :],
                          inc=[(qL, 16)])
                    kb.wait(V, qL, qL.n)
                    if t == 0:
                        kb.wait(S, mm, xln_g)
                    emit_ln(xt_sb, xln_sb[:, t * D:(t + 1) * D], bn6_sb, stat_sb)
                xln_done = se.n

                kb.wait(T, se, xln_done)
                for c in range(NCH):
                    for t in range(NT):
                        kb.op(T, "transpose", ps_tr[:, t * 128:(t + 1) * 128],
                              xln_sb[:, t * D + c * 128: t * D + (c + 1) * 128], ident_sb[:],
                              inc=([(mm, 1)] if t == NT - 1 else None))
                    kb.wait(V, mm, mm.n)
                    if c == 0:
                        kb.wait(V, mm, xlt_g)
                        kb.wait(V, qS, qS.n)
                    kb.op(V, "tensor_copy", out=xlt_sb[:, c * 514 + 1: c * 514 + 513],
                          in_=ps_tr[:], inc=[(ve, 1)])
                    kb.wait(T, ve, ve.n)
                for c in range(NCH):
                    kb.op(T, "transpose", ps_tr[:, c * 2:(c + 1) * 2],
                          xlnh_sb[:, c * 128:(c + 1) * 128], ident_sb[0:2, 0:2],
                          inc=([(mm, 1)] if c == NCH - 1 else None))
                kb.wait(V, mm, mm.n)
                for c in range(NCH):
                    kb.op(V, "tensor_copy", out=xlt_sb[:, c * 514: c * 514 + 1],
                          in_=ps_tr[:, c * 2: c * 2 + 1], inc=[(ve, 1)])
                    kb.op(V, "tensor_copy", out=xlt_sb[:, c * 514 + 513: c * 514 + 514],
                          in_=ps_tr[:, c * 2 + 1: c * 2 + 2], inc=[(ve, 1)])
                kb.wait(T, ve, ve.n)

                kb.wait(Y, ve, rope_g)
                kb.op(Y, "dma_start", out=cos_sb.rearrange("p (c q) -> p c q", q=SL),
                      in_=costab[s].transpose([1, 0, 2]), inc=[(qL, 16)])
                kb.op(Y, "dma_start", out=sin_sb.rearrange("p (c q) -> p c q", q=SL),
                      in_=sintab[s].transpose([1, 0, 2]), inc=[(qL, 16)])
                rope_ld = qL.n

                qk_psg = [p1_psg[0], p1_psg[1]]
                elu_g = [0, 0]
                for f in range(16):
                    dst = qel_sb if f < 8 else kel_sb
                    fc = f % 8
                    fp = f % 2
                    psq = ps0 if fp == 0 else ps1
                    eex = eexp_sb if fp == 0 else rexp_sb
                    ere = erel_sb if fp == 0 else rrel_sb
                    kb.wait(T, se, qk_psg[fp])
                    for c in range(NCH):
                        kb.op(T, "matmul", psq[:],
                              wqk_sb[:, c * 2048 + f * 128: c * 2048 + (f + 1) * 128],
                              xlt_sb[:, c * 514 + 1: c * 514 + 513],
                              start=(c == 0), stop=(c == NCH - 1),
                              inc=([(mm, 1)] if c == NCH - 1 else None))
                    kb.wait(S, mm, mm.n)
                    kb.wait(S, ve, elu_g[fp])
                    if f == 0:
                        kb.wait(S, qS, qS.n)
                    kb.op(S, "activation", out=eex, in_=psq[:], func=AF.Exp,
                          bias=bqk_sb[:, f:f + 1], scale=1.0, inc=[(se, 1)])
                    kb.op(S, "activation", out=ere, in_=psq[:], func=AF.Relu,
                          bias=bqk_sb[:, f:f + 1], scale=1.0, inc=[(se, 1)])
                    qk_psg[fp] = se.n
                    kb.wait(V, se, se.n)
                    kb.op(V, "tensor_scalar", out=eex, in0=eex, scalar1=1.0,
                          scalar2=None, op0=ALU.min, inc=[(ve, 1)])
                    kb.op(V, "tensor_tensor", out=dst[:, fc * SL:(fc + 1) * SL], in0=eex,
                          in1=ere, op=ALU.add, inc=[(ve, 1)])
                    elu_g[fp] = ve.n
                qk_done = ve.n
                kb.wait(S, ve, qk_done)
                kb.op(S, "dma_start", out=qel_d[s], in_=qel_sb, inc=[(qS, 16)])

                for c in range(NCH):
                    kb.op(V, "tensor_reduce", out=red8_sb[:, c:c + 1],
                          in_=kel_sb[:, c * SL:(c + 1) * SL], axis=AX.X, op=ALU.add,
                          inc=([(ve, 1)] if c == NCH - 1 else None))
                kb.wait(V, ve, ve.n)
                kb.op(V, "tensor_tensor", out=ksum_sb[:, b * NCH:(b + 1) * NCH],
                      in0=ksum_sb[:, b * NCH:(b + 1) * NCH],
                      in1=red8_sb[:], op=ALU.add, inc=[(ve, 1)])

                kb.wait(V, qL, rope_ld)
                for (src, isq) in ((qel_sb, True), (kel_sb, False)):
                    dstr = qr2_sb if isq else kr2_sb
                    for jc in range(4):
                        co = cos_sb[:, jc * SL:(jc + 1) * SL]
                        si = sin_sb[:, jc * SL:(jc + 1) * SL]
                        ev = src[:, jc * SL:(jc + 1) * SL]
                        od = src[:, (4 + jc) * SL:(5 + jc) * SL]
                        kb.wait(V, qS, qS.n)
                        if not isq:
                            kb.wait(V, mm, mm.n)
                        kb.op(V, "tensor_tensor", out=eexp_sb[:], in0=co, in1=ev, op=ALU.mult, inc=[(ve, 1)])
                        kb.op(V, "tensor_tensor", out=erel_sb[:], in0=si, in1=od, op=ALU.mult, inc=[(ve, 1)])
                        kb.wait(V, ve, ve.n)
                        kb.op(V, "tensor_tensor", out=dstr[:, 0:SL],
                              in0=eexp_sb[:], in1=erel_sb[:], op=ALU.subtract, inc=[(ve, 1)])
                        kb.op(V, "tensor_tensor", out=eexp_sb[:], in0=co, in1=od, op=ALU.mult, inc=[(ve, 1)])
                        kb.op(V, "tensor_tensor", out=erel_sb[:], in0=si, in1=ev, op=ALU.mult, inc=[(ve, 1)])
                        kb.wait(V, ve, ve.n)
                        kb.op(V, "tensor_tensor", out=dstr[:, SL:2 * SL],
                              in0=eexp_sb[:], in1=erel_sb[:], op=ALU.add, inc=[(ve, 1)])
                        pair_done = ve.n
                        if isq:
                            kb.wait(S, ve, pair_done)
                            kb.op(S, "dma_start", out=qr_d[s, :, jc * SL:(jc + 1) * SL],
                                  in_=dstr[:, 0:SL], inc=[(qS, 16)])
                            kb.op(S, "dma_start", out=qr_d[s, :, (4 + jc) * SL:(5 + jc) * SL],
                                  in_=dstr[:, SL:2 * SL], inc=[(qS, 16)])
                        else:
                            kb.wait(T, ve, pair_done)
                            for half, h in ((0, jc), (1, jc + 4)):
                                trg = ps_tr if half == 0 else ps_tr2
                                kb.wait(T, ve, ktr_g[half])
                                for t in range(NT):
                                    kb.op(T, "transpose", trg[:, t * 128:(t + 1) * 128],
                                          dstr[:, half * SL + t * 128: half * SL + (t + 1) * 128],
                                          ident_sb[:], inc=([(mm, 1)] if t == NT - 1 else None))
                                kb.wait(V, mm, mm.n)
                                kb.wait(V, mm, kcp_g[half])
                                for t in range(NT):
                                    kb.op(V, "tensor_copy", out=krt2_sb[half][:, t * 128:(t + 1) * 128],
                                          in_=trg[:, t * 128:(t + 1) * 128],
                                          inc=([(ve, 1)] if t == NT - 1 else None))
                                ktr_g[half] = ve.n
                                kb.wait(T, ve, ktr_g[half])
                                for t in range(NT):
                                    kb.op(T, "matmul", ps_kv[:, h * HD:(h + 1) * HD],
                                          krt2_sb[half][:, t * 128:(t + 1) * 128],
                                          xln_sb[:, t * D + h * 128:t * D + (h + 1) * 128],
                                          start=(t == 0), stop=(t == NT - 1),
                                          skip_group_check=True,
                                          inc=([(mm, 1)] if t == NT - 1 else None))
                                kcp_g[half] = mm.n
                kv_mm = mm.n
                rope_g = ve.n
                xln_g = kv_mm
                kb.wait(S, mm, kv_mm)
                kb.wait(S, qG, qG.n)
                kb.op(S, "copy", kvev_sb[:, 0:SL], ps_kv[:, 0:SL], inc=[(se, 1)])
                kb.op(S, "copy", kvev_sb[:, SL:2 * SL], ps_kv[:, SL:2 * SL], inc=[(se, 1)])
                kv_ev = se.n
                kb.wait(T, se, kv_ev)
                kb.wait(G, se, kv_ev)
                kb.wait(G, qG, qG.n)
                if s % 2 == 0:
                    kb.op(G, "dma_start", out=kv_in[:, b * D:(b + 1) * D], in_=kvev_sb[:],
                          inc=[(qG, 16)])
                else:
                    kb.op(G, "dma_start", out=kv_in[:, b * D:(b + 1) * D], in_=kvev_sb[:],
                          accum_op=ALU.add, inc=[(qG, 16)])

                lep_psg = {"ps0": qk_psg[0], "ps1": qk_psg[1], "ps3": p1_psg[2], "ps4": p1_psg[3]}
                for t in range(NT):
                    pse = (ps0, ps1) if t % 2 == 0 else (ps3, ps4)
                    pk = ("ps0", "ps1") if t % 2 == 0 else ("ps3", "ps4")
                    for half in range(2):
                        pst = pse[half]
                        kb.wait(T, se, lep_psg[pk[half]])
                        for k in range(3):
                            for c in range(NCH):
                                kb.op(T, "matmul", pst[:],
                                      xlt_sb[:, c * 514 + t * 128 + k: c * 514 + t * 128 + k + 128],
                                      lepew_sb[:, (k * NCH + c) * D + half * SL:(k * NCH + c) * D + (half + 1) * SL],
                                      start=(k == 0 and c == 0), stop=False)
                        kb.op(T, "matmul", pst[:], ones_bf_sb[0:1, :],
                              lepeb_sb[0:1, half * SL:(half + 1) * SL],
                              start=False, stop=True, inc=[(mm, 1)])
                        kb.wait(S, mm, mm.n)
                        if half == 0:
                            kb.wait(S, qS, qS.n)
                        kb.op(S, "copy", lepet_sb2[t % 2][:, half * SL:(half + 1) * SL],
                              pst[:], inc=[(se, 1)])
                        lep_psg[pk[half]] = se.n
                    kb.op(S, "dma_start", out=lepe_d[s, t * 128:(t + 1) * 128, :],
                          in_=lepet_sb2[t % 2], inc=[(qS, 16)])
                p1_psg = [lep_psg["ps0"], lep_psg["ps1"], lep_psg["ps3"], lep_psg["ps4"]]
                xlt_g = mm.n

        mm_p1 = mm.n

        if max_phase >= 2:
            kb.wait(G, ve, ve.n)
            kb.wait(G, qG, qG.n)
            kb.op(G, "dma_start", out=kv_in[:, 4096:4128], in_=ksum_sb[:], inc=[(qG, 16)])
            kb.wait(G, qG, qG.n)
            kb.op(G, "collective_compute", "AllReduce", ALU.add,
                  replica_groups=[list(range(NCORES))],
                  ins=[kv_in.opt()], outs=[kv_out.opt()], inc=[(qCkv, 1)])
            kb.wait(Y, qCkv, qCkv.n)
            kb.wait(Y, mm, mm.n)
            kb.wait(Y, ve, ve.n)
            kb.wait(Y, se, se.n)
            kb.wait(Y, qS, qS.n)
            kb.op(Y, "dma_start", out=kvf_sb, in_=kv_out[:, :], inc=[(qL, 16)])
            kb.wait(V, qL, qL.n)
            kb.op(V, "tensor_scalar", out=kvbf_sb[:], in0=kvf_sb[:, 0:4096], scalar1=ISQN,
                  scalar2=None, op0=ALU.mult, inc=[(ve, 1)])
            kb.op(V, "tensor_scalar", out=kmean_sb[:], in0=kvf_sb[:, 4096:4128], scalar1=1.0 / N,
                  scalar2=None, op0=ALU.mult, inc=[(ve, 1)])
            kb.op(V, "memset", sall_sb[:], 0.0, inc=[(ve, 1)])
            kb.wait(V, ve, ve.n)
            for b in range(B):
                for c in range(NCH):
                    base = (b * NCH + c) * NH
                    h0 = 2 * (c % 4)
                    kb.op(V, "tensor_copy", out=sall_sb[0:64, base + h0: base + h0 + 1],
                          in_=kmean_sb[0:64, b * NCH + c: b * NCH + c + 1], inc=[(ve, 1)])
                    kb.op(V, "tensor_copy", out=sall_sb[64:128, base + h0 + 1: base + h0 + 2],
                          in_=kmean_sb[64:128, b * NCH + c: b * NCH + c + 1], inc=[(ve, 1)])
            p2_done = ve.n
            if dev:
                kb.wait(S, ve, p2_done)
                kb.op(S, "dma_start", out=dbg2_d[:, 0:4096], in_=kvbf_sb[:], inc=[(qS, 16)])
                kb.op(S, "dma_start", out=dbg_d[:, 0:32], in_=kmean_sb[:], inc=[(qS, 16)])
                kb.op(S, "dma_start", out=dbg2_d[:, 4096:4096 + 256], in_=sall_sb[:], inc=[(qS, 16)])

        if max_phase >= 3:
            kb.op(V, "memset", rowbase_sb[:], 0.0, inc=[(ve, 1)])
            kb.op(V, "memset", cnt1_sb[:], 0.0, inc=[(ve, 1)])
            for s in range(NSEG):
                b = s // 2
                g0 = s * NT
                kb.wait(Y, mm, mm.n)
                kb.wait(Y, ve, ve.n)
                kb.wait(Y, se, se.n)
                kb.wait(Y, qS, qS.n)
                kb.op(Y, "dma_start", out=p3qel_sb, in_=qel_d[s], inc=[(qL, 16)])
                kb.op(Y, "dma_start", out=p3qr_sb, in_=qr_d[s], inc=[(qL, 16)])
                ld_qq = qL.n
                kb.wait(T, qL, ld_qq)
                kb.wait(T, ve, ve.n)
                for c in range(NCH):
                    kb.op(T, "matmul", ps2[0:8, :],
                          sall_sb[:, (b * NCH + c) * NH:(b * NCH + c + 1) * NH],
                          p3qel_sb[:, c * SL:(c + 1) * SL],
                          start=(c == 0), stop=(c == NCH - 1),
                          inc=([(mm, 1)] if c == NCH - 1 else None))
                kb.wait(S, mm, mm.n)
                kb.op(S, "copy", zden_sb[0:8, :], ps2[0:8, :], inc=[(se, 1)])
                kb.wait(T, se, se.n)
                for t in range(NT):
                    kb.op(T, "transpose", ps2[:, 448 + t * 8: 456 + t * 8],
                          zden_sb[0:8, t * 128:(t + 1) * 128], identf_sb[0:8, 0:8],
                          inc=([(mm, 1)] if t == NT - 1 else None))
                kb.wait(V, mm, mm.n)
                kb.op(V, "tensor_scalar", out=zt_sb[:, 0:32], in0=ps2[:, 448:480], scalar1=1e-6,
                      scalar2=None, op0=ALU.add, inc=[(ve, 1)])
                kb.op(V, "reciprocal", out=z_sb[:, 0:32], in_=zt_sb[:, 0:32], inc=[(ve, 1)])
                z_done = ve.n
                kb.wait(T, ve, z_done)

                og = {0: 0, 1: 0}
                ldg = {0: 0, 1: 0}
                for t in range(NT):
                    g = g0 + t
                    pr = t % 2
                    odst = [(h, ps_kv[:, h * HD:(h + 1) * HD]) for h in range(NH)]
                    kb.wait(T, ve, og[1 - pr])
                    for h, dstp in odst:
                        kb.op(T, "matmul", dstp,
                              p3qr_sb[:, h * SL + t * 128: h * SL + (t + 1) * 128],
                              kvbf_sb[:, (b * NH + h) * HD:(b * NH + h + 1) * HD],
                              start=(h % 4 == 0), stop=(h % 4 == 3),
                              skip_group_check=True,
                              inc=([(mm, 1)] if h == NH - 1 else None))
                    o_mm = mm.n
                    kb.wait(Y, ve, ldg[pr])
                    kb.wait(Y, qX2[pr], qX2[pr].n)
                    kb.op(Y, "dma_start", out=p3x_sb[pr], in_=x_shard[s, t * 128:(t + 1) * 128, :],
                          inc=[(qL, 16)])
                    kb.op(Y, "dma_start", out=p3lep_sb[pr], in_=lepe_d[s, t * 128:(t + 1) * 128, :],
                          inc=[(qL, 16)])
                    ld_xt = qL.n
                    kb.wait(V, mm, o_mm)
                    for h, dstp in odst:
                        kb.op(V, "tensor_scalar", out=attnz_sb[pr][:, h * HD:(h + 1) * HD],
                              in0=dstp,
                              scalar1=z_sb[:, t * 8 + h: t * 8 + h + 1], scalar2=None,
                              op0=ALU.mult, inc=([(ve, 1)] if h == NH - 1 else None))
                    og[pr] = ve.n
                    kb.wait(V, qL, ld_xt)
                    kb.wait(V, qX2[pr], qX2[pr].n)
                    kb.op(V, "scalar_tensor_tensor", out=x2t3_sb[pr], in0=p3lep_sb[pr][:],
                          scalar=1.0, in1=p3x_sb[pr][:], op0=ALU.mult, op1=ALU.add, inc=[(ve, 1)])
                    kb.op(V, "tensor_tensor", out=x2t3_sb[pr], in0=x2t3_sb[pr], in1=attnz_sb[pr],
                          op=ALU.add, inc=[(ve, 1)])
                    x2_done = ve.n
                    kb.wait(S, ve, x2_done)
                    kb.op(S, "dma_start", out=x2_d[s, t * 128:(t + 1) * 128, :], in_=x2t3_sb[pr],
                          inc=[(qX2[pr], 16)])
                    bnp, stp = (bn6_sb, stat_sb) if pr == 0 else (hbn2_sb, stat2_sb)
                    kb.op(V, "bn_stats", out=bnp[:, 0:6], in_=x2t3_sb[pr][:, 0:SL], inc=[(ve, 1)])
                    kb.op(V, "bn_stats", out=bnp[:, 6:12], in_=x2t3_sb[pr][:, SL:D], inc=[(ve, 1)])
                    kb.op(V, "bn_aggr", out=stp[:, 0:2], in_=bnp[:], inc=[(ve, 1)])
                    kb.wait(S, ve, ve.n)
                    kb.op(S, "activation", out=stp[:, 2:3], in_=stp[:, 1:2], func=AF.Sqrt,
                          bias=eps_sb[:, 0:1], scale=1.0, inc=[(se, 1)])
                    kb.wait(V, se, se.n)
                    kb.op(V, "reciprocal", out=stp[:, 3:4], in_=stp[:, 2:3], inc=[(ve, 1)])
                    kb.op(V, "scalar_tensor_tensor", out=stp[:, 5:6], in0=stp[:, 0:1],
                          scalar=-1.0, in1=stp[:, 3:4], op0=ALU.mult, op1=ALU.mult, inc=[(ve, 1)])
                    kb.wait(S, ve, ve.n)
                    kb.op(S, "activation", out=xln2_sb[:, t * D:(t + 1) * D], in_=x2t3_sb[pr][:],
                          func=AF.Identity, bias=stp[:, 5:6], scale=stp[:, 3:4], inc=[(se, 1)])
                    ldg[pr] = ve.n
                    ln2_done = se.n
                    kb.op(S, "dma_start", out=x2l_d[s * SL + t * 128: s * SL + (t + 1) * 128, :],
                          in_=xln2_sb[:, t * D:(t + 1) * D], inc=[(qS, 16)])

                kb.wait(T, se, se.n)
                for c in range(NCH):
                    for t in range(NT):
                        kb.op(T, "transpose", ps_tr[:, t * 128:(t + 1) * 128],
                              xln2_sb[:, t * D + c * 128: t * D + (c + 1) * 128], ident_sb[:],
                              inc=([(mm, 1)] if t == NT - 1 else None))
                    kb.wait(V, mm, mm.n)
                    kb.op(V, "tensor_copy", out=xlt2_sb[:, c * SL:(c + 1) * SL],
                          in_=ps_tr[:], inc=[(ve, 1)])
                    kb.wait(T, ve, ve.n)
                xlt2_done = ve.n
                kb.wait(S, ve, xlt2_done)
                kb.op(S, "dma_start", out=xlt2_d[s], in_=xlt2_sb, inc=[(qS, 16)])

                for t in range(NT):
                    g = g0 + t
                    pr = t % 2
                    rcol = 0 if pr == 0 else 128
                    ccol = 256 if pr == 0 else 320
                    tcol = 384 if pr == 0 else 448
                    for c in range(NCH):
                        kb.op(T, "matmul", ps3[:, rcol:rcol + 72],
                              xlt2_sb[:, c * SL + t * 128: c * SL + (t + 1) * 128],
                              gate_sb[:, c * 72:(c + 1) * 72],
                              start=(c == 0), stop=(c == NCH - 1),
                              inc=([(mm, 1)] if c == NCH - 1 else None))
                    kb.wait(S, mm, mm.n)
                    kb.op(S, "copy", lgt_sb[pr][:, 0:72], ps3[:, rcol:rcol + 72], inc=[(se, 1)])
                    lg_ev = se.n
                    kb.op(S, "activation", out=sg_sb[:, g:g + 1], in_=lgt_sb[pr][:, 64:65],
                          func=AF.Sigmoid, bias=0.0, scale=1.0, inc=[(se, 1)])
                    if dev:
                        kb.op(S, "dma_start", out=lg_d[g][:, 0:72], in_=lgt_sb[pr][:, 0:72],
                              inc=[(qS, 16)])
                        kb.wait(S, qS, qS.n)
                    kb.wait(V, se, lg_ev)
                    kb.op(V, "max", out=m8_sb[pr], in_=lgt_sb[pr][:, 0:64], inc=[(ve, 1)])
                    kb.op(V, "match_replace", out=repl_sb, in_to_replace=m8_sb[pr],
                          in_values=lgt_sb[pr][:, 0:64], imm_value=-1e30, inc=[(ve, 1)])
                    kb.op(V, "tensor_tensor", out=selt_sb[pr], in0=lgt_sb[pr][:, 0:64], in1=repl_sb,
                          op=ALU.not_equal, inc=[(ve, 1)])
                    kb.op(V, "tensor_scalar", out=rsum_sb[:, 0:1], in0=m8_sb[pr][:, 0:1],
                          scalar1=-1.0, scalar2=None, op0=ALU.mult, inc=[(ve, 1)])
                    kb.wait(S, ve, ve.n)
                    kb.op(S, "activation", out=eall_sb, in_=lgt_sb[pr][:, 0:64], func=AF.Exp,
                          bias=rsum_sb[:, 0:1], scale=1.0, inc=[(se, 1)])
                    kb.wait(V, se, se.n)
                    kb.wait(V, qWM[pr], qWM[pr].n)
                    kb.op(V, "tensor_tensor", out=wm3_sb[pr], in0=eall_sb, in1=selt_sb[pr],
                          op=ALU.mult, inc=[(ve, 1)])
                    kb.op(V, "tensor_reduce", out=rsum_sb[:, 1:2], in_=wm3_sb[pr], axis=AX.X,
                          op=ALU.add, inc=[(ve, 1)])
                    kb.op(V, "reciprocal", out=rsum_sb[:, 2:3], in_=rsum_sb[:, 1:2], inc=[(ve, 1)])
                    kb.op(V, "tensor_scalar", out=wm3_sb[pr], in0=wm3_sb[pr],
                          scalar1=rsum_sb[:, 2:3], scalar2=None, op0=ALU.mult, inc=[(ve, 1)])
                    wsel_done = ve.n
                    kb.wait(S, ve, wsel_done)
                    kb.op(S, "dma_start", out=wa_d[s, :, t * E:(t + 1) * E], in_=wm3_sb[pr],
                          inc=[(qWM[pr], 16)])
                    kb.op(V, "tensor_tensor", out=eall_sb, in0=selt_sb[pr], in1=c64p1_sb[:],
                          op=ALU.mult, inc=[(ve, 1)])
                    kb.op(V, "max", out=ed8_sb, in_=eall_sb, inc=[(ve, 1)])
                    kb.op(V, "tensor_scalar", out=edesc_sb[:, g * 8:(g + 1) * 8], in0=ed8_sb,
                          scalar1=-1.0, scalar2=None, op0=ALU.add, inc=[(ve, 1)])
                    kb.wait(T, ve, ve.n)
                    kb.op(T, "matmul", ps4[:, ccol - 256:ccol - 192], ustrict_sb[:], selt_sb[pr],
                          start=True, stop=True, inc=[(mm, 1)])
                    kb.op(T, "matmul", ps4[0:1, tcol - 256:tcol - 192], onec_bf_sb[:], selt_sb[pr],
                          start=True, stop=True, inc=[(mm, 1)])
                    cs_mm = mm.n
                    kb.wait(S, mm, cs_mm)
                    kb.wait(S, qCL[pr], qCL[pr].n)
                    kb.op(S, "copy", clt_sb[pr], ps4[:, ccol - 256:ccol - 192], inc=[(se, 1)])
                    kb.op(S, "copy", tot_sb[0:1, pr * 64:pr * 64 + 64],
                          ps4[0:1, tcol - 256:tcol - 192], inc=[(se, 1)])
                    ev_c = se.n
                    kb.wait(V, se, ev_c)
                    kb.op(V, "tensor_copy", out=rowbase_sb[0:1, g * E:(g + 1) * E],
                          in_=cnt1_sb[0:1, s * E:(s + 1) * E], inc=[(ve, 1)])
                    kb.op(V, "tensor_tensor", out=cnt1_sb[0:1, s * E:(s + 1) * E],
                          in0=cnt1_sb[0:1, s * E:(s + 1) * E], in1=tot_sb[0:1, pr * 64:pr * 64 + 64],
                          op=ALU.add, inc=[(ve, 1)])
                    kb.wait(S, ve, ve.n)
                    kb.op(S, "dma_start", out=cl_d[s, :, t * E:(t + 1) * E], in_=clt_sb[pr],
                          inc=[(qCL[pr], 16)])

        if max_phase >= 4:
            kb.wait(G, mm, mm_p1)
            for xh in range(2):
                kb.op(G, "dma_start",
                      out=sguv.rearrange("p (c x f) -> p c x f", x=2, f=2048)[:, :, xh, :],
                      in_=sgu_w[:, xh * 2048:(xh + 1) * 2048].rearrange("(c p) f -> p c f", p=128),
                      inc=[(qW5, 16)])
            kb.op(G, "dma_start", out=wv(65536, 16384, BF16).rearrange("p (c f) -> p c f", f=SL),
                  in_=sdown_w[:, 0:SL].rearrange("(c p) f -> p c f", p=128), inc=[(qW5, 16)])
            kb.op(G, "dma_start", out=wv(81920, 16384, BF16).rearrange("p (c f) -> p c f", f=SL),
                  in_=sdown_w[:, SL:D].rearrange("(c p) f -> p c f", p=128), inc=[(qW5, 16)])
            sd0_sb = wv(65536, 16384, BF16)
            sd1_sb = wv(81920, 16384, BF16)
            p5w_ld = qW5.n

            kb.wait(S, ve, ve.n)
            kb.op(S, "dma_start", out=cnt_in.rearrange("s e -> (s e)").unsqueeze(0),
                  in_=cnt1_sb[0:1, :], inc=[(qS, 16)])
            kb.wait(G, qS, qS.n)
            kb.op(G, "collective_compute", "AllGather", ALU.bypass,
                  replica_groups=[list(range(NCORES))],
                  ins=[cnt_in.opt()], outs=[cnt_out.opt()], inc=[(qCcnt, 1)])
            kb.op(G, "collective_compute", "AllGather", ALU.bypass,
                  replica_groups=[list(range(NCORES))],
                  ins=[x2l_d.opt()], outs=[x2a_d.opt()], inc=[(qCx2, 1)])
            kb.wait(Y, qCcnt, qCcnt.n)
            kb.wait(Y, mm, mm.n)
            kb.wait(Y, ve, ve.n)
            kb.wait(Y, se, se.n)
            for q_ in (qCL[0], qCL[1], qWM[0], qWM[1], qX2[0], qX2[1]):
                kb.wait(Y, q_, q_.n)
            kb.op(Y, "dma_start", out=cnts_sb[0:64, 0:64], in_=cnt_out[:, :], inc=[(qL, 16)])
            ld_cnt = qL.n
            kb.wait(T, qL, ld_cnt)
            for s_ in range(NSEG):
                kb.op(T, "matmul", ps2[0:1, s_ * 64:(s_ + 1) * 64],
                      maskcols_sb[:, s_:s_ + 1], cnts_sb[0:64, 0:64],
                      start=True, stop=True, inc=[(mm, 1)])
            base_mm = mm.n
            kb.wait(V, mm, base_mm)
            for s_ in range(NSEG):
                kb.op(V, "tensor_tensor",
                      out=rowbase_sb[0:1, s_ * 4 * E:(s_ + 1) * 4 * E].rearrange("o (t e) -> o t e", e=E),
                      in0=rowbase_sb[0:1, s_ * 4 * E:(s_ + 1) * 4 * E].rearrange("o (t e) -> o t e", e=E),
                      in1=ps2[0:1, s_ * 64:(s_ + 1) * 64].unsqueeze(1).broadcast_to([1, 4, 64]),
                      op=ALU.add, inc=[(ve, 1)])
            rb_done = ve.n
            kb.wait(T, ve, rb_done)

            smk = {"n": 0}
            posh_guard = [0] * NPOSH
            hl_guard = [0] * NHL

            def p4_tile(g):
                s_, t = g // 4, g % 4
                par = g % NHL
                if t == 0:
                    kb.wait(Y, ve, ve.n)
                    kb.op(Y, "dma_start", out=clseg_sb, in_=cl_d[s_], inc=[(qL, 16)])
                    kb.op(Y, "dma_start", out=waseg_sb, in_=wa_d[s_], inc=[(qL, 16)])
                    kb.wait(V, qL, qL.n)
                kb.wait(T, ve, ve.n)
                kb.op(T, "matmul", ps2[:, 0:64], ones_f_sb[:],
                      rowbase_sb[0:1, g * E:(g + 1) * E], start=True, stop=True, inc=[(mm, 1)])
                kb.wait(V, mm, mm.n)
                kb.op(V, "tensor_tensor", out=cg_sb, in0=clseg_sb[:, t * E:(t + 1) * E],
                      in1=ps2[:, 0:64], op=ALU.add, inc=[(ve, 1)])
                kb.wait(T, ve, ve.n)
                kb.op(V, "tensor_tensor",
                      out=mask_sb.rearrange("p (k e) -> p k e", e=E),
                      in0=c64_sb[:].unsqueeze(1).broadcast_to([128, 8, E]),
                      in1=edesc_sb[:, g * 8:(g + 1) * 8].unsqueeze(2).broadcast_to([128, 8, E]),
                      op=ALU.is_equal, inc=[(ve, 1)])
                kb.op(V, "tensor_tensor",
                      out=wmp_sb.rearrange("p (k e) -> p k e", e=E),
                      in0=mask_sb.rearrange("p (k e) -> p k e", e=E),
                      in1=waseg_sb[:, t * E:(t + 1) * E].unsqueeze(1).broadcast_to([128, 8, E]),
                      op=ALU.mult, inc=[(ve, 1)])
                kb.wait(V, mm, hl_guard[par])
                kb.op(V, "tensor_scalar",
                      out=mhl_sb[par].rearrange("p (k x e) -> p k x e", x=2, e=E)[:, :, 0, :],
                      in0=mask_sb.rearrange("p (k e) -> p k e", e=E),
                      scalar1=tok_hi_sb[:, g:g + 1], scalar2=None, op0=ALU.mult, inc=[(ve, 1)])
                kb.op(V, "tensor_scalar",
                      out=mhl_sb[par].rearrange("p (k x e) -> p k x e", x=2, e=E)[:, :, 1, :],
                      in0=mask_sb.rearrange("p (k e) -> p k e", e=E),
                      scalar1=tok_lo_sb[:, g:g + 1], scalar2=None, op0=ALU.mult, inc=[(ve, 1)])
                kb.op(V, "tensor_copy",
                      out=whl_sb[par].rearrange("p (k x e) -> p k x e", x=2, e=E)[:, :, 0, :],
                      in_=wmp_sb.rearrange("p (k e) -> p k e", e=E), inc=[(ve, 1)])
                kb.op(V, "tensor_tensor",
                      out=whl_sb[par].rearrange("p (k x e) -> p k x e", x=2, e=E)[:, :, 1, :],
                      in0=wmp_sb.rearrange("p (k e) -> p k e", e=E),
                      in1=whl_sb[par].rearrange("p (k x e) -> p k x e", x=2, e=E)[:, :, 0, :],
                      op=ALU.subtract, inc=[(ve, 1)])
                for k in range(8):
                    kb.op(V, "scalar_tensor_tensor", out=junk_sb,
                          in0=mask_sb[:, k * E:(k + 1) * E], scalar=1.0, in1=cg_sb,
                          op0=ALU.mult, op1=ALU.mult,
                          accum_out=pos_sb[:, g * 8 + k: g * 8 + k + 1], inc=[(ve, 1)])
                for k in range(8):
                    pp = (g * 8 + k) % NPOSH
                    kb.wait(V, mm, posh_guard[pp])
                    kb.op(V, "tensor_scalar", out=posh_sb[pp],
                          in0=c512_sb[:], scalar1=pos_sb[:, g * 8 + k: g * 8 + k + 1],
                          scalar2=None, op0=ALU.is_equal, inc=[(ve, 1)])
                    kb.wait(T, ve, ve.n)
                    first = smk["n"] == 0
                    last = (g == 31 and k == 7)
                    kb.op(T, "matmul", ps3[:], mhl_sb[par][:, k * 128:(k + 1) * 128],
                          posh_sb[pp], start=first, stop=last, skip_group_check=True)
                    kb.op(T, "matmul", ps4[:], whl_sb[par][:, k * 128:(k + 1) * 128],
                          posh_sb[pp], start=first, stop=last, skip_group_check=True,
                          inc=[(mm, 1)])
                    posh_guard[pp] = mm.n
                    smk["n"] += 1
                hl_guard[par] = mm.n

            def p5_seg(s_):
                kb.wait(Y, mm, mm.n)
                kb.wait(Y, ve, ve.n)
                kb.wait(Y, se, se.n)
                kb.wait(Y, qS, qS.n)
                kb.op(Y, "dma_start", out=p5xlt_sb, in_=xlt2_d[s_], inc=[(qL, 16)])
                ld_xlt = qL.n
                if s_ == 0:
                    kb.wait(Y, qE[0], qE[0].n)
                    kb.wait(Y, qE[1], qE[1].n)
                    kb.wait(T, qW5, p5w_ld)
                kb.wait(T, qL, ld_xlt)
                for fc in range(16):
                    for c in range(NCH):
                        kb.op(T, "matmul", ps0[:],
                              sguv[:, c * 4096 + fc * 128: c * 4096 + (fc + 1) * 128],
                              p5xlt_sb[:, c * SL:(c + 1) * SL],
                              start=(c == 0), stop=(c == NCH - 1))
                    for c in range(NCH):
                        kb.op(T, "matmul", ps1[:],
                              sguv[:, c * 4096 + (fc + 16) * 128: c * 4096 + (fc + 17) * 128],
                              p5xlt_sb[:, c * SL:(c + 1) * SL],
                              start=(c == 0), stop=(c == NCH - 1),
                              inc=([(mm, 1)] if c == NCH - 1 else None))
                    gu_mm = mm.n
                    kb.wait(S, mm, gu_mm)
                    kb.op(S, "activation", out=silu_sb, in_=ps0[:], func=AF.Silu,
                          bias=0.0, scale=1.0, inc=[(se, 1)])
                    kb.wait(V, se, se.n)
                    kb.op(V, "tensor_tensor", out=h_sb[:, fc * SL:(fc + 1) * SL],
                          in0=silu_sb, in1=ps1[:], op=ALU.mult, inc=[(ve, 1)])
                    kb.wait(T, ve, ve.n)
                for t in range(NT):
                    g = s_ * 4 + t
                    for half, sdw, pso in ((0, sd0_sb, ps0), (1, sd1_sb, ps1)):
                        for c in range(16):
                            kb.op(T, "matmul", pso[:],
                                  h_sb[:, c * SL + t * 128: c * SL + (t + 1) * 128],
                                  sdw[:, c * SL:(c + 1) * SL],
                                  start=(c == 0), stop=(c == 15),
                                  inc=([(mm, 1)] if c == 15 else None))
                    dn_mm = mm.n
                    kb.wait(Y, ve, ve.n)
                    kb.op(Y, "dma_start", out=p5x_sb, in_=x2_d[s_, t * 128:(t + 1) * 128, :],
                          inc=[(qL, 16)])
                    kb.wait(V, qL, qL.n)
                    kb.wait(V, mm, dn_mm)
                    kb.op(V, "scalar_tensor_tensor", out=p5s_sb[:, 0:SL], in0=ps0[:],
                          scalar=sg_sb[:, g:g + 1], in1=p5x_sb[:, 0:SL],
                          op0=ALU.mult, op1=ALU.add, inc=[(ve, 1)])
                    kb.op(V, "scalar_tensor_tensor", out=p5s_sb[:, SL:2 * SL], in0=ps1[:],
                          scalar=sg_sb[:, g:g + 1], in1=p5x_sb[:, SL:D],
                          op0=ALU.mult, op1=ALU.add, inc=[(ve, 1)])
                    x2s_done = ve.n
                    kb.wait(S, ve, x2s_done)
                    kb.op(S, "dma_start", out=x2s_d[s_, t * 128:(t + 1) * 128, :],
                          in_=p5s_sb, inc=[(qXS[s_ // 2], 16)])
                    kb.wait(T, ve, x2s_done)
                    kb.wait(V, qXS[s_ // 2], qXS[s_ // 2].n)

            for s_ in range(NSEG):
                for t in range(NT):
                    p4_tile(s_ * 4 + t)
            sm_mm = mm.n

            kb.wait(V, mm, sm_mm)
            kb.op(V, "tensor_scalar", out=eh_sb[0], in0=edesc_sb[:], scalar1=8.0, scalar2=None,
                  op0=ALU.is_ge, inc=[(ve, 1)])
            for i, thr in enumerate((16.0, 24.0, 32.0, 40.0, 48.0, 56.0)):
                kb.op(V, "scalar_tensor_tensor", out=eh_sb[(i + 1) % 2], in0=edesc_sb[:],
                      scalar=thr, in1=eh_sb[i % 2], op0=ALU.is_ge, op1=ALU.add, inc=[(ve, 1)])
            ehf = eh_sb[0]
            kb.op(V, "scalar_tensor_tensor", out=a1_sb, in0=edesc_sb[:], scalar=512.0,
                  in1=ehf, op0=ALU.mult, op1=ALU.add, inc=[(ve, 1)])
            kb.op(V, "tensor_tensor", out=a1_sb, in0=a1_sb, in1=pos_sb[:], op=ALU.add, inc=[(ve, 1)])
            kb.op(V, "tensor_scalar", out=alive_sb, in0=pos_sb[:], scalar1=512.0, scalar2=None,
                  op0=ALU.is_lt, inc=[(ve, 1)])
            kb.op(V, "tensor_scalar", out=a1_sb, in0=a1_sb, scalar1=zrow_sb[:, 0:1], scalar2=None,
                  op0=ALU.subtract, inc=[(ve, 1)])
            kb.op(V, "tensor_tensor", out=a1_sb, in0=a1_sb, in1=alive_sb, op=ALU.mult, inc=[(ve, 1)])
            kb.op(V, "tensor_scalar", out=addrf_sb[:], in0=a1_sb, scalar1=zrow_sb[:, 0:1],
                  scalar2=None, op0=ALU.add, inc=[(ve, 1)])
            kb.op(V, "tensor_copy", out=addru_sb[:], in_=addrf_sb[:], inc=[(ve, 1)])
            if dev:
                kb.wait(S, ve, ve.n)
                kb.op(S, "dma_start", out=dbg_d[:, 32:288], in_=edesc_sb[:], inc=[(qS, 16)])
                kb.op(S, "dma_start", out=dbg_d[:, 288:544], in_=pos_sb[:], inc=[(qS, 16)])
                kb.op(S, "dma_start", out=dbg_d[:, 544:800], in_=addrf_sb[:], inc=[(qS, 16)])

            kb.wait(S, mm, sm_mm)
            kb.op(S, "copy", smhlb_sb2[:, 0:SL], ps3[:], inc=[(se, 1)])
            kb.op(S, "copy", smhlb_sb2[:, SL:2 * SL], ps4[:], inc=[(se, 1)])
            hl_ev = se.n
            kb.wait(T, se, hl_ev)
            kb.op(T, "matmul", ps2[0:64, 0:SL], comb_sb[:, 0:64], smhlb_sb2[:, 0:SL],
                  start=True, stop=True, inc=[(mm, 1)])
            kb.wait(S, mm, mm.n)
            kb.op(S, "copy", smv_sb[0:64, :], ps2[0:64, 0:SL], inc=[(se, 1)])
            kb.wait(T, se, se.n)
            kb.op(T, "matmul", ps2[0:64, 0:SL], comb_sb[:, 64:128], smhlb_sb2[:, SL:2 * SL],
                  start=True, stop=True, inc=[(mm, 1)])
            kb.wait(S, mm, mm.n)
            kb.op(S, "copy", wv_sb[0:64, :], ps2[0:64, 0:SL], inc=[(se, 1)])
            smw_done = se.n
            kb.wait(S, se, smw_done)
            for r in range(NCORES):
                kb.op(S, "dma_start", out=smw_in[16 * r:16 * r + 8, :],
                      in_=smv_sb[8 * r:8 * r + 8, :], inc=[(qS, 16)])
                kb.op(S, "dma_start", out=smw_in[16 * r + 8:16 * r + 16, :],
                      in_=wv_sb[8 * r:8 * r + 8, :], inc=[(qS, 16)])
            kb.wait(G, qS, qS.n)
            kb.op(G, "collective_compute", "ReduceScatter", ALU.add,
                  replica_groups=[list(range(NCORES))],
                  ins=[smw_in.opt()], outs=[smw_out.opt()], inc=[(qCsm, 1)])
            kb.wait(Y, qCsm, qCsm.n)
            kb.op(Y, "dma_start", out=smloc_sb[0:16, :], in_=smw_out[:, :], inc=[(qL, 16)])
            kb.wait(T, qL, qL.n)
            for t in range(NT):
                kb.op(T, "transpose", ps2[:, t * 16:(t + 1) * 16],
                      smloc_sb[0:16, t * 128:(t + 1) * 128], identf_sb[0:16, 0:16],
                      inc=([(mm, 1)] if t == NT - 1 else None))
            kb.wait(V, mm, mm.n)
            for t in range(NT):
                kb.op(V, "tensor_scalar", out=smT_sb[:, t * 8:(t + 1) * 8],
                      in0=ps2[:, t * 16: t * 16 + 8], scalar1=-1.0, scalar2=0.0,
                      op0=ALU.add, op1=ALU.max, inc=[(ve, 1)])
                kb.op(V, "tensor_copy", out=wt_sb[:, t * 8:(t + 1) * 8],
                      in_=ps2[:, t * 16 + 8: t * 16 + 16], inc=[(ve, 1)])
            kb.op(V, "tensor_copy", out=smTu_sb[:, 0:32], in_=smT_sb[:, 0:32], inc=[(ve, 1)])
            p4_done = ve.n
            if dev:
                kb.wait(S, ve, p4_done)
                kb.op(S, "dma_start", out=dbg_d[:, 800:832], in_=smT_sb[:, 0:32], inc=[(qS, 16)])
                kb.op(S, "dma_start", out=dbg_d[:, 832:864], in_=wt_sb[:, 0:32], inc=[(qS, 16)])

        if max_phase >= 6:
            kb.wait(V, qS, qS.n)
            kb.op(V, "memset", zero_sb[0:1, :], 0.0, inc=[(ve, 1)])
            kb.wait(S, ve, ve.n)
            kb.op(S, "dma_start", out=x4l_d[X4ROWS - 1: X4ROWS, :], in_=zero_sb[0:1, :],
                  inc=[(qS, 16)])
            kb.wait(S, qS, qS.n)
            kb.wait(G, ve, ve.n)
            kb.wait(G, qCx2, qCx2.n)
            kb.wait(G, qS, qS.n)
            kb.wait(G, qG, qG.n)

            def load_exp(e, buf):
                kb.op(G, "dma_start", out=egu_sb[buf].rearrange("p (c f) -> p c f", f=D),
                      in_=egu_w[e].rearrange("(c p) f -> p c f", p=128), inc=[(qE[buf], 16)])
                kb.op(G, "dma_start", out=edn_sb[buf].rearrange("p (c f) -> p c f", f=D),
                      in_=edown_w[e].rearrange("(c p) f -> p c f", p=128), inc=[(qE[buf], 16)])

            kb.wait(G, mm, mm.n)
            load_exp(0, 0)
            for e in range(EL):
                buf = e % 2
                w_ld = qE[buf].n
                if e + 1 < EL:
                    kb.wait(G, tE, max(0, e))
                    load_exp(e + 1, (e + 1) % 2)
                for t in range(NT):
                    kb.wait(G, mm, mm.n)
                    kb.op(G, "indirect_dma_start", out=rows_sb, out_offset=None,
                          in_=x2a_d, in_offset=IOA(smTu_sb[:, t * 8 + e: t * 8 + e + 1], 0),
                          inc=[(qG, 16)])
                    kb.wait(T, qG, qG.n)
                    for c in range(NCH):
                        kb.op(T, "transpose", ps_tr[:, (c % 4) * 128:(c % 4 + 1) * 128],
                              rows_sb[:, c * 128:(c + 1) * 128], ident_sb[:],
                              inc=[(mm, 1)] if (c % 4 == 3) else None)
                        if c % 4 == 3:
                            kb.wait(V, mm, mm.n)
                            for cc in range(c - 3, c + 1):
                                kb.op(V, "tensor_copy",
                                      out=rowsT_sb[:, cc * SL + t * 128: cc * SL + (t + 1) * 128],
                                      in_=ps_tr[:, (cc % 4) * 128:(cc % 4 + 1) * 128],
                                      inc=([(ve, 1)] if cc % 4 == 3 else None))
                            kb.wait(T, ve, ve.n)
                rows_done = ve.n
                kb.wait(T, qE[buf], w_ld)
                if e == 0:
                    kb.wait(T, se, se.n)
                for fc in range(4):
                    for c in range(NCH):
                        kb.op(T, "matmul", ps3[:],
                              egu_sb[buf][:, c * D + fc * 128: c * D + (fc + 1) * 128],
                              rowsT_sb[:, c * SL:(c + 1) * SL],
                              start=(c == 0), stop=(c == NCH - 1))
                    for c in range(NCH):
                        kb.op(T, "matmul", ps4[:],
                              egu_sb[buf][:, c * D + (fc + 4) * 128: c * D + (fc + 5) * 128],
                              rowsT_sb[:, c * SL:(c + 1) * SL],
                              start=(c == 0), stop=(c == NCH - 1),
                              inc=([(mm, 1)] if c == NCH - 1 else None))
                    kb.wait(S, mm, mm.n)
                    kb.op(S, "activation", out=silu6_sb, in_=ps3[:], func=AF.Silu,
                          bias=0.0, scale=1.0, inc=[(se, 1)])
                    kb.wait(V, se, se.n)
                    kb.op(V, "tensor_tensor", out=he_sb[:, fc * SL:(fc + 1) * SL],
                          in0=silu6_sb, in1=ps4[:], op=ALU.mult, inc=[(ve, 1)])
                    kb.wait(T, ve, ve.n)
                for t in range(NT):
                    for half, pso in ((0, ps3), (1, ps4)):
                        for c in range(4):
                            kb.op(T, "matmul", pso[:],
                                  he_sb[:, c * SL + t * 128: c * SL + (t + 1) * 128],
                                  edn_sb[buf][:, c * D + half * SL: c * D + half * SL + SL],
                                  start=(c == 0), stop=(c == 3),
                                  inc=([(mm, 1)] if c == 3 else None))
                    dn_mm = mm.n
                    kb.wait(V, mm, dn_mm)
                    kb.wait(V, qS, qS.n)
                    kb.op(V, "tensor_scalar", out=oute_sb[:, 0:SL], in0=ps3[:],
                          scalar1=wt_sb[:, t * 8 + e: t * 8 + e + 1], scalar2=None,
                          op0=ALU.mult, inc=[(ve, 1)])
                    kb.op(V, "tensor_scalar", out=oute_sb[:, SL:2 * SL], in0=ps4[:],
                          scalar1=wt_sb[:, t * 8 + e: t * 8 + e + 1], scalar2=None,
                          op0=ALU.mult, inc=[(ve, 1)])
                    kb.wait(S, ve, ve.n)
                    kb.op(S, "dma_start", out=x4l_d[e * CAP + t * 128: e * CAP + (t + 1) * 128, :],
                          in_=oute_sb, inc=[(qS, 16)])
                    kb.wait(T, ve, ve.n)
                tE.inc(1)
                kb.emit(T, lambda eng, s_=tE: eng.sem_inc(s_.h, 1))
            kb.wait(G, qS, qS.n)
            kb.wait(G, qG, qG.n)
            kb.op(G, "collective_compute", "AllGather", ALU.bypass,
                  replica_groups=[list(range(NCORES))],
                  ins=[x4l_d.opt()], outs=[x4a_d.opt()], inc=[(qCx4, 1)])
            p6_mm = mm.n
            p6_ve = ve.n
            p6_se = se.n

            for s_ in range(NSEG):
                p5_seg(s_)

        if max_phase >= 7:
            kb.wait(G, qCx4, qCx4.n)
            kb.wait(G, ve, p6_ve)
            kb.wait(G, se, p6_se)
            for grp in range(4):
                tiles = list(range(grp * 8, (grp + 1) * 8))
                kb.wait(Y, mm, p6_mm)
                kb.wait(Y, ve, p6_ve)
                kb.wait(Y, qE[0], qE[0].n)
                kb.wait(Y, qE[1], qE[1].n)
                kb.wait(Y, qG, qG.n)
                kb.wait(Y, qXS[grp], qXS[grp].n)
                kb.wait(Y, qS, qS.n)
                for i, gt in enumerate(tiles):
                    s, t = gt // 4, gt % 4
                    kb.op(Y, "dma_start", out=fin_sb[i], in_=x2s_d[s, t * 128:(t + 1) * 128, :],
                          inc=[(qL, 16)])
                kb.wait(G, qL, qL.n)
                for k in range(8):
                    if k > 0:
                        kb.wait(G, qG, qG.n)
                    for i, gt in enumerate(tiles):
                        kb.op(G, "indirect_dma_start", out=fin_sb[i], out_offset=None,
                              in_=x4a_d, in_offset=IOA(addru_sb[:, gt * 8 + k: gt * 8 + k + 1], 0),
                              compute_op=ALU.add, inc=[(qG, 16)])
                kb.wait(Y, qG, qG.n)
                for i, gt in enumerate(tiles):
                    kb.op(Y, "dma_start", out=out_d[gt * 128:(gt + 1) * 128, :], in_=fin_sb[i],
                          inc=[(qS, 16)])

        kb.wait(Y, qS, qS.n)
        kb.wait(Y, qG, qG.n)
        for _s in (qCkv, qCcnt, qCx2, qCsm, qCx4, qE[0], qE[1], qW5,
                   qX2[0], qX2[1], qCL[0], qCL[1], qWM[0], qWM[1],
                   qXS[0], qXS[1], qXS[2], qXS[3]):
            kb.wait(Y, _s, _s.n)
        kb.wait(Y, mm, mm.n)
        kb.wait(Y, ve, ve.n)
        kb.wait(Y, se, se.n)
        kb.wait(Y, qL, qL.n)

        with nc.Block() as block:
            @block.tensor
            def _(eng):
                for en, fn in kb.ops:
                    if en == T:
                        fn(eng)

            @block.vector
            def _(eng):
                for en, fn in kb.ops:
                    if en == V:
                        fn(eng)

            @block.scalar
            def _(eng):
                for en, fn in kb.ops:
                    if en == S:
                        fn(eng)

            @block.gpsimd
            def _(eng):
                for en, fn in kb.ops:
                    if en == G:
                        fn(eng)

            @block.sync
            def _(eng):
                for en, fn in kb.ops:
                    if en == Y:
                        fn(eng)

    return nc


def _perm_evenodd():
    q = np.arange(1024)
    p = np.concatenate([q[0::2], q[1::2]])
    return np.concatenate([p, 1024 + p])


def prepare_inputs(inputs):
    x = np.asarray(inputs["L0_attn_x"])
    wqk = np.asarray(inputs["L0_attn_qk_w"])
    bqk = np.asarray(inputs["L0_attn_qk_b"])
    lepe_w = np.asarray(inputs["L0_attn_lepe_w"])
    lepe_b = np.asarray(inputs["L0_attn_lepe_b"])
    gate_w = np.asarray(inputs["L0_ffn_gate_w"])
    sg_w = np.asarray(inputs["L0_ffn_shared_gate_w"])
    sgu_w = np.asarray(inputs["L0_ffn_shared_gate_up_w"])
    sdown_w = np.asarray(inputs["L0_ffn_shared_down_w"])
    egu_w = np.asarray(inputs["L0_ffn_expert_gate_up_w"])
    edown_w = np.asarray(inputs["L0_ffn_expert_down_w"])

    perm = _perm_evenodd()
    wqk_p = np.ascontiguousarray(wqk[:, perm]).astype(np.float32)
    bqk_p = bqk[perm]
    bqk_cols = np.ascontiguousarray(bqk_p.reshape(16, 128).T).astype(np.float32)

    gate_ext = np.zeros((D, 72), np.float32)
    gate_ext[:, :64] = gate_w
    gate_ext[:, 64:65] = sg_w

    theta = 1.0 / (10000.0 ** (np.arange(512, dtype=np.float32) / 512.0))
    c64c = np.tile(np.arange(E, dtype=np.float32), (128, 1))
    c512c = np.tile(np.arange(CAP, dtype=np.float32), (128, 1))
    ustrictc = np.triu(np.ones((128, 128), np.float32), 1).astype(BF)
    identc = np.eye(128, dtype=BF)
    identfc = np.eye(128, dtype=np.float32)

    in_maps = []
    for c in range(NCORES):
        segs = [(b, c + 8 * h) for b in range(B) for h in range(2)]
        x_shard = np.stack([x[b, 512 * j:512 * (j + 1), :] for (b, j) in segs])
        halo = np.zeros((NSEG, 2, D), np.float32)
        for si, (b, j) in enumerate(segs):
            if j > 0:
                halo[si, 0] = x[b, 512 * j - 1]
            if j < 15:
                halo[si, 1] = x[b, 512 * (j + 1)]
        cost = np.zeros((NSEG, 4, 128, SL), np.float32)
        sint = np.zeros((NSEG, 4, 128, SL), np.float32)
        for si, (b, j) in enumerate(segs):
            pos = np.arange(512 * j, 512 * (j + 1), dtype=np.float32)
            ang = pos[None, :] * theta[:, None]
            cost[si] = np.cos(ang).reshape(4, 128, SL)
            sint[si] = np.sin(ang).reshape(4, 128, SL)

        loc = np.arange(NSEG * SL).reshape(32, 128).T
        v = (c * NSEG * SL + loc + 1).astype(np.float32)
        tok_hic = np.floor(v / 256.0).astype(np.float32)
        tok_loc = (v - 256.0 * tok_hic).astype(np.float32)

        maskcol = np.zeros((E, 8), np.float32)
        for s_own in range(NSEG):
            b_own, h_own = s_own // 2, s_own % 2
            g_own = 16 * b_own + c + 8 * h_own
            for r in range(NCORES):
                for l in range(8):
                    bb, hh = l // 2, l % 2
                    g = 16 * bb + r + 8 * hh
                    if g < g_own:
                        maskcol[r * 8 + l, s_own] = 1.0

        comb = np.zeros((128, 128), np.float32)
        for e in range(64):
            comb[e, e] = 256.0
            comb[e + 64, e] = 1.0
            comb[e, 64 + e] = 1.0
            comb[e + 64, 64 + e] = 1.0
        in_maps.append(dict(
            x_shard=x_shard, x_halo=halo, comb_bf=comb.astype(BF),
            wqk_p=wqk_p, bqk_cols=bqk_cols,
            lepe_w=lepe_w.astype(np.float32), lepe_b=lepe_b.reshape(1, D).astype(np.float32),
            costab=cost.astype(BF), sintab=sint.astype(BF),
            gate_ext=gate_ext, sgu_w=sgu_w.astype(np.float32),
            sdown_w=sdown_w.astype(np.float32),
            egu_w=np.ascontiguousarray(egu_w[8 * c:8 * c + 8]).astype(np.float32),
            edown_w=np.ascontiguousarray(edown_w[8 * c:8 * c + 8]).astype(np.float32),
            c64=c64c, c64p1=c64c + 1.0, c512=c512c,
            ustrict=ustrictc, ident_bf=identc, ident_f=identfc,
            ones_bf=np.ones((1, 128), BF), onec_bf=np.ones((128, 1), BF),
            ones_f=np.ones((1, 128), np.float32),
            tok_hi=tok_hic, tok_lo=tok_loc, maskcols=maskcol,
            zrow_col=np.full((128, 1), float(c * X4ROWS + X4ROWS - 1), np.float32),
            eps_col=np.full((128, 1), 1e-5, np.float32),
        ))
    return in_maps


def assemble_output(results):
    out = np.zeros((B, N, D), np.float32)
    for c in range(NCORES):
        r = results[c]["out_d"].reshape(NSEG, SL, D)
        segs = [(b, c + 8 * h) for b in range(B) for h in range(2)]
        for si, (b, j) in enumerate(segs):
            out[b, 512 * j:512 * (j + 1), :] = r[si]
    return out


def kernel(**inputs):
    in_maps = prepare_inputs(inputs)
    nc = build_program()
    res = run_bass_kernel_spmd(nc, in_maps, core_ids=list(range(NCORES)))
    return assemble_output(res.results)
